# version 02
# baseline (speedup 1.0000x reference)
import numpy as np
from dataclasses import dataclass
from contextlib import ExitStack

import concourse.bass as bass
import concourse.bacc as bacc
import concourse.mybir as mybir
import concourse.library_config as LC
from concourse.bass_core_rust import IndirectOffsetOnAxis
from concourse.bass_utils import run_bass_kernel_spmd
from concourse.expressions import smax

F32 = mybir.dt.float32
BF16 = mybir.dt.bfloat16
I32 = mybir.dt.int32
I16 = mybir.dt.int16
AF = mybir.ActivationFunctionType
ALU = mybir.AluOpType
ts = bass.ts
ds = bass.ds
X = mybir.AxisListType.X


@dataclass
class KCfg:
    B: int = 4
    S: int = 8192
    D: int = 1024
    E: int = 8
    F: int = 4096
    FF: int = 2048
    ST: int = 64
    n_cores: int = 8
    L: int = 64
    SPAN: int = 256
    CH: int = 512
    eps: float = 1e-5
    debug: bool = False

    @property
    def T(self): return self.B * self.S // self.n_cores
    @property
    def TW(self): return self.T + 128
    @property
    def cap(self): return int(1.25 * self.B * self.S / self.E)
    @property
    def FCH(self): return self.D // 128
    @property
    def FCH_F(self): return self.F // 128
    @property
    def FCH_FF(self): return self.FF // 128
    @property
    def halves(self): return self.n_cores // self.B
    @property
    def NTILE(self): return self.TW // 128
    @property
    def NROW(self): return self.TW
    @property
    def NSLOT(self):
        base = self.TW + self.E * 128 + 256
        return ((base + 255) // 256) * 256
    @property
    def MAXSPANS(self):
        return self.TW // 256 + self.E + 1
    @property
    def PARK(self): return self.NSLOT - 128
    @property
    def WVMIN(self):
        return 128 * max(self.E, (self.TW - 128 + 127) // 128)
    @property
    def NZCH(self):
        return (self.NSLOT - self.WVMIN + 511) // 512
    @property
    def EOUTW(self): return self.NSLOT + self.NZCH * 512
    @property
    def chunk_plan(self):
        return [(o, self.CH) for o in range(0, self.T, self.CH)] + [(self.T, 128)]
    @property
    def NCHUNKS(self): return self.T // self.L
    CHW: int = 256
    @property
    def NCH7(self): return self.T // self.CHW
    @property
    def nch(self): return self.T // self.L
    @property
    def NBK(self): return self.T // 512
    @property
    def CPB(self): return 512 // self.L
    @property
    def HQ(self): return self.T // 4


def col128(v):
    K = v.shape[0]
    return np.ascontiguousarray(v.reshape(K // 128, 128).T)


def stage_inputs(cfg, inputs):
    c = cfg
    x = np.ascontiguousarray(np.asarray(inputs["L0_attn_x"], np.float32).reshape(-1, c.D))
    T = c.T
    w = {k: np.ascontiguousarray(np.asarray(v, np.float32)) for k, v in inputs.items()
         if k != "L0_attn_x"}
    convw = w["L0_attn_conv_w"][:, 0, :]
    com = dict(
        w1T=np.ascontiguousarray(w["L0_attn_w1_w"].T), v1T=np.ascontiguousarray(w["L0_attn_v1_w"].T),
        w2T=np.ascontiguousarray(w["L0_attn_w2_w"].T),
        ff1T=np.ascontiguousarray(w["L0_attn_ff1_w"].T), ff2T=np.ascontiguousarray(w["L0_attn_ff2_w"].T),
        BmT=np.ascontiguousarray(w["L0_attn_Bm"].T), CmT=np.ascontiguousarray(w["L0_attn_Cm"].T),
        Amat=w["L0_attn_A"], Wr=w["L0_ffn_Wr"],
        W1=np.ascontiguousarray(w["L0_ffn_W1"].reshape(c.E * c.D, c.F)),
        W2=np.ascontiguousarray(w["L0_ffn_W2"].reshape(c.E * c.F, c.D)),
        convw=np.ascontiguousarray(np.stack([col128(convw[:, i]) for i in range(3)], 1)
                                   .reshape(128, 3 * c.FCH)),
        w1b=col128(w["L0_attn_w1_b"]), v1b=col128(w["L0_attn_v1_b"]), w2b=col128(w["L0_attn_w2_b"]),
        ff1b=col128(w["L0_attn_ff1_b"]), ff2b=col128(w["L0_attn_ff2_b"]),
        ln1g=col128(w["L0_attn_ln1_g"]), ln1b=col128(w["L0_attn_ln1_b"]),
        ln2g=col128(w["L0_attn_ln2_g"]), ln2b=col128(w["L0_attn_ln2_b"]),
    )
    maps = []
    for core in range(c.n_cores):
        sh = x[core * T:(core + 1) * T]
        hL = np.zeros((1, c.D), np.float32) if core % c.halves == 0 else x[core * T - 1:core * T]
        hR = (np.zeros((1, c.D), np.float32) if core % c.halves == c.halves - 1
              else x[(core + 1) * T:(core + 1) * T + 1])
        pad = np.zeros((126, c.D), np.float32)
        xs = np.ascontiguousarray(np.concatenate([sh, hL, hR, pad], 0).T)
        maskB = np.zeros((8, 1), np.float32); maskB[:core, 0] = 1.0
        maskR = np.zeros((8, 1), np.float32); maskR[:core + 1, 0] = 1.0
        lv = (core % c.halves) != 0
        rv = (core % c.halves) != c.halves - 1
        w1bcol = col128(w["L0_attn_w1_b"])
        zcol = np.zeros_like(w1bcol)
        w1bLR = np.ascontiguousarray(np.concatenate(
            [w1bcol if lv else zcol, w1bcol if rv else zcol], 1))
        convmask = np.zeros((128, 2), np.float32)
        convmask[:, 0] = 1.0 if lv else 0.0
        convmask[:, 1] = 1.0 if rv else 0.0
        sibsel = np.zeros((8, 1), np.float32)
        if core % c.halves == 1:
            sibsel[core - 1, 0] = 1.0
        m = dict(com)
        m.update(xs=xs, maskB=maskB,
                 maskLR=np.ascontiguousarray(np.concatenate([maskB, maskR], 1)),
                 w1bLR=w1bLR, convmask=convmask, sibsel=sibsel)
        maps.append(m)
    return maps


class KB:
    PERSISTENT = {"cst", "x3w", "bar", "dbg"}

    def __init__(self, cfg: KCfg):
        self.c = cfg
        self.nc = bacc.Bacc("TRN2", target_bir_lowering=False)
        self.stack = ExitStack()
        self.t = {}
        self.sems = {}
        self.H = {}
        self.phases = []
        self._free_hw = []
        self._free_sw = []
        self._sem_class = {}
        self._nalloc = 0
        self.bar_total = 0

    def sem(self, name, sw=False):
        if name in self.sems:
            return self.sems[name]
        cls = "sw" if sw else "hw"
        if name in self.PERSISTENT:
            h = self.stack.enter_context(self.nc.semaphore(name))
        else:
            pool = self._free_sw if sw else self._free_hw
            if pool:
                h = pool.pop()
            else:
                h = self.stack.enter_context(self.nc.semaphore(f"{cls}{self._nalloc}"))
                self._nalloc += 1
        self.sems[name] = h
        self._sem_class[name] = cls
        return h

    def new_generation(self):
        clears = []
        for n in list(self.sems):
            if n in self.PERSISTENT:
                continue
            h = self.sems.pop(n)
            clears.append(h)
            (self._free_sw if self._sem_class[n] == "sw" else self._free_hw).append(h)
        self.phases.append(("GEN", clears))

    def sb(self, name, shape, dtype):
        if name not in self.t:
            self.t[name] = self.stack.enter_context(self.nc.sbuf_tensor(name, shape, dtype))
        return self.t[name]

    def dram(self, name, shape, dtype, kind="Internal"):
        self.H[name] = self.nc.dram_tensor(name, shape, dtype, kind=kind).ap()
        return self.H[name]

    def phase(self, **eng_fns):
        self.phases.append(eng_fns)

    def arena_group(self, name):
        self.arena_off = 0
        self.arena_grp = name
        self.arena_limit = self.ARENA_BYTES - (self.TAIL if name in ("RT", "PA", "AT", "A7") else 0)

    def at(self, name, shape, dtype):
        key = f"{self.arena_grp}_{name}"
        if key in self.t:
            return self.t[key]
        import numpy as _np
        elems = int(_np.prod(shape[1:]))
        nbytes = elems * {F32: 4, BF16: 2, I32: 4, I16: 2}[dtype]
        off = (self.arena_off + 31) // 32 * 32
        assert off + nbytes <= self.arena_limit, f"arena overflow {key}: {off}+{nbytes}"
        h = self.nc.alloc_sbuf_tensor_at(key, list(shape), dtype,
                                         offset=self.arena_base + off)
        self.arena_off = off + nbytes
        self.t[key] = h
        return h

    def arena_mark(self):
        return self.arena_off

    def arena_reset(self, off):
        self.arena_off = off

    def finish(self):
        nc = self.nc
        groups, cur = [], []
        for ph in self.phases:
            if isinstance(ph, tuple) and ph[0] == "GEN":
                groups.append((cur, ph[1]))
                cur = []
            else:
                cur.append(ph)
        groups.append((cur, None))
        for gi, (phs, clears) in enumerate(groups):
            with nc.Block(f"g{gi}") as blk:
                for eng_name in ["sync", "tensor", "vector", "scalar", "gpsimd"]:
                    def runner(e, _phs=phs, _n=eng_name):
                        for ph in _phs:
                            fn = ph.get(_n)
                            if fn is not None:
                                fn(e)
                    getattr(blk, eng_name)(runner)
            if clears is not None:
                nc.all_engine_barrier()
                for h in clears:
                    nc.gpsimd.dma_reset(range(h.num, h.num + 1))
                    nc.gpsimd.sem_clear(h)
                nc.all_engine_barrier()
        nc.compile()
        nc.finalize()
        return nc

    def declare(self):
        c = self.c
        self.dram("xs", [c.D, c.TW], F32, "ExternalInput")
        self.dram("maskB", [8, 1], F32, "ExternalInput")
        self.dram("maskLR", [8, 2], F32, "ExternalInput")
        for nm, shp in [("w1T", [c.D, c.D]), ("v1T", [c.D, c.D]), ("w2T", [c.D, c.D]),
                        ("ff1T", [c.D, c.FF]), ("ff2T", [c.FF, c.D]),
                        ("BmT", [c.D, c.ST]), ("CmT", [c.ST, c.D]), ("Amat", [c.ST, c.ST]),
                        ("Wr", [c.D, c.E]), ("W1", [c.E * c.D, c.F]), ("W2", [c.E * c.F, c.D]),
                        ("convw", [128, 3 * c.FCH]),
                        ("w1b", [128, c.FCH]), ("v1b", [128, c.FCH]), ("w2b", [128, c.FCH]),
                        ("ff1b", [128, c.FCH_FF]), ("ff2b", [128, c.FCH]),
                        ("ln1g", [128, c.FCH]), ("ln1b", [128, c.FCH]),
                        ("ln2g", [128, c.FCH]), ("ln2b", [128, c.FCH]),
                        ("w1bLR", [128, 2 * c.FCH]), ("convmask", [128, 2]),
                        ("sibsel", [8, 1])]:
            self.dram(nm, shp, F32, "ExternalInput")
        self.dram("out", [c.D, c.T], F32, "ExternalOutput")
        self.dram("lnh", [c.D, c.TW], F32)
        self.dram("glh", [c.D, c.NSLOT], BF16)
        self.dram("eout", [c.D, c.EOUTW], F32)
        self.dram("R2", [c.D, c.TW], F32)
        self.dram("x3", [c.D, c.TW], F32)
        self.dram("hh", [c.D, c.T], BF16)
        self.dram("rowscr", [1, max(c.NSLOT, c.NROW)], I16)
        self.dram("ag_in", [1, 16], F32)
        self.dram("ag_out", [c.n_cores, 16], F32)
        self.dram("ag2_in", [1, 128], F32)
        self.dram("ag2_out", [c.n_cores, 128], F32)
        self.dram("ag3_in", [1, 16], F32)
        self.dram("ag3_out", [c.n_cores, 16], F32)
        self.CONST_ident = self.nc.inline_tensor(np.eye(128, dtype=np.float32), name="identc").ap()
        self.CONST_triU = self.nc.inline_tensor(
            np.triu(np.ones((128, 128), np.float32), 1), name="triUc").ap()

        self.dram("pmkrow", [1, c.NROW], F32)
        self.dram("fbrow", [1, c.NROW], F32)
        self.dram("rows2", [1, 2 * c.T], F32)
        for nm in ["eidx", "pos", "pmax", "kept", "gpos", "dsttm"]:
            self.sb("tm_" + nm, [128, c.NTILE], F32)
        self.sb("ohf", [128, c.NTILE * 8], F32)
        self.sb("ones_col_bf", [128, 1], BF16)
        self.sb("ones_col_f", [128, 1], F32)
        self.sb("one1_f", [1, 1], F32)
        self.sb("ones_row_f", [1, 128], F32)
        self.sb("onesP", [128, 128], F32)
        self.sb("onesP_bf", [128, 128], BF16)
        self.sb("epat8", [128, 8], F32)
        self.sb("epatm9", [128, 8], F32)
        self.sb("negS", [1, 8], F32)
        self.sb("negmat", [128, 8], F32)
        self.sb("ident_f", [128, 128], F32)
        self.sb("triU_bf", [128, 128], BF16)
        self.sb("wr_bf", [128, c.FCH * 8], BF16)
        self.sb("wr_f", [128, c.FCH * 8], F32)
        self.sb("maskB_sb", [8, 1], F32)
        self.sb("maskLR_sb", [8, 2], F32)
        self.sb("agrow", [1, 16], F32)
        self.sb("cnt_row", [1, 16], F32)
        self.sb("agres", [8, 16], F32)
        self.sb("seg_start", [1, 16], F32)
        self.sb("halo2", [16, 8], F32)
        self.sb("scr512", [1, 512], F32)
        self.sb("fwdi", [128, c.NSLOT // 16], I16)
        self.sb("backi", [128, c.NROW // 16], I16)
        self.sb("spantab", [1, 32], I32)
        self.banks = [self.stack.enter_context(self.nc.psum_tensor(f"bank{i}", [128, 512], F32))
                      for i in range(8)]
        free = 192 * 1024 - self.nc.sbuf_base
        self.ARENA_BYTES = (free - 512) // 32 * 32
        self.arena_base = self.nc.sbuf_base
        got = self.nc.bump_sbuf(self.ARENA_BYTES)
        assert got is not None, "arena reservation failed"
        self.TAIL = (c.NROW * 4 + 31) // 32 * 32
        rpT = self.nc.alloc_sbuf_tensor_at("rowpackT", [128, c.NROW], F32,
                                           offset=self.arena_base + self.ARENA_BYTES - self.TAIL)
        self.t["rowpackT"] = rpT
        self.t["muR"] = rpT[0:1, :]
        self.t["rR"] = rpT[32:33, :]
        self.t["hsbT"] = self.nc.alloc_sbuf_tensor_at(
            "hsbT", [64, c.T], BF16,
            offset=self.arena_base + self.ARENA_BYTES - self.TAIL)
        self.arena_off = 0
        self.arena_grp = "g"
        self.arena_limit = self.ARENA_BYTES

    def ph_consts(self):
        c, t, H = self.c, self.t, self.H
        ld = self.sem("c_ld")
        cst = self.sem("cst")
        cmm = self.sem("c_mm")

        def sync(e):
            e.dma_start(out=t["ident_f"][:], in_=self.CONST_ident[:]).then_inc(ld, 16)
            e.dma_start(out=t["maskB_sb"][:], in_=H["maskB"][:]).then_inc(ld, 16)
            e.dma_start(out=t["maskLR_sb"][:], in_=H["maskLR"][:]).then_inc(ld, 16)
            for f in range(c.FCH):
                e.dma_start(out=t["wr_f"][:, f * 8:f * 8 + c.E],
                            in_=H["Wr"][f * 128:(f + 1) * 128, :]).then_inc(ld, 16)

        ldg = self.sem("c_ldg", sw=True)

        def gpsimd(e):
            e.dma_start(out=t["triU_bf"][:], in_=self.CONST_triU[:]).then_inc(ldg, 16)
            e.drain()
            e.sem_inc(cst, 1)

        def vector(e):
            e.wait_ge(ld, 16 * (3 + c.FCH))
            e.wait_ge(ldg, 16)
            e.wait_ge(cst, 1)
            e.memset(t["ones_col_bf"][:], 1.0)
            e.memset(t["ones_col_f"][:], 1.0)
            e.memset(t["one1_f"][:], 1.0)
            e.memset(t["ones_row_f"][:], 1.0)
            e.memset(t["onesP"][:], 1.0)
            e.memset(t["onesP_bf"][:], 1.0)
            for k in range(8):
                e.memset(t["epat8"][:, k:k + 1], float(k))
                e.memset(t["epatm9"][:, k:k + 1], float(k - 9) if k < c.E else 0.0)
            e.memset(t["wr_bf"][:], 0.0)
            e.memset(t["agrow"][:], 0.0)
            e.memset(t["cnt_row"][:], 0.0)
            e.memset(t["agres"][:], 0.0)
            e.drain()
            e.tensor_copy(t["wr_bf"][:, 0:c.FCH * 8], t["wr_f"][:, 0:c.FCH * 8])
            e.drain().then_inc(cst, 1)
            e.wait_ge(cmm, 1)
            e.tensor_scalar(t["negS"][:, 0:c.E], self.banks[7][0:1, 0:c.E], -1.0, None, op0=ALU.mult)
            if c.E < 8:
                e.memset(t["negS"][:, c.E:8], 0.0)
            e.drain().then_inc(cst, 1)
            e.wait_ge(cmm, 2)
            e.tensor_copy(t["negmat"][:], self.banks[6][:, 0:8])
            e.drain().then_inc(cst, 1)

        def tensor(e):
            e.wait_ge(cst, 2)
            for f in range(c.FCH):
                mm = e.matmul(self.banks[7][0:1, 0:c.E], t["ones_col_f"][:],
                              t["wr_f"][:, f * 8:f * 8 + c.E],
                              start=(f == 0), stop=(f == c.FCH - 1))
            mm.then_inc(cmm, 1)
            e.wait_ge(cst, 3)
            e.matmul(self.banks[6][:, 0:8], t["ones_row_f"][0:1, :], t["negS"][0:1, 0:8],
                     start=True, stop=True).then_inc(cmm, 1)

        self.phase(sync=sync, gpsimd=gpsimd, vector=vector, tensor=tensor)

    def ph_router(self, call):
        c, t, H = self.c, self.t, self.H
        pre = "r_"
        src = H["xs"] if call == 1 else H["x3"]
        plan = c.chunk_plan
        tiles = []
        tiles_of = {ci: [] for ci in range(len(plan))}
        for ci, (o, w) in enumerate(plan):
            for k in range(w // 128):
                tiles.append((len(tiles), ci, k * 128))
                tiles_of[ci].append(tiles[-1])
        NTI = len(tiles)
        assert NTI == c.NTILE

        ld = self.sem(pre + "ld")
        vc = self.sem(pre + "vc")
        sq = self.sem(pre + "sq")
        smm = self.sem(pre + "mm")
        srow = self.sem(pre + "row")
        sqr = self.sem(pre + "sqr")
        sbc = self.sem(pre + "bc")
        sln = self.sem(pre + "ln")
        slnw = self.sem(pre + "lnw")
        stok = self.sem(pre + "tok")
        sdif = self.sem(pre + "dif")
        sexp = self.sem(pre + "exp")
        soh = self.sem(pre + "oh")
        spos = self.sem(pre + "pos")
        sposv = self.sem(pre + "posv")
        sfin = self.sem(pre + "fin")

        self.arena_group("RT")
        xb = self.at("xb", [128, 2, c.FCH, 512], F32)
        xbf = self.at("xbf", [128, 2, c.FCH, 512], BF16)
        xsq = self.at("xsq", [128, 2, c.FCH, 512], BF16)
        lnb = self.at("lnb", [128, 2, c.FCH, 512], F32)
        scrb_t = self.at("scr512b", [1, 512], F32)
        mxc = self.sb("rt_mxc", [128, 1], F32)
        difc = self.sb("rt_difc", [128, 8], F32)
        exc = self.sb("rt_exc", [128, 8], F32)
        sc8 = self.sb("rt_sc8", [128, 8], F32)
        smc = self.sb("rt_smc", [128, 1], F32)
        ohc = self.sb("rt_ohc", [128, 8], F32)
        ohb = self.sb("rt_ohb", [128, 8], BF16)
        rcolsb = self.sb("rt_rcol", [128, 1], F32)
        b_mu, b_sq, b_bc1, b_bc2, b_tok, b_pos, b_cnt = (self.banks[i] for i in range(7))
        NC = len(plan)

        def sync(e):
            e.wait_ge(self.sem("cst"), 3)
            if call == 3:
                e.wait_ge(self.sem("x3w"), 16 * (c.NCH7 + c.FCH))
            for ci, (o, w) in enumerate(plan):
                if ci >= 2:
                    e.wait_ge(sln, ci - 1)
                    last_tile_prev = sum(pw // 128 for (po, pw) in plan[:ci - 1])
                    e.wait_ge(stok, last_tile_prev)
                if ci >= 1:
                    e.wait_ge(ld, 16 * ci)
                e.dma_start(out=xb[:, ci % 2, :, 0:w],
                            in_=src[:, o:o + w].rearrange("(f p) t -> p f t", p=128)).then_inc(ld, 16)
                e.wait_ge(sln, ci + 1)
                e.dma_start(out=H["lnh"][:, o:o + w].rearrange("(f p) t -> p f t", p=128),
                            in_=lnb[:, ci % 2, :, 0:w]).then_inc(slnw, 16)

        def vector(e):
            if call == 3:
                e.memset(t["cnt_row"][:], 0.0)
                e.drain()
            for ci, (o, w) in enumerate(plan):
                e.wait_ge(ld, 16 * (ci + 1))
                e.tensor_copy(xbf[:, ci % 2, :, 0:w], xb[:, ci % 2, :, 0:w])
                e.drain().then_inc(vc, 1)
                e.wait_ge(smm, ci + 1)
                mu = t["muR"][0:1, o:o + w]
                r = t["rR"][0:1, o:o + w]
                scr = t["scr512"][0:1, 0:w]
                scrb = scrb_t[0:1, 0:w]
                e.tensor_scalar(mu, b_mu[0:1, 0:w], 1.0 / c.D, None, op0=ALU.mult)
                e.tensor_scalar(scr, b_sq[0:1, 0:w], 1.0 / c.D, None, op0=ALU.mult)
                e.drain()
                e.tensor_tensor(scrb, mu, mu, ALU.mult)
                e.drain()
                e.tensor_tensor(scr, scr, scrb, ALU.subtract)
                e.drain()
                e.tensor_scalar(scr, scr, c.eps, None, op0=ALU.add)
                e.drain()
                e.reciprocal(r, scr)
                e.drain().then_inc(srow, 1)
                e.wait_ge(sbc, ci + 1)
                for f in range(c.FCH):
                    e.tensor_tensor(lnb[:, ci % 2, f, 0:w], xb[:, ci % 2, f, 0:w],
                                    b_bc1[:, 0:w], ALU.subtract)
                e.drain()
                for f in range(c.FCH):
                    e.tensor_tensor(lnb[:, ci % 2, f, 0:w], lnb[:, ci % 2, f, 0:w],
                                    b_bc2[:, 0:w], ALU.mult)
                e.drain().then_inc(sln, 1)
                for (tix, _ci, k) in tiles_of[ci]:
                    e.wait_ge(stok, tix + 1)
                    zt = b_tok[:, ts(tix % 4, 10)]
                    e.tensor_tensor(difc[:, 0:c.E], t["negmat"][:, 0:c.E],
                                    zt[:, 9:10].broadcast_to([128, c.E]), ALU.mult)
                    e.drain()
                    e.tensor_tensor(difc[:, 0:c.E], difc[:, 0:c.E], zt[:, 0:c.E], ALU.add)
                    e.drain()
                    e.tensor_reduce(mxc[:, 0:1], difc[:, 0:c.E], axis=X, op=ALU.max)
                    e.drain()
                    e.tensor_tensor(difc[:, 0:c.E], difc[:, 0:c.E],
                                    mxc[:, 0:1].broadcast_to([128, c.E]), ALU.subtract)
                    e.drain().then_inc(sdif, 1)
                    e.tensor_single_scalar(ohc[:, 0:c.E], difc[:, 0:c.E], 0.0, ALU.is_equal)
                    e.drain()
                    e.tensor_tensor(sc8[:, 0:c.E], ohc[:, 0:c.E], t["epatm9"][:, 0:c.E], ALU.mult)
                    e.drain()
                    e.tensor_reduce(t["tm_eidx"][:, tix:tix + 1], sc8[:, 0:c.E], axis=X, op=ALU.min)
                    e.drain()
                    e.tensor_scalar(t["tm_eidx"][:, tix:tix + 1], t["tm_eidx"][:, tix:tix + 1],
                                    9.0, None, op0=ALU.add)
                    e.drain()
                    e.tensor_tensor(ohc[:, 0:c.E],
                                    t["tm_eidx"][:, tix:tix + 1].broadcast_to([128, c.E]),
                                    t["epat8"][:, 0:c.E], ALU.is_equal)
                    e.drain()
                    e.tensor_copy(ohb[:, 0:c.E], ohc[:, 0:c.E])
                    if c.E < 8:
                        e.memset(ohb[:, c.E:8], 0.0)
                    e.drain().then_inc(soh, 1)
                    e.wait_ge(sexp, tix + 1)
                    e.tensor_reduce(smc[:, 0:1], exc[:, 0:c.E], axis=X, op=ALU.add)
                    e.drain()
                    e.reciprocal(t["tm_pmax"][:, tix:tix + 1], smc[:, 0:1])
                    e.drain()
                    e.wait_ge(spos, tix + 1)
                    e.tensor_copy(sc8[:, 0:c.E], b_cnt[:, 32:32 + c.E])
                    e.drain()
                    e.tensor_tensor(sc8[:, 0:c.E], sc8[:, 0:c.E],
                                    b_pos[:, ts(tix % 4, 8)][:, 0:c.E], ALU.add)
                    e.drain()
                    e.tensor_tensor(sc8[:, 0:c.E], sc8[:, 0:c.E], ohc[:, 0:c.E], ALU.mult)
                    e.drain()
                    e.tensor_reduce(t["tm_pos"][:, tix:tix + 1], sc8[:, 0:c.E], axis=X, op=ALU.add)
                    e.drain()
                    if tix == NTI - 1:
                        e.tensor_copy(t["agrow"][0:1, 0:8], t["cnt_row"][0:1, 0:8])
                        e.drain()
                    e.tensor_tensor(t["cnt_row"][0:1, 0:8], t["cnt_row"][0:1, 0:8],
                                    b_cnt[0:1, ts(tix % 4, 8)], ALU.add)
                    e.drain().then_inc(sposv, 1)
            e.drain().then_inc(sfin, 1)

        def scalar(e):
            for ci, (o, w) in enumerate(plan):
                e.wait_ge(vc, ci + 1)
                e.activation(xsq[:, ci % 2, :, 0:w], xbf[:, ci % 2, :, 0:w], AF.Square)
                e.drain().then_inc(sq, 1)
                e.wait_ge(srow, ci + 1)
                r = t["rR"][0:1, o:o + w]
                e.activation(r, r, AF.Sqrt)
                e.drain().then_inc(sqr, 1)
                for (tix, _ci, k) in tiles_of[ci]:
                    e.wait_ge(sdif, tix + 1)
                    e.copy(rcolsb[:], b_tok[:, ts(tix % 4, 10)][:, 8:9])
                    e.drain()
                    e.activation(exc[:, 0:c.E], difc[:, 0:c.E], AF.Exp, scale=rcolsb[:])
                    e.drain().then_inc(sexp, 1)

        def tensor(e):
            e.wait_ge(self.sem("cst"), 4)
            for ci, (o, w) in enumerate(plan):
                if ci >= 1:
                    e.wait_ge(srow, ci)
                e.wait_ge(vc, ci + 1)
                for f in range(c.FCH):
                    e.matmul(b_mu[0:1, 0:w], t["ones_col_bf"][:], xbf[:, ci % 2, f, 0:w],
                             start=(f == 0), stop=(f == c.FCH - 1))
                e.wait_ge(sq, ci + 1)
                for f in range(c.FCH):
                    mm = e.matmul(b_sq[0:1, 0:w], t["ones_col_bf"][:], xsq[:, ci % 2, f, 0:w],
                                  start=(f == 0), stop=(f == c.FCH - 1))
                mm.then_inc(smm, 1)
                if ci >= 1:
                    e.wait_ge(sln, ci)
                e.wait_ge(sqr, ci + 1)
                e.matmul(b_bc1[:, 0:w], t["ones_row_f"][0:1, :], t["muR"][0:1, o:o + w],
                         start=True, stop=True)
                e.matmul(b_bc2[:, 0:w], t["onesP"][32:33, :], t["rR"][0:1, o:o + w],
                         start=True, stop=True).then_inc(sbc, 1)
                for (tix, _ci, k) in tiles_of[ci]:
                    if tix >= 4:
                        e.wait_ge(sexp, tix - 3)
                    zt = b_tok[:, ts(tix % 4, 10)]
                    for f in range(c.FCH):
                        e.matmul(zt[:, 0:8], xbf[:, ci % 2, f, k:k + 128],
                                 t["wr_bf"][:, f * 8:(f + 1) * 8], start=(f == 0),
                                 stop=(f == c.FCH - 1))
                    e.matmul(zt[:, 8:9], t["rR"][0:1, o + k:o + k + 128],
                             t["onesP"][32:33, 0:1], start=True, stop=True)
                    e.matmul(zt[:, 9:10], t["muR"][0:1, o + k:o + k + 128], t["one1_f"][:],
                             start=True, stop=True).then_inc(stok, 1)
                    e.wait_ge(soh, tix + 1)
                    if tix >= 1:
                        e.wait_ge(sposv, tix)
                    if tix >= 4:
                        e.wait_ge(sposv, tix - 3)
                    pt = b_pos[:, ts(tix % 4, 8)]
                    e.matmul(pt, t["triU_bf"][:], ohb[:], start=True, stop=True)
                    e.matmul(b_cnt[:, 32:40], t["ones_row_f"][0:1, :], t["cnt_row"][0:1, 0:8],
                             start=True, stop=True)
                    ctile = b_cnt[0:1, ts(tix % 4, 8)]
                    e.matmul(ctile, t["ones_col_bf"][:], ohb[:], start=True, stop=True).then_inc(spos, 1)

        self.phase(sync=sync, vector=vector, scalar=scalar, tensor=tensor)
        return dict(plan=plan, tiles=tiles, NTI=NTI, sfin=sfin, slnw=slnw, ld=ld)

    def ph_postag(self, call):
        c, t, H = self.c, self.t, self.H
        pre = "pa_"
        agi = H["ag_in"] if call == 1 else H["ag3_in"]
        ago = H["ag_out"] if call == 1 else H["ag3_out"]
        rfin = self.sem("r_fin")
        rlnw = self.sem("r_lnw")
        NT, NR, NS = c.NTILE, c.NROW, c.NSLOT
        SFW = NS // 16
        SBK = NR // 16
        NCH = len(c.chunk_plan)

        sgd = self.sem(pre + "gd", sw=True)
        sag = self.sem(pre + "ag", sw=True)
        smm = self.sem(pre + "mm")
        svb = self.sem(pre + "vb")
        svo = self.sem(pre + "vo")
        svseg = self.sem(pre + "vseg")
        svtm = self.sem(pre + "vtm")
        sve = self.sem(pre + "ve")
        svA = self.sem(pre + "vA")
        svB = self.sem(pre + "vB")
        svc = self.sem(pre + "vc")
        str_ = self.sem(pre + "tr")
        sg1 = self.sem(pre + "g1")
        sg2 = self.sem(pre + "g2")
        sga = self.sem(pre + "ga")
        sdr = self.sem(pre + "dr")
        sdl = self.sem(pre + "dl")
        sdw = self.sem(pre + "dw")
        fin = self.sem(pre + "fin")

        b_base, b_seg, b_tr, b_h = self.banks[0], self.banks[1], self.banks[2], self.banks[3]
        trsc = self.sb("pa_trsc", [64, 128], F32)
        baserow = self.sb("pa_baserow", [1, 16], F32)
        adj2 = self.sb("pa_adj2", [128, 1], I32)
        adj2f = self.sb("pa_adj2f", [128, 1], F32)
        sc = self.sb("pa_sc", [128, 8 * NT], F32)
        segi = self.sb("pa_segi", [1, 16], I32)
        segf = self.sb("pa_segf", [1, 16], F32)
        tilesf = self.sb("pa_tilesf", [1, 16], F32)
        spansf = self.sb("pa_spansf", [1, 16], F32)
        cumf = self.sb("pa_cumf", [1, 32], F32)
        spantab = self.t["spantab"]
        fwdi = self.t["fwdi"]
        backi = self.t["backi"]
        self.arena_group("PA")
        lnld = self.at("lnld", [128, c.TW], F32)
        g1r = self.at("g1r", [16, NS], I16)
        desti = self.at("desti", [16, NR], I16)
        pdesti = self.at("pdesti", [16, NR], I16)
        iotad = self.at("iotad", [16, NR], I16)
        ilvp = self.at("ilvp2", [1, 2 * NS], I16)
        _mk = self.arena_mark()
        rowsf = self.at("rowsf", [1, 4 * NR], F32)
        self.arena_reset(_mk)
        gfo = self.at("gfo", [128, NS], F32)
        gfb = self.at("gfb", [128, NS], BF16)
        destrow = rowsf[0:1, 0:NR]
        pidxf = rowsf[0:1, NR:2 * NR]
        pmrow = rowsf[0:1, 2 * NR:3 * NR]
        scrow = rowsf[0:1, 3 * NR:4 * NR]
        grow = gfo[0:1, 0:NS]
        gfwd = ilvp[0:1, 0:NS]
        ilv = ilvp[0:1, NS:2 * NS]

        def gpsimd(e):
            e.wait_ge(rfin, 1)
            e.wait_ge(rlnw, 16 * NCH)
            e.dma_start(out=agi[0:1, 0:16], in_=t["agrow"][0:1, 0:16]).then_inc(sgd, 16)
            e.wait_ge(sgd, 16)
            e.collective_compute("AllGather", ALU.bypass,
                                 replica_groups=[list(range(c.n_cores))],
                                 ins=[agi[:]], outs=[ago[:]]).then_inc(sag, 1)
            e.wait_ge(sag, 1)
            e.dma_start(out=t["agres"][0:c.n_cores, :], in_=ago[0:c.n_cores, :]).then_inc(sgd, 16)
            e.iota(adj2[:], [[0, 1]], base=-1, channel_multiplier=1)
            e.drain().then_inc(sg1, 1)
            e.memset(iotad[:], 0)
            e.drain()
            e.iota(iotad[0:1, 0:NR], [[1, NR]], base=1, channel_multiplier=0)
            e.drain().then_inc(sg1, 1)
            e.load_library(LC.local_scatter)
            PIECE = 2016
            npiece = (NS + PIECE - 1) // PIECE
            for k in range(npiece):
                pw = min(PIECE, NS - k * PIECE)
                e.wait_ge(svA, k + 1)
                e.local_scatter(g1r[:, k * PIECE:k * PIECE + pw], iotad[:],
                                pdesti[:], channels=16, num_elems=pw, num_idxs=NR)
                e.drain().then_inc(sg2, 1)
            e.wait_ge(svB, 1)
            e.tensor_copy(ilv.rearrange("o (p s) -> o p s", p=16),
                          gfwd.rearrange("o (s p) -> o p s", p=16))
            e.drain()
            e.dma_start(out=H["rowscr"][0:1, 0:NS], in_=ilv).then_inc(sgd, 16)
            e.wait_ge(sgd, 48)
            for k in range(8):
                e.dma_start(out=fwdi[16 * k:16 * (k + 1), :],
                            in_=H["rowscr"][0:1, 0:NS].rearrange("o (p s) -> (o p) s", p=16)
                            ).then_inc(sgd, 16)
            e.tensor_copy(ilv[0:1, 0:NR].rearrange("o (p s) -> o p s", p=16),
                          desti[0:1, 0:NR].rearrange("o (s p) -> o p s", p=16))
            e.drain()
            e.wait_ge(sgd, 48 + 128)
            e.dma_start(out=H["rowscr"][0:1, 0:NR], in_=ilv[0:1, 0:NR]).then_inc(sgd, 16)
            e.wait_ge(sgd, 48 + 128 + 16)
            for k in range(8):
                e.dma_start(out=backi[16 * k:16 * (k + 1), :],
                            in_=H["rowscr"][0:1, 0:NR].rearrange("o (p s) -> (o p) s", p=16)
                            ).then_inc(sgd, 16)
            e.wait_ge(sgd, 48 + 256 + 16)
            e.load_library(LC.ap_gather)
            for f in range(c.FCH):
                e.wait_ge(sdl, 16 * (f + 1))
                if f >= 1:
                    e.wait_ge(svc, f)
                e.ap_gather(gfo[:], lnld[:], fwdi[:], channels=128, num_elems=c.TW,
                            d=1, num_idxs=NS)
                e.drain().then_inc(sga, 1)
            e.sem_inc(fin, 1)

        def tensor(e):
            e.wait_ge(sgd, 32)
            e.matmul(b_base[0:1, 0:8], t["maskB_sb"][:, 0:1], t["agres"][0:8, 0:8],
                     start=True, stop=True)
            e.matmul(b_h[0:2, 0:8], t["maskLR_sb"][:, 0:2], t["agres"][0:8, 0:8],
                     start=True, stop=True).then_inc(smm, 1)
            e.wait_ge(svb, 1)
            e.matmul(b_base[:, 8:16], t["ones_row_f"][0:1, :], baserow[0:1, 0:8],
                     start=True, stop=True).then_inc(smm, 1)
            e.wait_ge(svseg, 1)
            e.matmul(b_seg[:, 0:8], t["ones_row_f"][0:1, :], t["seg_start"][0:1, 0:8],
                     start=True, stop=True).then_inc(smm, 1)
            e.wait_ge(svtm, 1)
            for i, nm in enumerate(["tm_dsttm", "tm_pmax", "tm_kept"]):
                if i >= 1:
                    e.wait_ge(sve, i)
                e.transpose(b_tr[0:NT, 0:128], t[nm][:, 0:NT], t["ident_f"][:]).then_inc(str_, 1)
                e.drain()

        def vector(e):
            e.wait_ge(smm, 1)
            e.wait_ge(rlnw, 16 * NCH)
            e.tensor_copy(baserow[0:1, 0:8], b_base[0:1, 0:8])
            e.tensor_copy(t["halo2"][0:2, 0:8], b_h[0:2, 0:8])
            e.drain().then_inc(svb, 1)
            e.wait_ge(rfin, 1)
            e.tensor_tensor(t["ohf"][:].rearrange("p (n e) -> p n e", e=8),
                            t["tm_eidx"][:, 0:NT].rearrange("p (n o) -> p n o", o=1)
                            .broadcast_to([128, NT, 8]),
                            t["epat8"][:].rearrange("p (o e) -> p o e", o=1)
                            .broadcast_to([128, NT, 8]), ALU.is_equal)
            e.drain().then_inc(svo, 1)
            e.wait_ge(smm, 2)
            e.tensor_tensor(sc[:].rearrange("p (n e) -> p n e", e=8),
                            t["ohf"][:].rearrange("p (n e) -> p n e", e=8),
                            b_base[:, 8:16].rearrange("p (o e) -> p o e", o=1)
                            .broadcast_to([128, NT, 8]), ALU.mult)
            e.drain()
            e.tensor_reduce(t["tm_gpos"][:, 0:NT], sc[:].rearrange("p (n e) -> p n e", e=8),
                            axis=X, op=ALU.add)
            e.drain()
            e.tensor_tensor(t["tm_gpos"][:, 0:NT], t["tm_gpos"][:, 0:NT],
                            t["tm_pos"][:, 0:NT], ALU.add)
            e.drain()
            if call == 1:
                e.wait_ge(sg1, 1)
                e.tensor_copy(adj2f[0:2, :], adj2[0:2, :])
                e.drain()
                e.tensor_tensor(sc[0:2, 0:8], t["ohf"][0:2, (NT - 1) * 8:NT * 8],
                                t["halo2"][0:2, 0:8], ALU.mult)
                e.drain()
                e.tensor_reduce(sc[0:2, 8:9], sc[0:2, 0:8], axis=X, op=ALU.add)
                e.drain()
                e.tensor_tensor(t["tm_gpos"][0:2, NT - 1:NT], sc[0:2, 8:9],
                                adj2f[0:2, 0:1], ALU.add)
                e.drain()
            e.tensor_single_scalar(t["tm_kept"][:, 0:NT], t["tm_gpos"][:, 0:NT],
                                   float(c.cap), ALU.is_lt)
            e.drain()
            e.tensor_tensor(t["tm_pmax"][:, 0:NT], t["tm_pmax"][:, 0:NT],
                            t["tm_kept"][:, 0:NT], ALU.mult)
            e.drain()
            e.tensor_scalar(t["tm_pmax"][:, 0:NT], t["tm_pmax"][:, 0:NT], 0.5, None,
                            op0=ALU.mult)
            e.tensor_scalar(t["tm_kept"][:, 0:NT], t["tm_kept"][:, 0:NT], -0.5, 0.5,
                            op0=ALU.mult, op1=ALU.add)
            e.drain()
            e.tensor_copy(segi[0:1, 0:8], t["cnt_row"][0:1, 0:8])
            e.drain()
            e.tensor_scalar(segi[0:1, 0:8], segi[0:1, 0:8], 127, None, op0=ALU.add)
            e.drain()
            e.tensor_scalar(segi[0:1, 0:8], segi[0:1, 0:8], 7, None, op0=ALU.arith_shift_right)
            e.drain()
            e.tensor_scalar_max(segi[0:1, 0:8], segi[0:1, 0:8], 1)
            e.drain()
            e.tensor_copy(tilesf[0:1, 0:8], segi[0:1, 0:8])
            e.drain()
            e.tensor_scalar(tilesf[0:1, 0:8], tilesf[0:1, 0:8], 128.0, None, op0=ALU.mult)
            e.drain()
            e.memset(segf[0:1, 0:16], 1.0)
            e.drain()
            e.tensor_tensor_scan(t["seg_start"][0:1, 0:8], segf[0:1, 0:8], tilesf[0:1, 0:8],
                                 0.0, op0=ALU.mult, op1=ALU.add)
            e.drain()
            e.tensor_tensor(t["seg_start"][0:1, 0:8], t["seg_start"][0:1, 0:8],
                            tilesf[0:1, 0:8], ALU.subtract)
            e.drain().then_inc(svseg, 1)
            e.tensor_scalar(segi[0:1, 0:8], segi[0:1, 0:8], 1, None, op0=ALU.add)
            e.drain()
            e.tensor_scalar(segi[0:1, 0:8], segi[0:1, 0:8], 1, None, op0=ALU.arith_shift_right)
            e.drain()
            e.tensor_copy(spansf[0:1, 0:8], segi[0:1, 0:8])
            e.drain()
            e.tensor_tensor_scan(cumf[0:1, 1:9], segf[0:1, 0:8], spansf[0:1, 0:8],
                                 0.0, op0=ALU.mult, op1=ALU.add)
            e.drain()
            e.memset(cumf[0:1, 0:1], 0.0)
            e.drain()
            e.tensor_copy(spantab[0:1, 0:9], cumf[0:1, 0:9])
            e.drain()
            e.tensor_scalar(cumf[0:1, 16:24], cumf[0:1, 0:8], -256.0, None, op0=ALU.mult)
            e.drain()
            e.tensor_tensor(cumf[0:1, 16:24], cumf[0:1, 16:24], t["seg_start"][0:1, 0:8],
                            ALU.add)
            e.drain()
            e.tensor_copy(spantab[0:1, 16:24], cumf[0:1, 16:24])
            e.drain()
            e.wait_ge(smm, 3)
            e.tensor_tensor(sc[:].rearrange("p (n e) -> p n e", e=8),
                            t["ohf"][:].rearrange("p (n e) -> p n e", e=8),
                            b_seg[:, 0:8].rearrange("p (o e) -> p o e", o=1)
                            .broadcast_to([128, NT, 8]), ALU.mult)
            e.drain()
            e.tensor_reduce(t["tm_dsttm"][:, 0:NT], sc[:].rearrange("p (n e) -> p n e", e=8),
                            axis=X, op=ALU.add)
            e.drain()
            e.tensor_tensor(t["tm_dsttm"][:, 0:NT], t["tm_dsttm"][:, 0:NT],
                            t["tm_pos"][:, 0:NT], ALU.add)
            e.drain().then_inc(svtm, 1)
            for i in range(3):
                e.wait_ge(str_, i + 1)
                if i >= 1:
                    e.wait_ge(sdr, 16 * i)
                e.tensor_copy(trsc[0:NT, :], b_tr[0:NT, 0:128])
                e.drain().then_inc(sve, 1)
            e.wait_ge(sdr, 16)
            e.wait_ge(sg1, 2)
            e.memset(desti[:], -1)
            e.drain()
            e.tensor_copy(desti[0:1, 0:NR], destrow)
            e.drain()
            PIECE = 2016
            npiece = (NS + PIECE - 1) // PIECE
            e.memset(pdesti[:], -1)
            e.drain()
            for k in range(npiece):
                pw = min(PIECE, NS - k * PIECE)
                if k >= 1:
                    e.wait_ge(sg2, k)
                e.tensor_scalar(pidxf, destrow, float(-k * PIECE),
                                None, op0=ALU.add)
                e.drain()
                e.tensor_single_scalar(pmrow, pidxf, -0.5, ALU.is_gt)
                e.drain()
                e.tensor_single_scalar(scrow, pidxf, float(pw) - 0.5,
                                       ALU.is_lt)
                e.drain()
                e.tensor_tensor(pmrow, pmrow, scrow,
                                ALU.mult)
                e.drain()
                e.tensor_tensor(pidxf, pidxf, pmrow, ALU.mult)
                e.drain()
                e.tensor_tensor(pidxf, pidxf, pmrow, ALU.add)
                e.drain()
                e.tensor_scalar(pidxf, pidxf, -1.0, None, op0=ALU.add)
                e.drain()
                e.tensor_copy(pdesti[0:1, 0:NR], pidxf)
                e.drain().then_inc(svA, 1)
            e.wait_ge(sg2, npiece)
            e.tensor_copy(grow[0:1, 0:NS], g1r[0:1, 0:NS])
            e.drain()
            e.tensor_scalar(grow[0:1, 0:NS], grow[0:1, 0:NS], -1.0, None, op0=ALU.add)
            e.drain()
            e.tensor_scalar_max(grow[0:1, 0:NS], grow[0:1, 0:NS], 0.0)
            e.drain()
            e.tensor_copy(gfwd[0:1, 0:NS], grow[0:1, 0:NS])
            e.drain().then_inc(svB, 1)
            for f in range(c.FCH):
                e.wait_ge(sga, f + 1)
                if f >= 1:
                    e.wait_ge(sdw, 16 * f)
                e.tensor_copy(gfb[:], gfo[:])
                e.drain().then_inc(svc, 1)

        def sync(e):
            targets = [destrow, H["pmkrow"][0:1, 0:NR], H["fbrow"][0:1, 0:NR]]
            for i in range(3):
                e.wait_ge(sve, i + 1)
                if i >= 1:
                    e.wait_ge(sdr, 16 * i)
                e.dma_start(out=targets[i], in_=trsc[0:NT, :]).then_inc(sdr, 16)
            for f in range(c.FCH):
                e.wait_ge(rlnw, 16 * NCH)
                e.wait_ge(svB, 1)
                if f >= 1:
                    e.wait_ge(sga, f)
                    e.wait_ge(sdl, 16 * f)
                e.dma_start(out=lnld[:], in_=H["lnh"][f * 128:(f + 1) * 128, :]).then_inc(sdl, 16)
                e.wait_ge(svc, f + 1)
                if f >= 1:
                    e.wait_ge(sdw, 16 * f)
                e.dma_start(out=H["glh"][f * 128:(f + 1) * 128, :], in_=gfb[:]).then_inc(sdw, 16)

        self.phase(gpsimd=gpsimd, tensor=tensor, vector=vector, sync=sync)
        return dict(fin=fin, sdw=sdw)

    def ph_moe_ffn(self, call):
        c, t, H = self.c, self.t, self.H
        pre = "mf_"
        pafin = self.sem("pa_fin")
        padw = self.sem("pa_dw")
        E, FCH, FCH_F = c.E, c.FCH, c.FCH_F
        SP = c.SPAN
        GW = FCH * SP
        OW = FCH * SP

        sptab = self.sem(pre + "ptab")
        swr1 = self.sem(pre + "w1", sw=True)
        swr2 = self.sem(pre + "w2", sw=True)
        se2 = self.sem(pre + "e2")
        sgl = self.sem(pre + "gl")
        sm1 = self.sem(pre + "m1")
        sh = self.sem(pre + "h")
        sm2 = self.sem(pre + "m2")
        sm2v = self.sem(pre + "m2v")
        sew = self.sem(pre + "ew")
        fin = self.sem(pre + "fin")

        self.arena_group("MO")
        w1sb = self.at("w1sb", [128, FCH * c.F], BF16)
        w2sb = self.at("w2sb", [128, FCH_F * c.D], BF16)
        glbuf = self.at("glbuf", [128, 2 * FCH * SP], BF16)
        hbuf = self.at("hbuf", [128, FCH_F * SP], BF16)
        o2buf = self.at("o2buf", [128, FCH * SP], F32)
        zero512 = self.at("zero512", [128, 512], F32)
        spt = t["spantab"]

        def loadex(e, ex, need_off=True):
            lo = e.value_load(spt[0:1, ex:ex + 1], min_val=0, max_val=c.MAXSPANS)
            hi = e.value_load(spt[0:1, ex + 1:ex + 2], min_val=0, max_val=c.MAXSPANS)
            offv = (e.value_load(spt[0:1, 16 + ex:17 + ex], min_val=-c.NSLOT, max_val=c.NSLOT)
                    if need_off else None)
            return lo, hi, offv

        szr = self.sem(pre + "zr", sw=True)

        def gpsimd(e):
            e.wait_ge(pafin, 1)
            e.wait_ge(padw, 16 * FCH)
            e.memset(zero512[:], 0.0)
            e.drain()
            e.sem_inc(sptab, 1)
            cumE = e.value_load(spt[0:1, E:E + 1], min_val=0, max_val=c.MAXSPANS)
            offl = e.value_load(spt[0:1, 16 + E - 1:16 + E], min_val=-c.NSLOT, max_val=c.NSLOT)
            wv = self.nc.s_assert_within(e.snap(cumE * SP + offl), c.WVMIN, c.NSLOT,
                                         skip_runtime_assert=True)
            for f in range(FCH):
                for j in range(c.NZCH):
                    e.dma_start(out=H["eout"][f * 128:(f + 1) * 128, ds(wv + j * 512, 512)],
                                in_=zero512[:]).then_inc(szr, 16)
            for ex in range(E):
                if ex >= 1:
                    e.wait_ge(se2, ex)
                e.dma_start(out=w1sb[:].rearrange("p (k f) -> p k f", k=FCH),
                            in_=H["W1"][ex * c.D:(ex + 1) * c.D, :]
                            .rearrange("(k p) f -> p k f", p=128)).then_inc(swr1, 16)
                e.dma_start(out=w2sb[:].rearrange("p (k f) -> p k f", k=FCH_F),
                            in_=H["W2"][ex * c.F:(ex + 1) * c.F, :]
                            .rearrange("(k p) f -> p k f", p=128)).then_inc(swr2, 16)

        def sync(e):
            e.wait_ge(sptab, 1)
            e.wait_ge(padw, 16 * FCH)
            for ex in range(E):
                lo, hi, offv = loadex(e, ex)
                with e.Fori(lo, hi) as i:
                    e.wait_ge(sm1, smax((i - 1) * FCH_F, 0))
                    e.wait_ge(sgl, i * 16)
                    col = self.nc.s_assert_within(e.snap(i * SP + offv), 0,
                                                  c.NSLOT - SP, skip_runtime_assert=True)
                    e.dma_start(out=glbuf[:, ds((i % 2) * GW, GW)]
                                .rearrange("p (f s) -> p f s", f=FCH),
                                in_=H["glh"][:, ds(col, SP)]
                                .rearrange("(f p) s -> p f s", p=128)).then_inc(sgl, 16)

        def tensor(e):
            e.wait_ge(sptab, 1)
            for ex in range(E):
                lo, hi, offv = loadex(e, ex, need_off=False)
                e.wait_ge(swr1, 16 * (ex + 1))
                with e.Fori(lo, hi) as i:
                    e.wait_ge(sgl, (i + 1) * 16)
                    for m in range(FCH_F):
                        gm = i * FCH_F + m
                        e.wait_ge(sh, smax(gm - 3, 0))
                        for k in range(FCH):
                            mm = e.matmul(self.banks[m % 4][:, 0:SP],
                                          w1sb[:, k * c.F + m * 128:k * c.F + (m + 1) * 128],
                                          glbuf[:, ds((i % 2) * GW + k * SP, SP)],
                                          start=(k == 0), stop=(k == FCH - 1))
                        mm.then_inc(sm1, 1)
                    e.wait_ge(swr2, 16 * (ex + 1))
                    e.wait_ge(sh, (i + 1) * FCH_F)
                    for m2 in range(FCH):
                        gm2 = i * FCH + m2
                        e.wait_ge(sm2v, smax(gm2 - 3, 0))
                        for f in range(FCH_F):
                            mm = e.matmul(self.banks[4 + m2 % 4][:, 0:SP],
                                          w2sb[:, f * c.D + m2 * 128:f * c.D + (m2 + 1) * 128],
                                          hbuf[:, ds(f * SP, SP)],
                                          start=(f == 0), stop=(f == FCH_F - 1))
                        mm.then_inc(sm2, 1)
                    e.drain()
                e.sem_inc(se2, 1)

        def scalar_direct(e):
            e.wait_ge(sptab, 1)
            for ex in range(E):
                lo, hi, offv = loadex(e, ex)
                with e.Fori(lo, hi) as i:
                    e.wait_ge(sm2, i * FCH)
                    for m in range(FCH_F):
                        gm = i * FCH_F + m
                        e.wait_ge(sm1, gm + 1)
                        e.activation(hbuf[:, ds(m * SP, SP)], self.banks[m % 4][:, 0:SP],
                                     AF.Relu)
                        e.drain().then_inc(sh, 1)
                    e.wait_ge(sm2v, (i + 1) * FCH)
                    e.wait_ge(sew, i * 16)
                    col = self.nc.s_assert_within(e.snap(i * SP + offv), 0,
                                                  c.NSLOT - SP, skip_runtime_assert=True)
                    e.dma_start(out=H["eout"][:, ds(col, SP)]
                                .rearrange("(f p) s -> p f s", p=128),
                                in_=o2buf[:, 0:OW]
                                .rearrange("p (f s) -> p f s", f=FCH)).then_inc(sew, 16)

        def vector(e):
            e.wait_ge(sptab, 1)
            for ex in range(E):
                lo, hi, offv = loadex(e, ex, need_off=False)
                with e.Fori(lo, hi) as i:
                    e.wait_ge(sew, i * 16)
                    for m2 in range(FCH):
                        gm2 = i * FCH + m2
                        e.wait_ge(sm2, gm2 + 1)
                        e.tensor_copy(o2buf[:, ds(m2 * SP, SP)],
                                      self.banks[4 + m2 % 4][:, 0:SP])
                        e.drain().then_inc(sm2v, 1)
            e.drain().then_inc(fin, 1)

        self.phase(gpsimd=gpsimd, sync=sync, tensor=tensor, scalar=scalar_direct,
                   vector=vector)
        return dict(fin=fin, sew=sew, szr=szr)

    def ph_combine(self, call):
        c, t, H = self.c, self.t, self.H
        pre = "cb_"
        mffin = self.sem("mf_fin")
        sew = self.sem("mf_ew")
        NR, NS = c.NROW, c.NSLOT
        src_h = H["xs"] if call == 1 else H["x3"]
        dst_h = H["R2"] if call == 1 else H["out"]
        plan = c.chunk_plan if call == 1 else c.chunk_plan[:-1]
        W = c.TW if call == 1 else c.T
        spt = t["spantab"]

        sel = self.sem(pre + "el")
        sg = self.sem(pre + "g")
        sx = self.sem(pre + "x")
        sl = self.sem(pre + "l")
        sbc = self.sem(pre + "bc")
        sv = self.sem(pre + "v")
        sw = self.sem(pre + "w")
        srt = self.sem(pre + "rt", sw=True)
        fin = self.sem(pre + "fin")

        self.arena_group("CB")
        eosl = self.at("eosl", [128, NS], F32)
        ge = self.at("ge", [128, NR], F32)
        xsl = self.at("xsl", [128, c.TW], F32)
        lnl = self.at("lnl", [128, c.TW], F32)
        osl = self.at("osl", [128, c.TW], F32)
        rowbf = self.at("rowbf", [128, NR], BF16)
        b_p, b_f = self.banks[6], self.banks[7]
        NCHU = len(plan)
        FCH = c.FCH

        sdr_pa = self.sem("pa_dr")

        szr = self.sem("mf_zr")

        def entry(e):
            e.wait_ge(mffin, 1)
            e.wait_ge(sdr_pa, 48)
            e.wait_ge(szr, 16 * c.FCH * c.NZCH)
            cumE = e.value_load(spt[0:1, c.E:c.E + 1], min_val=0, max_val=c.MAXSPANS)
            e.wait_ge(sew, cumE * 16)

        def sync(e):
            entry(e)
            for f in range(FCH):
                if f >= 1:
                    e.wait_ge(sg, f)
                    e.wait_ge(sel, 16 * f)
                e.dma_start(out=eosl[:], in_=H["eout"][f * 128:(f + 1) * 128, 0:NS]
                            ).then_inc(sel, 16)
                if f >= 1:
                    e.wait_ge(sv, f * NCHU)
                    e.wait_ge(sx, 16 * f)
                e.dma_start(out=xsl[:, 0:c.TW], in_=src_h[f * 128:(f + 1) * 128, :]
                            ).then_inc(sx, 16)
                if f >= 1:
                    e.wait_ge(sl, 16 * f)
                e.dma_start(out=lnl[:, 0:c.TW], in_=H["lnh"][f * 128:(f + 1) * 128, :]
                            ).then_inc(sl, 16)

        def gpsimd(e):
            entry(e)
            e.dma_start(out=rowbf[0:1, 0:NR], in_=H["pmkrow"][0:1, 0:NR]).then_inc(srt, 16)
            e.dma_start(out=rowbf[32:33, 0:NR], in_=H["fbrow"][0:1, 0:NR]).then_inc(srt, 16)
            e.load_library(LC.ap_gather)
            for f in range(FCH):
                e.wait_ge(sel, 16 * (f + 1))
                if f >= 1:
                    e.wait_ge(sv, f * NCHU)
                e.ap_gather(ge[:, 0:NR], eosl[:], t["backi"][:], channels=128,
                            num_elems=NS, d=1, num_idxs=NR)
                e.drain().then_inc(sg, 1)

        def tensor(e):
            entry(e)
            e.wait_ge(srt, 32)
            for f in range(FCH):
                for ci, (o, w) in enumerate(plan):
                    gi = f * NCHU + ci
                    if gi >= 1:
                        e.wait_ge(sv, gi)
                    e.matmul(b_p[:, 0:w], t["onesP_bf"][0:1, :], rowbf[0:1, o:o + w],
                             start=True, stop=True)
                    e.matmul(b_f[:, 0:w], t["onesP_bf"][32:33, :], rowbf[32:33, o:o + w],
                             start=True, stop=True).then_inc(sbc, 1)

        def vector(e):
            e.wait_ge(mffin, 1)
            for f in range(FCH):
                e.wait_ge(sg, f + 1)
                e.wait_ge(sx, 16 * (f + 1))
                e.wait_ge(sl, 16 * (f + 1))
                if f >= 1:
                    e.wait_ge(sw, 16 * f)
                for ci, (o, w) in enumerate(plan):
                    gi = f * NCHU + ci
                    e.wait_ge(sbc, gi + 1)
                    e.tensor_tensor(osl[:, o:o + w], ge[:, o:o + w], b_p[:, 0:w], ALU.mult)
                    e.drain()
                    e.tensor_tensor(osl[:, o:o + w], osl[:, o:o + w], xsl[:, o:o + w], ALU.add)
                    e.drain()
                    e.tensor_tensor(lnl[:, o:o + w], lnl[:, o:o + w], b_f[:, 0:w], ALU.mult)
                    e.drain()
                    e.tensor_tensor(osl[:, o:o + w], osl[:, o:o + w], lnl[:, o:o + w], ALU.add)
                    e.drain().then_inc(sv, 1)

        def scalar(e):
            entry(e)
            for f in range(FCH):
                e.wait_ge(sv, (f + 1) * NCHU)
                if f >= 1:
                    e.wait_ge(sw, 16 * f)
                e.dma_start(out=dst_h[f * 128:(f + 1) * 128, 0:W], in_=osl[:, 0:W]
                            ).then_inc(sw, 16)
            e.wait_ge(sw, 16 * FCH)
            e.sem_inc(fin, 1)

        self.phase(sync=sync, gpsimd=gpsimd, tensor=tensor, vector=vector, scalar=scalar)
        return dict(fin=fin)

    def ph_attn_front(self):
        c, t, H = self.c, self.t, self.H
        FCH, L, nch = c.FCH, c.L, c.nch
        plan = c.chunk_plan
        NC = len(plan)
        a4w = self.sem("a4_w", sw=True)
        tb1 = self.sem("a4_tb1")
        tb2 = self.sem("a4_tb2")
        al = self.sem("a4_l")
        av1 = self.sem("a4_v1")
        asq = self.sem("a4_sq")
        amm = self.sem("a4_mm")
        arow1 = self.sem("a4_row1")
        asq1 = self.sem("a4_sq1")
        arow2 = self.sem("a4_row2")
        asq2 = self.sem("a4_sq2")
        arow = self.sem("a4_row")
        abc = self.sem("a4_bc")
        at1 = self.sem("a4_t1")
        ah = self.sem("a4_h")
        ahw = self.sem("a4_hw")
        ax1 = self.sem("a4_x1")
        axe = self.sem("a4_xe")
        arw = self.sem("a4_rw")

        self.arena_group("AT")
        x1T = self.at("x1T", [128, FCH, c.T + 2], BF16)
        ubf = self.at("ubf", [64, c.T], BF16)
        Btab = self.at("Btab", [64, (L + 1) * 64], BF16)
        ALp = self.at("ALp", [64, nch * 64], BF16)
        Broll = self.at("Broll", [64, 128], F32)
        ALf = self.at("ALf", [64, 64], F32)
        Fb = self.at("Fb", [64, max(nch, 16)], BF16)
        SarrB = self.at("SarrB", [64, max(nch, 16)], BF16)
        ag2row = self.at("ag2row", [1, 128], F32)
        ag2sb = self.at("ag2sb", [8, 128], F32)
        sinrow = self.at("sinrow", [1, 128], F32)
        sincolB = self.at("sincolB", [64, 16], BF16)
        colbuf = self.at("colbuf", [64, 16], F32)
        amsb = self.at("amsb", [64, 64], F32)
        bmsb = self.at("bmsb", [128, FCH, c.ST], BF16)
        convsb = self.at("convsb", [128, 3 * FCH], F32)
        lg = self.at("lg", [128, FCH], F32)
        lb = self.at("lb", [128, FCH], F32)
        w1bsb = self.at("w1bsb", [128, FCH], F32)
        w1bLRsb = self.at("w1bLRsb", [128, 2 * FCH], F32)
        cmasksb = self.at("cmasksb", [128, 2], F32)
        sibselsb = self.at("sibselsb", [8, 16], F32)
        zb6 = self.at("zb6", [64, 128], BF16)
        self.AT_MARK4 = self.arena_mark()
        w1sbA = self.at("w1sbA", [128, FCH, c.D], BF16)
        R2c4 = self.at("R2c4", [128, FCH, 512], F32)
        sqh = self.at("sqh", [128, FCH, 512], BF16)
        rowsc = self.at("rowsc", [1, 2048], F32)
        self.AT_MARK5 = self.arena_mark()
        xbf = self.at("xbf4", [128, FCH, 512], BF16)
        scr = t["scr512"]
        b_sx, b_sq2, b_wq, b_mwq = self.banks[7], self.banks[4], self.banks[5], self.banks[6]

        def gpsimd(e):
            loads = [
                (w1sbA[:], H["w1T"][:].rearrange("(k p) o -> p k o", p=128)),
                (bmsb[:], H["BmT"][:].rearrange("(k p) s -> p k s", p=128)),
                (amsb[:], H["Amat"][:]),
                (convsb[:], H["convw"][:]),
                (lg[:], H["ln1g"][:]), (lb[:], H["ln1b"][:]),
                (w1bsb[:], H["w1b"][:]), (w1bLRsb[:], H["w1bLR"][:]),
                (cmasksb[:], H["convmask"][:]), (sibselsb[:, 0:1], H["sibsel"][:]),
            ]
            for o, i in loads:
                e.dma_start(out=o, in_=i).then_inc(a4w, 16)
        self.NA4W = 160

        def tensor(e):
            e.wait_ge(a4w, self.NA4W)
            for k in range(1, L + 1):
                e.wait_ge(tb2, k)
                e.matmul(self.banks[0][0:64, 0:64], amsb[:],
                         Broll[:, ((k - 1) % 2) * 64:((k - 1) % 2) * 64 + 64],
                         start=True, stop=True).then_inc(tb1, 1)
            e.wait_ge(tb2, L + 1)
            e.matmul(self.banks[1][0:64, 0:64], Broll[:, (L % 2) * 64:(L % 2) * 64 + 64],
                     t["ident_f"][0:64, 0:64], start=True, stop=True).then_inc(tb1, 1)
            for k2 in range(1, nch):
                e.wait_ge(tb2, L + 2 + k2)
                e.matmul(self.banks[0][0:64, 0:64], ALf[:],
                         Broll[:, ((k2 - 1) % 2) * 64:((k2 - 1) % 2) * 64 + 64],
                         start=True, stop=True).then_inc(tb1, 1)
            for ci, (o, w) in enumerate(plan):
                e.wait_ge(av1, ci + 1)
                e.wait_ge(asq, ci + 1)
                for k in range(FCH):
                    e.matmul(b_sx[0:1, 0:w], t["ones_col_bf"][:], xbf[:, k, 0:w],
                             start=(k == 0), stop=(k == FCH - 1))
                for k in range(FCH):
                    mm = e.matmul(b_sq2[0:1, 0:w], t["ones_col_bf"][:], sqh[:, k, 0:w],
                                  start=(k == 0), stop=(k == FCH - 1))
                mm.then_inc(amm, 1)
                e.wait_ge(arow, ci + 1)
                e.matmul(b_wq[:, 0:w], t["ones_row_f"][0:1, :], rowsc[0:1, 1536:1536 + w],
                         start=True, stop=True)
                e.matmul(b_mwq[:, 0:w], t["ones_row_f"][0:1, :], scr[0:1, 0:w],
                         start=True, stop=True).then_inc(abc, 1)
                e.wait_ge(ah, ci + 1)
                for m in range(FCH):
                    if ci * FCH + m >= 4:
                        e.wait_ge(axe, ci * FCH + m - 3)
                    for k in range(FCH):
                        mm = e.matmul(self.banks[m % 4][:, 0:w],
                                      w1sbA[:, k, m * 128:(m + 1) * 128],
                                      sqh[:, k, 0:w], start=(k == 0), stop=(k == FCH - 1))
                    mm.then_inc(ax1, 1)

        def scalar(e):
            e.wait_ge(a4w, self.NA4W)
            e.activation(Broll[:, 0:64], t["ident_f"][0:64, 0:64], AF.Identity)
            e.activation(Btab[:, 0:64], t["ident_f"][0:64, 0:64], AF.Identity)
            e.drain().then_inc(tb2, 1)
            for k in range(1, L + 1):
                e.wait_ge(tb1, k)
                e.activation(Broll[:, (k % 2) * 64:(k % 2) * 64 + 64],
                             self.banks[0][0:64, 0:64], AF.Identity)
                e.activation(Btab[:, k * 64:(k + 1) * 64], self.banks[0][0:64, 0:64], AF.Identity)
                e.drain().then_inc(tb2, 1)
            e.wait_ge(tb1, L + 1)
            e.activation(ALf[:], self.banks[1][0:64, 0:64], AF.Identity)
            e.drain().then_inc(tb2, 1)
            e.activation(Broll[:, 0:64], t["ident_f"][0:64, 0:64], AF.Identity)
            e.activation(ALp[:, 0:64], t["ident_f"][0:64, 0:64], AF.Identity)
            e.drain().then_inc(tb2, 1)
            for k2 in range(1, nch):
                e.wait_ge(tb1, L + 1 + k2)
                e.activation(Broll[:, (k2 % 2) * 64:(k2 % 2) * 64 + 64],
                             self.banks[0][0:64, 0:64], AF.Identity)
                e.activation(ALp[:, k2 * 64:(k2 + 1) * 64], self.banks[0][0:64, 0:64], AF.Identity)
                e.drain().then_inc(tb2, 1)
            for ci, (o, w) in enumerate(plan):
                e.wait_ge(al, 16 * (ci + 1))
                if ci >= 1:
                    e.wait_ge(ax1, ci * FCH)
                    e.wait_ge(ahw, 16 * min(ci, 8))
                e.activation(sqh[:, :, 0:w], R2c4[:, :, 0:w], AF.Square)
                e.drain().then_inc(asq, 1)
                e.wait_ge(arow1, ci + 1)
                if ci >= 1:
                    e.wait_ge(arw, 32 * min(ci, 8))
                e.activation(rowsc[0:1, 1024:1024 + w], scr[0:1, 0:w], AF.Sqrt)
                e.drain().then_inc(asq1, 1)
                e.wait_ge(arow2, ci + 1)
                e.activation(rowsc[0:1, 1536:1536 + w], scr[0:1, 0:w], AF.Sqrt)
                e.drain().then_inc(asq2, 1)
                e.wait_ge(at1, ci + 1)
                for f in range(FCH):
                    e.activation(sqh[:, f, 0:w], sqh[:, f, 0:w], AF.Identity,
                                 bias=lb[:, f:f + 1], scale=lg[:, f:f + 1])
                e.drain().then_inc(ah, 1)
                if ci < NC - 1:
                    if ci >= 1:
                        e.wait_ge(ahw, 16 * ci)
                    e.dma_start(out=H["hh"][:, o:o + w].rearrange("(k p) t -> p k t", p=128),
                                in_=sqh[:, :, 0:w]).then_inc(ahw, 16)
                for m in range(FCH):
                    e.wait_ge(ax1, ci * FCH + m + 1)
                    if ci < NC - 1:
                        e.activation(x1T[:, m, 1 + o:1 + o + w], self.banks[m % 4][:, 0:w],
                                     AF.Identity, bias=w1bsb[:, m:m + 1])
                    else:
                        e.activation(x1T[:, m, 0:1], self.banks[m % 4][:, 0:1],
                                     AF.Identity, bias=w1bLRsb[:, m:m + 1],
                                     scale=cmasksb[:, 0:1])
                        e.activation(x1T[:, m, c.T + 1:c.T + 2], self.banks[m % 4][:, 1:2],
                                     AF.Identity, bias=w1bLRsb[:, FCH + m:FCH + m + 1],
                                     scale=cmasksb[:, 1:2])
                    e.drain().then_inc(axe, 1)

        def vector(e):
            for ci, (o, w) in enumerate(plan):
                e.wait_ge(al, 16 * (ci + 1))
                if ci >= 1:
                    e.wait_ge(ah, ci)
                e.tensor_copy(xbf[:, :, 0:w], R2c4[:, :, 0:w])
                e.drain().then_inc(av1, 1)
                e.wait_ge(amm, ci + 1)
                if ci >= 1:
                    e.wait_ge(arw, 32 * min(ci, 8))
                e.tensor_scalar(rowsc[0:1, 0:w], b_sx[0:1, 0:w], 1.0 / c.D, None, op0=ALU.mult)
                e.tensor_scalar(scr[0:1, 0:w], b_sq2[0:1, 0:w], 1.0 / c.D, None, op0=ALU.mult)
                e.drain()
                e.tensor_tensor(rowsc[0:1, 512:512 + w], rowsc[0:1, 0:w], rowsc[0:1, 0:w],
                                ALU.mult)
                e.drain()
                e.tensor_tensor(rowsc[0:1, 512:512 + w], scr[0:1, 0:w],
                                rowsc[0:1, 512:512 + w], ALU.subtract)
                e.drain()
                e.tensor_scalar(scr[0:1, 0:w], rowsc[0:1, 512:512 + w], c.eps, None,
                                op0=ALU.add)
                e.drain().then_inc(arow1, 1)
                e.wait_ge(asq1, ci + 1)
                e.reciprocal(rowsc[0:1, 1024:1024 + w], rowsc[0:1, 1024:1024 + w])
                e.drain()
                e.tensor_tensor(scr[0:1, 0:w], rowsc[0:1, 1024:1024 + w],
                                rowsc[0:1, 1024:1024 + w], ALU.mult)
                e.drain()
                e.tensor_tensor(scr[0:1, 0:w], rowsc[0:1, 512:512 + w], scr[0:1, 0:w],
                                ALU.mult)
                e.drain()
                e.tensor_scalar(scr[0:1, 0:w], scr[0:1, 0:w], c.eps, None, op0=ALU.add)
                e.drain().then_inc(arow2, 1)
                e.wait_ge(asq2, ci + 1)
                e.reciprocal(rowsc[0:1, 1536:1536 + w], rowsc[0:1, 1536:1536 + w])
                e.drain()
                e.tensor_tensor(rowsc[0:1, 1536:1536 + w], rowsc[0:1, 1536:1536 + w],
                                rowsc[0:1, 1024:1024 + w], ALU.mult)
                e.drain()
                e.tensor_tensor(scr[0:1, 0:w], rowsc[0:1, 0:w], rowsc[0:1, 1536:1536 + w],
                                ALU.mult)
                e.drain()
                e.tensor_tensor(rowsc[0:1, 0:w], rowsc[0:1, 0:w], rowsc[0:1, 1024:1024 + w],
                                ALU.mult)
                e.drain().then_inc(arow, 1)
                e.wait_ge(abc, ci + 1)
                for f in range(FCH):
                    e.tensor_tensor(sqh[:, f, 0:w], R2c4[:, f, 0:w], b_wq[:, 0:w], ALU.mult)
                e.drain()
                for f in range(FCH):
                    e.tensor_tensor(sqh[:, f, 0:w], sqh[:, f, 0:w], b_mwq[:, 0:w],
                                    ALU.subtract)
                e.drain().then_inc(at1, 1)

        def sync(e):
            for ci, (o, w) in enumerate(plan):
                if ci >= 1:
                    e.wait_ge(at1, ci)
                    e.wait_ge(asq, ci)
                    e.wait_ge(al, 16 * ci)
                e.dma_start(out=R2c4[:, :, 0:w],
                            in_=H["R2"][:, o:o + w].rearrange("(k p) t -> p k t", p=128)
                            ).then_inc(al, 16)
                if ci < NC - 1:
                    e.wait_ge(arow, ci + 1)
                    if ci >= 1:
                        e.wait_ge(arw, 32 * ci)
                    e.dma_start(out=H["rows2"][0:1, o:o + w],
                                in_=rowsc[0:1, 1024:1024 + w]).then_inc(arw, 16)
                    e.dma_start(out=H["rows2"][0:1, c.T + o:c.T + o + w],
                                in_=rowsc[0:1, 0:w]).then_inc(arw, 16)

        self.phase(gpsimd=gpsimd, tensor=tensor, scalar=scalar, vector=vector, sync=sync)

    def ph_attn_conv(self):
        c, t = self.c, self.t
        FCH, HQ = c.FCH, c.HQ
        x1T = self.t["AT_x1T"]
        ubf = self.t["AT_ubf"]
        convsb = self.t["AT_convsb"]
        bmsb = self.t["AT_bmsb"]
        self.arena_reset(self.AT_MARK5)
        self.arena_grp = "AT"
        tmpA = self.at("tmpA", [128, HQ], F32)
        tmpB = self.at("tmpB", [128, HQ], F32)
        tmps = [tmpA, tmpB]
        axe = self.sem("a4_xe")
        NXE = c.FCH * len(c.chunk_plan)
        acv = self.sem("a5_cv")
        acs = self.sem("a5_cs")
        au = self.sem("a5_u")
        auv = self.sem("a5_uv")
        NU = c.T // 512

        def vector(e):
            e.wait_ge(axe, NXE)
            for f in range(FCH):
                for q in range(4):
                    g = f * 4 + q
                    if g >= 2:
                        e.wait_ge(acs, g - 1)
                    tm = tmps[q % 2]
                    ho = q * HQ
                    e.tensor_scalar(tm[:], x1T[:, f, ho:ho + HQ],
                                    convsb[:, 0 * FCH + f:0 * FCH + f + 1], None,
                                    op0=ALU.mult)
                    e.drain()
                    e.scalar_tensor_tensor(tm[:], x1T[:, f, ho + 1:ho + 1 + HQ],
                                           convsb[:, 1 * FCH + f:1 * FCH + f + 1], tm[:],
                                           op0=ALU.mult, op1=ALU.add)
                    e.drain()
                    e.scalar_tensor_tensor(tm[:], x1T[:, f, ho + 2:ho + 2 + HQ],
                                           convsb[:, 2 * FCH + f:2 * FCH + f + 1], tm[:],
                                           op0=ALU.mult, op1=ALU.add)
                    e.drain().then_inc(acv, 1)
            for j in range(NU):
                e.wait_ge(au, j + 1)
                e.tensor_copy(ubf[0:64, j * 512:(j + 1) * 512]
                              .rearrange("s (r j) -> s j r", j=c.CPB),
                              self.banks[4 + j % 2][0:64, 0:512]
                              .rearrange("s (j r) -> s j r", r=c.L))
                e.drain().then_inc(auv, 1)

        def scalar(e):
            for f in range(FCH):
                for q in range(4):
                    g = f * 4 + q
                    e.wait_ge(acv, min(g + 2, f * 4 + 4))
                    ho = q * HQ
                    e.activation(x1T[:, f, 1 + ho:1 + ho + HQ], tmps[q % 2][:], AF.Silu)
                    e.drain().then_inc(acs, 1)

        def tensor(e):
            e.wait_ge(acs, 4 * FCH)
            for j in range(NU):
                if j >= 2:
                    e.wait_ge(auv, j - 1)
                for k in range(FCH):
                    mm = e.matmul(self.banks[4 + j % 2][0:64, 0:512],
                                  bmsb[:, k, 0:c.ST],
                                  x1T[:, k, 1 + j * 512:1 + j * 512 + 512],
                                  start=(k == 0), stop=(k == FCH - 1))
                mm.then_inc(au, 1)

        self.phase(vector=vector, scalar=scalar, tensor=tensor)

    def ph_attn_ssm(self):
        c, t, H = self.c, self.t, self.H
        L, nch, NBK, CPB = c.L, c.nch, c.NBK, c.CPB
        ubf = t["AT_ubf"]
        Btab, ALp = t["AT_Btab"], t["AT_ALp"]
        Fb, SarrB = t["AT_Fb"], t["AT_SarrB"]
        ag2row, ag2sb = t["AT_ag2row"], t["AT_ag2sb"]
        sinrow, sincolB, colbuf = t["AT_sinrow"], t["AT_sincolB"], t["AT_colbuf"]
        sibselsb = t["AT_sibselsb"]
        hsb = t["hsbT"]
        auv = self.sem("a5_uv")
        NU = c.T // 512

        stap = self.sem("a6_tap")
        sfb = self.sem("a6_fb")
        sl2 = self.sem("a6_l2")
        sso = self.sem("a6_so")
        sagw = self.sem("a6_agw")
        sag = self.sem("a6_ag", sw=True)
        sagr = self.sem("a6_agr", sw=True)
        ssr = self.sem("a6_sr")
        ssre = self.sem("a6_sre")
        ssin = self.sem("a6_sin")
        ssar = self.sem("a6_sar")
        scor = self.sem("a6_cor")
        shs = self.sem("a6_hs")
        fin = self.sem("a6_fin")
        szb = self.sem("a6_zb")
        zb6 = t["AT_zb6"]
        SarrP = self.banks[0][64:128, 0:nch + 1]
        SK = dict(skip_group_check=True)

        def tensor(e):
            e.wait_ge(auv, NU)
            e.wait_ge(szb, 1)
            e.matmul(SarrP[:], ubf[0:64, 0:64], zb6[:, 0:nch + 1],
                     start=True, stop=True, **SK)
            e.matmul(self.banks[1][64:65, 0:64], ubf[0:64, 0:1], zb6[:, 0:64],
                     start=True, stop=True, **SK)
            e.matmul(self.banks[2][64:65, 0:128], ubf[0:64, 0:1], zb6[:, 0:128],
                     start=True, stop=True, **SK)
            e.matmul(self.banks[3][64:128, 0:1], ubf[0:64, 0:64], zb6[:, 0:1],
                     start=True, stop=True, **SK)
            for b in range(NBK):
                for k in range(L):
                    mm = e.matmul(self.banks[b][0:64, k * CPB:512],
                                  Btab[:, k * 64:(k + 1) * 64],
                                  ubf[0:64, b * 512:b * 512 + (L - k) * CPB],
                                  start=(k == 0), stop=False, **SK)
                mm.then_inc(stap, 1)
            e.wait_ge(sfb, NBK)
            for k2 in range(nch):
                mm = e.matmul(SarrP[:, k2 + 1:nch + 1], ALp[:, k2 * 64:(k2 + 1) * 64],
                              Fb[:, 0:nch - k2], start=False, stop=False, **SK)
            mm.then_inc(sl2, 1)
            e.wait_ge(sso, 1)
            e.matmul(self.banks[1][64:65, 0:64], colbuf[:, 0:1], t["ident_f"][0:64, 0:64],
                     start=False, stop=False, **SK).then_inc(ssr, 1)
            e.wait_ge(sagr, 16)
            e.matmul(self.banks[2][64:65, 0:128], sibselsb[0:c.n_cores, 0:1],
                     ag2sb[0:c.n_cores, :],
                     start=False, stop=False, **SK).then_inc(ssr, 1)
            e.wait_ge(ssre, 1)
            e.matmul(self.banks[3][64:128, 0:1], sinrow[0:1, 0:64], t["one1_f"][:],
                     start=False, stop=False, **SK).then_inc(ssr, 1)
            e.wait_ge(ssre, 2)
            for j in range(nch):
                e.matmul(SarrP[:, j:j + 1], ALp[:, j * 64:(j + 1) * 64], sincolB[:, 0:1],
                         start=False, stop=False, **SK)
            e.drain().then_inc(ssin, 1)
            e.wait_ge(ssar, 1)
            for b in range(NBK):
                for k in range(1, L + 1):
                    mm = e.matmul(self.banks[b][0:64, (k - 1) * CPB:k * CPB],
                                  Btab[:, k * 64:(k + 1) * 64],
                                  SarrB[:, b * CPB:(b + 1) * CPB],
                                  start=False, stop=False, **SK)
                mm.then_inc(scor, 1)

        def vector(e):
            e.memset(zb6[:], 0.0)
            e.drain().then_inc(szb, 1)
            for b in range(NBK):
                e.wait_ge(stap, b + 1)
                e.tensor_copy(Fb[:, b * CPB:(b + 1) * CPB],
                              self.banks[b][0:64, (L - 1) * CPB:512])
                e.drain().then_inc(sfb, 1)
            e.memset(ag2row[:], 0.0)
            e.wait_ge(sl2, 1)
            e.tensor_copy(colbuf[:, 0:1], SarrP[:, nch:nch + 1])
            e.drain().then_inc(sso, 1)
            e.wait_ge(ssr, 1)
            e.tensor_copy(ag2row[0:1, 0:64], self.banks[1][64:65, 0:64])
            e.drain().then_inc(sso, 1)
            e.wait_ge(ssr, 2)
            e.tensor_copy(sinrow[:], self.banks[2][64:65, 0:128])
            e.drain().then_inc(ssre, 1)
            e.wait_ge(ssr, 3)
            e.tensor_copy(sincolB[:, 0:1], self.banks[3][64:128, 0:1])
            e.drain().then_inc(ssre, 1)
            e.wait_ge(ssin, 1)
            e.tensor_copy(SarrB[:, 0:nch], SarrP[:, 0:nch])
            e.drain().then_inc(ssar, 1)
            for b in range(NBK):
                e.wait_ge(scor, b + 1)
                e.tensor_copy(
                    hsb[0:64, b * 512:(b + 1) * 512].rearrange("s (j r) -> s j r", r=L),
                    self.banks[b][0:64, 0:512].rearrange("s (r j) -> s j r", j=CPB))
                e.drain().then_inc(shs, 1)
            e.drain().then_inc(fin, 1)

        def sync(e):
            e.wait_ge(sso, 2)
            e.dma_start(out=H["ag2_in"][:], in_=ag2row[:]).then_inc(sagw, 16)

        def gpsimd(e):
            e.wait_ge(sagw, 16)
            e.collective_compute("AllGather", ALU.bypass,
                                 replica_groups=[list(range(c.n_cores))],
                                 ins=[H["ag2_in"][:]], outs=[H["ag2_out"][:]]
                                 ).then_inc(sag, 1)
            e.wait_ge(sag, 1)
            e.dma_start(out=ag2sb[0:c.n_cores, :], in_=H["ag2_out"][0:c.n_cores, :]
                        ).then_inc(sagr, 16)

        self.phase(tensor=tensor, vector=vector, sync=sync, gpsimd=gpsimd)
        return dict(fin=fin)

    def ph_attn_tail(self):
        c, t, H = self.c, self.t, self.H
        FCH, FCH_FF, CHW = c.FCH, c.FCH_FF, c.CHW
        NCH7 = c.NCH7
        hsb = t["hsbT"]
        a6fin = self.sem("a6_fin")

        w7 = self.sem("a7_w", sw=True)
        zv = self.sem("a7_zv")
        l7r = self.sem("a7_lr")
        l7h = self.sem("a7_lh")
        l7w = self.sem("a7_lw")
        t7bc = self.sem("a7_bc")
        t7v = self.sem("a7_tv")
        s7v = self.sem("a7_sv")
        t7y = self.sem("a7_ty")
        v7yv = self.sem("a7_yv")
        t7s = self.sem("a7_ts")
        v7sa = self.sem("a7_sa")
        v7s = self.sem("a7_vs")
        v7sb = self.sem("a7_vsb")
        s7q = self.sem("a7_sq")
        t7st = self.sem("a7_st")
        v7var = self.sem("a7_var")
        s7r = self.sem("a7_sr")
        v7r3 = self.sem("a7_r3")
        t7b3 = self.sem("a7_b3")
        v7u2 = self.sem("a7_u2")
        s7sn = self.sem("a7_sn")
        t7f1 = self.sem("a7_f1")
        s7g = self.sem("a7_g")
        t7f2 = self.sem("a7_f2")
        v7o = self.sem("a7_vo")
        v7x3 = self.sem("a7_x3")
        x3w = self.sem("x3w")

        self.arena_group("A7")
        v1sb = self.at("v1sb", [128, FCH, c.D], BF16)
        w2sb = self.at("w2sb", [128, FCH, c.D], BF16)
        cmsb = self.at("cmsb", [64, c.D], BF16)
        ff1sb = self.at("ff1sb", [128, FCH, c.FF], BF16)
        ff2sb = self.at("ff2sb", [128, FCH_FF, c.D], BF16)
        R2c7 = self.at("R2c7", [128, FCH, CHW], F32)
        s7 = self.at("s7", [128, FCH, CHW], F32)
        FB = max(FCH_FF, 2 * FCH)
        fbuf = self.at("fbuf", [128, FB * CHW], BF16)
        yvb = self.at("yvb", [128, FCH, CHW], BF16)
        rowbuf7 = self.at("rowbuf7", [1, 2 * CHW], F32)
        rowsc7 = self.at("rowsc7", [1, 4 * CHW], F32)
        zbuf = self.at("zbuf", [128, 128], F32)
        v1bsb = self.at("v1bsb", [128, FCH], F32)
        w2bsb = self.at("w2bsb", [128, FCH], F32)
        ff1bsb = self.at("ff1bsb", [128, FCH_FF], F32)
        ff2bsb = self.at("ff2bsb", [128, FCH], F32)
        l2g = self.at("l2g", [128, FCH], F32)
        l2b = self.at("l2b", [128, FCH], F32)
        hhc = fbuf[:, 0:FCH * CHW].rearrange("p (k t) -> p k t", k=FCH)
        vb = fbuf[:, FCH * CHW:2 * FCH * CHW].rearrange("p (k t) -> p k t", k=FCH)
        ssqv = fbuf[:, 0:FCH * CHW].rearrange("p (k t) -> p k t", k=FCH)
        b6, b7 = self.banks[6], self.banks[7]
        NW7 = 176

        def gpsimd(e):
            e.wait_ge(a6fin, 1)
            loads = [
                (v1sb[:], H["v1T"][:].rearrange("(k p) o -> p k o", p=128)),
                (w2sb[:], H["w2T"][:].rearrange("(k p) o -> p k o", p=128)),
                (cmsb[:], H["CmT"][:]),
                (ff1sb[:], H["ff1T"][:].rearrange("(k p) o -> p k o", p=128)),
                (ff2sb[:], H["ff2T"][:].rearrange("(k p) o -> p k o", p=128)),
                (v1bsb[:], H["v1b"][:]), (w2bsb[:], H["w2b"][:]),
                (ff1bsb[:], H["ff1b"][:]), (ff2bsb[:], H["ff2b"][:]),
                (l2g[:], H["ln2g"][:]), (l2b[:], H["ln2b"][:]),
            ]
            for o_, i_ in loads:
                e.dma_start(out=o_, in_=i_).then_inc(w7, 16)

        def sync(e):
            e.wait_ge(a6fin, 1)
            for ci in range(NCH7):
                o = ci * CHW
                if ci >= 1:
                    e.wait_ge(v7x3, ci)
                    e.wait_ge(l7r, 16 * ci)
                e.dma_start(out=R2c7[:],
                            in_=H["R2"][:, o:o + CHW].rearrange("(k p) t -> p k t", p=128)
                            ).then_inc(l7r, 16)
                if ci >= 1:
                    e.wait_ge(t7f2, ci * FCH)
                    e.wait_ge(l7h, 16 * ci)
                e.dma_start(out=hhc,
                            in_=H["hh"][:, o:o + CHW].rearrange("(k p) t -> p k t", p=128)
                            ).then_inc(l7h, 16)
                if ci >= 1:
                    e.wait_ge(t7bc, ci)
                    e.wait_ge(l7w, 16 * ci)
                e.dma_start(out=rowbuf7[0:1, :].rearrange("o (a t) -> o a t", a=2),
                            in_=H["rows2"][0:1, :].rearrange("o (a tt) -> o a tt", a=2)
                            [:, :, o:o + CHW]).then_inc(l7w, 16)
            e.wait_ge(zv, 1)
            e.wait_ge(v7x3, NCH7)
            for f in range(FCH):
                e.dma_start(out=H["x3"][f * 128:(f + 1) * 128, c.T:c.TW],
                            in_=zbuf[:]).then_inc(x3w, 16)

        def tensor(e):
            e.wait_ge(a6fin, 1)
            e.wait_ge(w7, NW7)
            for ci in range(NCH7):
                o = ci * CHW
                e.wait_ge(l7w, 16 * (ci + 1))
                if ci >= 1:
                    e.wait_ge(v7u2, ci)
                e.matmul(b6[:, 0:CHW], t["ones_row_f"][0:1, :], rowbuf7[0:1, 0:CHW],
                         start=True, stop=True)
                e.matmul(b6[:, CHW:2 * CHW], t["ones_row_f"][0:1, :],
                         rowbuf7[0:1, CHW:2 * CHW], start=True, stop=True).then_inc(t7bc, 1)
                e.wait_ge(l7h, 16 * (ci + 1))
                if ci >= 1:
                    e.wait_ge(s7g, ci * FCH_FF)
                for m in range(FCH):
                    if ci * FCH + m >= 2:
                        e.wait_ge(s7v, ci * FCH + m - 1)
                    for k in range(FCH):
                        mm = e.matmul(self.banks[m % 2][:, 0:CHW],
                                      v1sb[:, k, m * 128:(m + 1) * 128], hhc[:, k, :],
                                      start=(k == 0), stop=(k == FCH - 1))
                    mm.then_inc(t7v, 1)
                for m in range(FCH):
                    if ci * FCH + m >= 2:
                        e.wait_ge(v7yv, ci * FCH + m - 1)
                    e.matmul(self.banks[2 + m % 2][:, 0:CHW],
                             cmsb[0:64, m * 128:(m + 1) * 128], hsb[0:64, o:o + CHW],
                             start=True, stop=True).then_inc(t7y, 1)
                e.wait_ge(v7yv, (ci + 1) * FCH)
                if ci >= 1:
                    e.wait_ge(v7o, ci * FCH)
                for m in range(FCH):
                    if ci * FCH + m >= 2:
                        e.wait_ge(v7sa, ci * FCH + m - 1)
                    for k in range(FCH):
                        mm = e.matmul(self.banks[4 + m % 2][:, 0:CHW],
                                      w2sb[:, k, m * 128:(m + 1) * 128], yvb[:, k, :],
                                      start=(k == 0), stop=(k == FCH - 1))
                    mm.then_inc(t7s, 1)
                e.wait_ge(v7sb, ci + 1)
                e.wait_ge(s7q, ci + 1)
                for k in range(FCH):
                    e.matmul(b7[0:1, 0:CHW], t["ones_col_bf"][:], yvb[:, k, :],
                             start=(k == 0), stop=(k == FCH - 1))
                for k in range(FCH):
                    mm = e.matmul(b7[32:33, 0:CHW], t["ones_col_bf"][:], ssqv[:, k, :],
                                  start=(k == 0), stop=(k == FCH - 1))
                mm.then_inc(t7st, 1)
                e.wait_ge(v7r3, ci + 1)
                e.matmul(b6[:, 0:CHW], t["ones_row_f"][0:1, :], rowsc7[0:1, 0:CHW],
                         start=True, stop=True)
                e.matmul(b6[:, CHW:2 * CHW], t["ones_row_f"][0:1, :],
                         rowsc7[0:1, 2 * CHW:3 * CHW], start=True, stop=True).then_inc(t7b3, 1)
                e.wait_ge(s7sn, ci + 1)
                for mf in range(FCH_FF):
                    if ci * FCH_FF + mf >= 4:
                        e.wait_ge(s7g, ci * FCH_FF + mf - 3)
                    for k in range(FCH):
                        mm = e.matmul(self.banks[mf % 4][:, 0:CHW],
                                      ff1sb[:, k, mf * 128:(mf + 1) * 128], yvb[:, k, :],
                                      start=(k == 0), stop=(k == FCH - 1))
                    mm.then_inc(t7f1, 1)
                e.wait_ge(s7g, (ci + 1) * FCH_FF)
                for m in range(FCH):
                    if ci * FCH + m >= 2:
                        e.wait_ge(v7o, ci * FCH + m - 1)
                    for k in range(FCH_FF):
                        mm = e.matmul(self.banks[4 + m % 2][:, 0:CHW],
                                      ff2sb[:, k, m * 128:(m + 1) * 128],
                                      fbuf[:, k * CHW:(k + 1) * CHW],
                                      start=(k == 0), stop=(k == FCH_FF - 1))
                    mm.then_inc(t7f2, 1)

        def scalar(e):
            e.wait_ge(a6fin, 1)
            e.wait_ge(w7, NW7)
            for ci in range(NCH7):
                o = ci * CHW
                if ci >= 1:
                    e.wait_ge(t7f2, ci * FCH)
                for m in range(FCH):
                    e.wait_ge(t7v, ci * FCH + m + 1)
                    e.activation(vb[:, m, :], self.banks[m % 2][:, 0:CHW], AF.Silu,
                                 bias=v1bsb[:, m:m + 1])
                    e.drain().then_inc(s7v, 1)
                e.wait_ge(v7s, ci + 1)
                e.activation(ssqv[:, :, :], s7[:, :, :], AF.Square)
                e.drain().then_inc(s7q, 1)
                e.wait_ge(v7var, ci + 1)
                e.activation(rowsc7[0:1, 2 * CHW:3 * CHW], rowsc7[0:1, CHW:2 * CHW],
                             AF.Sqrt)
                e.drain().then_inc(s7r, 1)
                e.wait_ge(v7u2, ci + 1)
                for k in range(FCH):
                    e.activation(yvb[:, k, :], yvb[:, k, :], AF.Identity,
                                 scale=l2g[:, k:k + 1], bias=l2b[:, k:k + 1])
                e.drain().then_inc(s7sn, 1)
                for mf in range(FCH_FF):
                    e.wait_ge(t7f1, ci * FCH_FF + mf + 1)
                    e.activation(fbuf[:, mf * CHW:(mf + 1) * CHW],
                                 self.banks[mf % 4][:, 0:CHW], AF.Gelu,
                                 bias=ff1bsb[:, mf:mf + 1])
                    e.drain().then_inc(s7g, 1)
                e.wait_ge(v7x3, ci + 1)
                e.dma_start(out=H["x3"][:, o:o + CHW].rearrange("(k p) t -> p k t", p=128),
                            in_=s7[:]).then_inc(x3w, 16)

        def vector(e):
            e.wait_ge(a6fin, 1)
            e.memset(zbuf[:], 0.0)
            e.drain().then_inc(zv, 1)
            for ci in range(NCH7):
                for m in range(FCH):
                    e.wait_ge(t7y, ci * FCH + m + 1)
                    e.wait_ge(s7v, ci * FCH + m + 1)
                    e.tensor_tensor(yvb[:, m, :], vb[:, m, :],
                                    self.banks[2 + m % 2][:, 0:CHW], ALU.mult)
                    e.drain().then_inc(v7yv, 1)
                e.wait_ge(t7bc, ci + 1)
                e.wait_ge(l7r, 16 * (ci + 1))
                if ci >= 1:
                    e.wait_ge(x3w, 16 * ci)
                for k in range(FCH):
                    e.tensor_tensor(s7[:, k, :], R2c7[:, k, :], b6[:, 0:CHW], ALU.mult)
                e.drain()
                for k in range(FCH):
                    e.tensor_tensor(s7[:, k, :], s7[:, k, :], b6[:, CHW:2 * CHW],
                                    ALU.subtract)
                e.drain()
                for m in range(FCH):
                    e.wait_ge(t7s, ci * FCH + m + 1)
                    e.tensor_tensor(s7[:, m, :], s7[:, m, :],
                                    self.banks[4 + m % 2][:, 0:CHW], ALU.add)
                    e.drain()
                    e.tensor_scalar(s7[:, m, :], s7[:, m, :], w2bsb[:, m:m + 1], None,
                                    op0=ALU.add)
                    e.drain().then_inc(v7sa, 1)
                e.sem_inc(v7s, 1)
                e.tensor_copy(yvb[:, :, :], s7[:, :, :])
                e.drain().then_inc(v7sb, 1)
                e.wait_ge(t7st, ci + 1)
                e.tensor_scalar(rowsc7[0:1, 0:CHW], b7[0:1, 0:CHW], 1.0 / c.D, None,
                                op0=ALU.mult)
                e.tensor_scalar(rowsc7[0:1, 3 * CHW:4 * CHW], b7[32:33, 0:CHW], 1.0 / c.D,
                                None, op0=ALU.mult)
                e.drain()
                e.tensor_tensor(rowsc7[0:1, CHW:2 * CHW], rowsc7[0:1, 0:CHW],
                                rowsc7[0:1, 0:CHW], ALU.mult)
                e.drain()
                e.tensor_tensor(rowsc7[0:1, CHW:2 * CHW], rowsc7[0:1, 3 * CHW:4 * CHW],
                                rowsc7[0:1, CHW:2 * CHW], ALU.subtract)
                e.drain()
                e.tensor_scalar(rowsc7[0:1, CHW:2 * CHW], rowsc7[0:1, CHW:2 * CHW], c.eps,
                                None, op0=ALU.add)
                e.drain().then_inc(v7var, 1)
                e.wait_ge(s7r, ci + 1)
                e.reciprocal(rowsc7[0:1, 2 * CHW:3 * CHW], rowsc7[0:1, 2 * CHW:3 * CHW])
                e.drain().then_inc(v7r3, 1)
                e.wait_ge(t7b3, ci + 1)
                for k in range(FCH):
                    e.tensor_tensor(yvb[:, k, :], s7[:, k, :], b6[:, 0:CHW], ALU.subtract)
                e.drain()
                for k in range(FCH):
                    e.tensor_tensor(yvb[:, k, :], yvb[:, k, :], b6[:, CHW:2 * CHW],
                                    ALU.mult)
                e.drain().then_inc(v7u2, 1)
                for m in range(FCH):
                    e.wait_ge(t7f2, ci * FCH + m + 1)
                    e.tensor_tensor(s7[:, m, :], s7[:, m, :],
                                    self.banks[4 + m % 2][:, 0:CHW], ALU.add)
                    e.drain()
                    e.tensor_scalar(s7[:, m, :], s7[:, m, :], ff2bsb[:, m:m + 1], None,
                                    op0=ALU.add)
                    e.drain().then_inc(v7o, 1)
                for k in range(FCH):
                    e.tensor_tensor(s7[:, k, :], s7[:, k, :], R2c7[:, k, :], ALU.add)
                e.drain().then_inc(v7x3, 1)

        self.phase(gpsimd=gpsimd, sync=sync, tensor=tensor, scalar=scalar, vector=vector)
        return dict(x3w=x3w)

    def ph_debug_dump(self, names_2d):
        c, t = self.c, self.t
        dbg = self.sem("dbg")
        outs = []
        for sn, on in names_2d:
            shape = list(t[sn].shape) if hasattr(t[sn], 'shape') else None
            outs.append((sn, self.dram(on, list(t[sn].shape), t[sn].dtype, "ExternalOutput")))

        def gpsimd(e):
            for ph in []:
                pass
            for i, (sn, hap) in enumerate(outs):
                e.dma_start(out=hap[:], in_=t[sn][:]).then_inc(dbg, 16)
            e.wait_ge(dbg, 16 * len(outs))

        self.phase(gpsimd=gpsimd)


def kernel(**inputs):
    cfg = KCfg()
    nc, meta = build_full(cfg)
    maps = stage_inputs(cfg, inputs)
    res = run_bass_kernel_spmd(nc, maps, core_ids=list(range(cfg.n_cores)))
    outs = [r["out"] for r in res.results]
    full = np.concatenate([o.T for o in outs], axis=0)
    return np.ascontiguousarray(full.reshape(cfg.B, cfg.S, cfg.D))


def build_full(cfg):
    kb = KB(cfg)
    kb.declare()
    kb.ph_consts()
    kb.ph_router(1)
    kb.ph_postag(1)
    kb.ph_moe_ffn(1)
    kb.ph_combine(1)
    kb.new_generation()
    kb.ph_attn_front()
    kb.ph_attn_conv()
    kb.ph_attn_ssm()
    kb.ph_attn_tail()
    kb.new_generation()
    kb.ph_router(3)
    kb.ph_postag(3)
    kb.ph_moe_ffn(3)
    kb.ph_combine(3)
    nc = kb.finish()
    return nc, kb


# version 03 (final)
# speedup vs baseline: 1.0398x; 1.0398x over previous
import numpy as np
from dataclasses import dataclass
from contextlib import ExitStack

import concourse.bass as bass
import concourse.bacc as bacc
import concourse.mybir as mybir
import concourse.library_config as LC
from concourse.bass_core_rust import IndirectOffsetOnAxis
from concourse.bass_utils import run_bass_kernel_spmd
from concourse.expressions import smax

F32 = mybir.dt.float32
BF16 = mybir.dt.bfloat16
I32 = mybir.dt.int32
I16 = mybir.dt.int16
AF = mybir.ActivationFunctionType
ALU = mybir.AluOpType
ts = bass.ts
ds = bass.ds
X = mybir.AxisListType.X


@dataclass
class KCfg:
    B: int = 4
    S: int = 8192
    D: int = 1024
    E: int = 8
    F: int = 4096
    FF: int = 2048
    ST: int = 64
    n_cores: int = 8
    L: int = 64
    SPAN: int = 256
    CH: int = 512
    eps: float = 1e-5
    debug: bool = False

    @property
    def T(self): return self.B * self.S // self.n_cores
    @property
    def TW(self): return self.T + 128
    @property
    def cap(self): return int(1.25 * self.B * self.S / self.E)
    @property
    def FCH(self): return self.D // 128
    @property
    def FCH_F(self): return self.F // 128
    @property
    def FCH_FF(self): return self.FF // 128
    @property
    def halves(self): return self.n_cores // self.B
    @property
    def NTILE(self): return self.TW // 128
    @property
    def NROW(self): return self.TW
    @property
    def NSLOT(self):
        base = self.TW + self.E * 128 + 256
        return ((base + 255) // 256) * 256
    @property
    def MAXSPANS(self):
        return self.TW // 256 + self.E + 1
    @property
    def PARK(self): return self.NSLOT - 128
    @property
    def WVMIN(self):
        return 128 * max(self.E, (self.TW - 128 + 127) // 128)
    @property
    def NZCH(self):
        return (self.NSLOT - self.WVMIN + 511) // 512
    @property
    def EOUTW(self): return self.NSLOT + self.NZCH * 512
    @property
    def chunk_plan(self):
        return [(o, self.CH) for o in range(0, self.T, self.CH)] + [(self.T, 128)]
    @property
    def NCHUNKS(self): return self.T // self.L
    CHW: int = 256
    @property
    def NCH7(self): return self.T // self.CHW
    @property
    def nch(self): return self.T // self.L
    @property
    def NBK(self): return self.T // 512
    @property
    def CPB(self): return 512 // self.L
    @property
    def HQ(self): return self.T // 4


def col128(v):
    K = v.shape[0]
    return np.ascontiguousarray(v.reshape(K // 128, 128).T)


def stage_inputs(cfg, inputs):
    c = cfg
    x = np.ascontiguousarray(np.asarray(inputs["L0_attn_x"], np.float32).reshape(-1, c.D))
    T = c.T
    w = {k: np.ascontiguousarray(np.asarray(v, np.float32)) for k, v in inputs.items()
         if k != "L0_attn_x"}
    convw = w["L0_attn_conv_w"][:, 0, :]
    com = dict(
        w1T=np.ascontiguousarray(w["L0_attn_w1_w"].T), v1T=np.ascontiguousarray(w["L0_attn_v1_w"].T),
        w2T=np.ascontiguousarray(w["L0_attn_w2_w"].T),
        ff1T=np.ascontiguousarray(w["L0_attn_ff1_w"].T), ff2T=np.ascontiguousarray(w["L0_attn_ff2_w"].T),
        BmT=np.ascontiguousarray(w["L0_attn_Bm"].T), CmT=np.ascontiguousarray(w["L0_attn_Cm"].T),
        Amat=w["L0_attn_A"], Wr=w["L0_ffn_Wr"],
        W1=np.ascontiguousarray(w["L0_ffn_W1"].reshape(c.E * c.D, c.F)),
        W2=np.ascontiguousarray(w["L0_ffn_W2"].reshape(c.E * c.F, c.D)),
        convw=np.ascontiguousarray(np.stack([col128(convw[:, i]) for i in range(3)], 1)
                                   .reshape(128, 3 * c.FCH)),
        w1b=col128(w["L0_attn_w1_b"]), v1b=col128(w["L0_attn_v1_b"]), w2b=col128(w["L0_attn_w2_b"]),
        ff1b=col128(w["L0_attn_ff1_b"]), ff2b=col128(w["L0_attn_ff2_b"]),
        ln1g=col128(w["L0_attn_ln1_g"]), ln1b=col128(w["L0_attn_ln1_b"]),
        ln2g=col128(w["L0_attn_ln2_g"]), ln2b=col128(w["L0_attn_ln2_b"]),
    )
    maps = []
    for core in range(c.n_cores):
        sh = x[core * T:(core + 1) * T]
        hL = np.zeros((1, c.D), np.float32) if core % c.halves == 0 else x[core * T - 1:core * T]
        hR = (np.zeros((1, c.D), np.float32) if core % c.halves == c.halves - 1
              else x[(core + 1) * T:(core + 1) * T + 1])
        pad = np.zeros((126, c.D), np.float32)
        xs = np.ascontiguousarray(np.concatenate([sh, hL, hR, pad], 0).T)
        maskB = np.zeros((8, 1), np.float32); maskB[:core, 0] = 1.0
        maskR = np.zeros((8, 1), np.float32); maskR[:core + 1, 0] = 1.0
        lv = (core % c.halves) != 0
        rv = (core % c.halves) != c.halves - 1
        w1bcol = col128(w["L0_attn_w1_b"])
        zcol = np.zeros_like(w1bcol)
        w1bLR = np.ascontiguousarray(np.concatenate(
            [w1bcol if lv else zcol, w1bcol if rv else zcol], 1))
        convmask = np.zeros((128, 2), np.float32)
        convmask[:, 0] = 1.0 if lv else 0.0
        convmask[:, 1] = 1.0 if rv else 0.0
        sibsel = np.zeros((8, 1), np.float32)
        if core % c.halves == 1:
            sibsel[core - 1, 0] = 1.0
        m = dict(com)
        m.update(xs=xs, maskB=maskB,
                 maskLR=np.ascontiguousarray(np.concatenate([maskB, maskR], 1)),
                 w1bLR=w1bLR, convmask=convmask, sibsel=sibsel)
        maps.append(m)
    return maps


class KB:
    PERSISTENT = {"cst", "x3w", "bar", "dbg"}

    def __init__(self, cfg: KCfg):
        self.c = cfg
        self.nc = bacc.Bacc("TRN2", target_bir_lowering=False)
        self.stack = ExitStack()
        self.t = {}
        self.sems = {}
        self.H = {}
        self.phases = []
        self._free_hw = []
        self._free_sw = []
        self._sem_class = {}
        self._nalloc = 0
        self.bar_total = 0

    def sem(self, name, sw=False):
        if name in self.sems:
            return self.sems[name]
        cls = "sw" if sw else "hw"
        if name in self.PERSISTENT:
            h = self.stack.enter_context(self.nc.semaphore(name))
        else:
            pool = self._free_sw if sw else self._free_hw
            if pool:
                h = pool.pop()
            else:
                h = self.stack.enter_context(self.nc.semaphore(f"{cls}{self._nalloc}"))
                self._nalloc += 1
        self.sems[name] = h
        self._sem_class[name] = cls
        return h

    def new_generation(self):
        clears = []
        for n in list(self.sems):
            if n in self.PERSISTENT:
                continue
            h = self.sems.pop(n)
            clears.append(h)
            (self._free_sw if self._sem_class[n] == "sw" else self._free_hw).append(h)
        self.phases.append(("GEN", clears))

    def sb(self, name, shape, dtype):
        if name not in self.t:
            self.t[name] = self.stack.enter_context(self.nc.sbuf_tensor(name, shape, dtype))
        return self.t[name]

    def dram(self, name, shape, dtype, kind="Internal"):
        self.H[name] = self.nc.dram_tensor(name, shape, dtype, kind=kind).ap()
        return self.H[name]

    def phase(self, **eng_fns):
        self.phases.append(eng_fns)

    def arena_group(self, name):
        self.arena_off = 0
        self.arena_grp = name
        self.arena_limit = self.ARENA_BYTES - (self.TAIL if name in ("RT", "PA", "AT", "A7") else 0)

    def at(self, name, shape, dtype):
        key = f"{self.arena_grp}_{name}"
        if key in self.t:
            return self.t[key]
        import numpy as _np
        elems = int(_np.prod(shape[1:]))
        nbytes = elems * {F32: 4, BF16: 2, I32: 4, I16: 2}[dtype]
        off = (self.arena_off + 31) // 32 * 32
        assert off + nbytes <= self.arena_limit, f"arena overflow {key}: {off}+{nbytes}"
        h = self.nc.alloc_sbuf_tensor_at(key, list(shape), dtype,
                                         offset=self.arena_base + off)
        self.arena_off = off + nbytes
        self.t[key] = h
        return h

    def arena_mark(self):
        return self.arena_off

    def arena_reset(self, off):
        self.arena_off = off

    def finish(self):
        nc = self.nc
        groups, cur = [], []
        for ph in self.phases:
            if isinstance(ph, tuple) and ph[0] == "GEN":
                groups.append((cur, ph[1]))
                cur = []
            else:
                cur.append(ph)
        groups.append((cur, None))
        for gi, (phs, clears) in enumerate(groups):
            with nc.Block(f"g{gi}") as blk:
                for eng_name in ["sync", "tensor", "vector", "scalar", "gpsimd"]:
                    def runner(e, _phs=phs, _n=eng_name):
                        for ph in _phs:
                            fn = ph.get(_n)
                            if fn is not None:
                                fn(e)
                    getattr(blk, eng_name)(runner)
            if clears is not None:
                nc.all_engine_barrier()
                for h in clears:
                    nc.gpsimd.dma_reset(range(h.num, h.num + 1))
                    nc.gpsimd.sem_clear(h)
                nc.all_engine_barrier()
        nc.compile()
        nc.finalize()
        return nc

    def declare(self):
        c = self.c
        self.dram("xs", [c.D, c.TW], F32, "ExternalInput")
        self.dram("maskB", [8, 1], F32, "ExternalInput")
        self.dram("maskLR", [8, 2], F32, "ExternalInput")
        for nm, shp in [("w1T", [c.D, c.D]), ("v1T", [c.D, c.D]), ("w2T", [c.D, c.D]),
                        ("ff1T", [c.D, c.FF]), ("ff2T", [c.FF, c.D]),
                        ("BmT", [c.D, c.ST]), ("CmT", [c.ST, c.D]), ("Amat", [c.ST, c.ST]),
                        ("Wr", [c.D, c.E]), ("W1", [c.E * c.D, c.F]), ("W2", [c.E * c.F, c.D]),
                        ("convw", [128, 3 * c.FCH]),
                        ("w1b", [128, c.FCH]), ("v1b", [128, c.FCH]), ("w2b", [128, c.FCH]),
                        ("ff1b", [128, c.FCH_FF]), ("ff2b", [128, c.FCH]),
                        ("ln1g", [128, c.FCH]), ("ln1b", [128, c.FCH]),
                        ("ln2g", [128, c.FCH]), ("ln2b", [128, c.FCH]),
                        ("w1bLR", [128, 2 * c.FCH]), ("convmask", [128, 2]),
                        ("sibsel", [8, 1])]:
            self.dram(nm, shp, F32, "ExternalInput")
        self.dram("out", [c.D, c.T], F32, "ExternalOutput")
        self.dram("lnh", [c.D, c.TW], F32)
        self.dram("glh", [c.D, c.NSLOT], BF16)
        self.dram("eout", [c.D, c.EOUTW], F32)
        self.dram("R2", [c.D, c.TW], F32)
        self.dram("x3", [c.D, c.TW], F32)
        self.dram("hh", [c.D, c.T], BF16)
        self.dram("rowscr", [1, max(c.NSLOT, c.NROW)], I16)
        self.dram("ag_in", [1, 16], F32)
        self.dram("ag_out", [c.n_cores, 16], F32)
        self.dram("ag2_in", [1, 128], F32)
        self.dram("ag2_out", [c.n_cores, 128], F32)
        self.dram("ag3_in", [1, 16], F32)
        self.dram("ag3_out", [c.n_cores, 16], F32)
        self.CONST_ident = self.nc.inline_tensor(np.eye(128, dtype=np.float32), name="identc").ap()
        self.CONST_triU = self.nc.inline_tensor(
            np.triu(np.ones((128, 128), np.float32), 1), name="triUc").ap()

        self.dram("pmkrow", [1, c.NROW], F32)
        self.dram("fbrow", [1, c.NROW], F32)
        self.dram("rows2", [1, 2 * c.T], F32)
        for nm in ["eidx", "pos", "pmax", "kept", "gpos", "dsttm"]:
            self.sb("tm_" + nm, [128, c.NTILE], F32)
        self.sb("ohf", [128, c.NTILE * 8], F32)
        self.sb("ones_col_bf", [128, 1], BF16)
        self.sb("ones_col_f", [128, 1], F32)
        self.sb("one1_f", [1, 1], F32)
        self.sb("ones_row_f", [1, 128], F32)
        self.sb("onesP", [128, 128], F32)
        self.sb("onesP_bf", [128, 128], BF16)
        self.sb("epat8", [128, 8], F32)
        self.sb("epatm9", [128, 8], F32)
        self.sb("negS", [1, 8], F32)
        self.sb("negmat", [128, 8], F32)
        self.sb("ident_f", [128, 128], F32)
        self.sb("triU_bf", [128, 128], BF16)
        self.sb("wr_bf", [128, c.FCH * 8], BF16)
        self.sb("wr_f", [128, c.FCH * 8], F32)
        self.sb("maskB_sb", [8, 1], F32)
        self.sb("maskLR_sb", [8, 2], F32)
        self.sb("agrow", [1, 16], F32)
        self.sb("cnt_row", [1, 16], F32)
        self.sb("agres", [8, 16], F32)
        self.sb("seg_start", [1, 16], F32)
        self.sb("halo2", [16, 8], F32)
        self.sb("scr512", [1, 512], F32)
        self.sb("fwdi", [128, c.NSLOT // 16], I16)
        self.sb("backi", [128, c.NROW // 16], I16)
        self.sb("spantab", [1, 32], I32)
        self.banks = [self.stack.enter_context(self.nc.psum_tensor(f"bank{i}", [128, 512], F32))
                      for i in range(8)]
        free = 192 * 1024 - self.nc.sbuf_base
        self.ARENA_BYTES = (free - 512) // 32 * 32
        self.arena_base = self.nc.sbuf_base
        got = self.nc.bump_sbuf(self.ARENA_BYTES)
        assert got is not None, "arena reservation failed"
        self.TAIL = (c.NROW * 4 + 31) // 32 * 32
        rpT = self.nc.alloc_sbuf_tensor_at("rowpackT", [128, c.NROW], F32,
                                           offset=self.arena_base + self.ARENA_BYTES - self.TAIL)
        self.t["rowpackT"] = rpT
        self.t["muR"] = rpT[0:1, :]
        self.t["rR"] = rpT[32:33, :]
        self.t["hsbT"] = self.nc.alloc_sbuf_tensor_at(
            "hsbT", [64, c.T], BF16,
            offset=self.arena_base + self.ARENA_BYTES - self.TAIL)
        self.arena_off = 0
        self.arena_grp = "g"
        self.arena_limit = self.ARENA_BYTES

    def ph_consts(self):
        c, t, H = self.c, self.t, self.H
        ld = self.sem("c_ld")
        cst = self.sem("cst")
        cmm = self.sem("c_mm")

        def sync(e):
            e.dma_start(out=t["ident_f"][:], in_=self.CONST_ident[:]).then_inc(ld, 16)
            e.dma_start(out=t["maskB_sb"][:], in_=H["maskB"][:]).then_inc(ld, 16)
            e.dma_start(out=t["maskLR_sb"][:], in_=H["maskLR"][:]).then_inc(ld, 16)
            for f in range(c.FCH):
                e.dma_start(out=t["wr_f"][:, f * 8:f * 8 + c.E],
                            in_=H["Wr"][f * 128:(f + 1) * 128, :]).then_inc(ld, 16)

        ldg = self.sem("c_ldg", sw=True)

        def gpsimd(e):
            e.dma_start(out=t["triU_bf"][:], in_=self.CONST_triU[:]).then_inc(ldg, 16)
            e.drain()
            e.sem_inc(cst, 1)

        def vector(e):
            e.wait_ge(ld, 16 * (3 + c.FCH))
            e.wait_ge(ldg, 16)
            e.wait_ge(cst, 1)
            e.memset(t["ones_col_bf"][:], 1.0)
            e.memset(t["ones_col_f"][:], 1.0)
            e.memset(t["one1_f"][:], 1.0)
            e.memset(t["ones_row_f"][:], 1.0)
            e.memset(t["onesP"][:], 1.0)
            e.memset(t["onesP_bf"][:], 1.0)
            for k in range(8):
                e.memset(t["epat8"][:, k:k + 1], float(k))
                e.memset(t["epatm9"][:, k:k + 1], float(k - 9) if k < c.E else 0.0)
            e.memset(t["wr_bf"][:], 0.0)
            e.memset(t["agrow"][:], 0.0)
            e.memset(t["cnt_row"][:], 0.0)
            e.memset(t["agres"][:], 0.0)
            e.drain()
            e.tensor_copy(t["wr_bf"][:, 0:c.FCH * 8], t["wr_f"][:, 0:c.FCH * 8])
            e.drain().then_inc(cst, 1)
            e.wait_ge(cmm, 1)
            e.tensor_scalar(t["negS"][:, 0:c.E], self.banks[7][0:1, 0:c.E], -1.0, None, op0=ALU.mult)
            if c.E < 8:
                e.memset(t["negS"][:, c.E:8], 0.0)
            e.drain().then_inc(cst, 1)
            e.wait_ge(cmm, 2)
            e.tensor_copy(t["negmat"][:], self.banks[6][:, 0:8])
            e.drain().then_inc(cst, 1)

        def tensor(e):
            e.wait_ge(cst, 2)
            for f in range(c.FCH):
                mm = e.matmul(self.banks[7][0:1, 0:c.E], t["ones_col_f"][:],
                              t["wr_f"][:, f * 8:f * 8 + c.E],
                              start=(f == 0), stop=(f == c.FCH - 1))
            mm.then_inc(cmm, 1)
            e.wait_ge(cst, 3)
            e.matmul(self.banks[6][:, 0:8], t["ones_row_f"][0:1, :], t["negS"][0:1, 0:8],
                     start=True, stop=True).then_inc(cmm, 1)

        self.phase(sync=sync, gpsimd=gpsimd, vector=vector, tensor=tensor)

    def ph_router(self, call):
        c, t, H = self.c, self.t, self.H
        pre = "r_"
        src = H["xs"] if call == 1 else H["x3"]
        plan = c.chunk_plan
        tiles = []
        tiles_of = {ci: [] for ci in range(len(plan))}
        for ci, (o, w) in enumerate(plan):
            for k in range(w // 128):
                tiles.append((len(tiles), ci, k * 128))
                tiles_of[ci].append(tiles[-1])
        NTI = len(tiles)
        assert NTI == c.NTILE

        ld = self.sem(pre + "ld")
        vc = self.sem(pre + "vc")
        sq = self.sem(pre + "sq")
        smm = self.sem(pre + "mm")
        srow = self.sem(pre + "row")
        sqr = self.sem(pre + "sqr")
        sbc = self.sem(pre + "bc")
        sln = self.sem(pre + "ln")
        slnw = self.sem(pre + "lnw")
        stok = self.sem(pre + "tok")
        sdif = self.sem(pre + "dif")
        sexp = self.sem(pre + "exp")
        soh = self.sem(pre + "oh")
        spos = self.sem(pre + "pos")
        sposv = self.sem(pre + "posv")
        sfin = self.sem(pre + "fin")

        self.arena_group("RT")
        xb = self.at("xb", [128, 2, c.FCH, 512], F32)
        xbf = self.at("xbf", [128, 2, c.FCH, 512], BF16)
        xsq = self.at("xsq", [128, 2, c.FCH, 512], BF16)
        lnb = self.at("lnb", [128, 2, c.FCH, 512], F32)
        scrb_t = self.at("scr512b", [1, 512], F32)
        mxc = self.sb("rt_mxc", [128, 1], F32)
        difc = self.sb("rt_difc", [128, 8], F32)
        exc = self.sb("rt_exc", [128, 8], F32)
        sc8 = self.sb("rt_sc8", [128, 8], F32)
        smc = self.sb("rt_smc", [128, 1], F32)
        ohc = self.sb("rt_ohc", [128, 8], F32)
        ohb = self.sb("rt_ohb", [128, 8], BF16)
        rcolsb = self.sb("rt_rcol", [128, 1], F32)
        b_mu, b_sq, b_bc1, b_bc2, b_tok, b_pos, b_cnt = (self.banks[i] for i in range(7))
        NC = len(plan)

        def sync(e):
            e.wait_ge(self.sem("cst"), 3)
            if call == 3:
                e.wait_ge(self.sem("x3w"), 16 * (c.NCH7 + c.FCH))
            for ci, (o, w) in enumerate(plan):
                if ci >= 2:
                    e.wait_ge(sln, ci - 1)
                    last_tile_prev = sum(pw // 128 for (po, pw) in plan[:ci - 1])
                    e.wait_ge(stok, last_tile_prev)
                if ci >= 1:
                    e.wait_ge(ld, 16 * ci)
                e.dma_start(out=xb[:, ci % 2, :, 0:w],
                            in_=src[:, o:o + w].rearrange("(f p) t -> p f t", p=128)).then_inc(ld, 16)
                e.wait_ge(sln, ci + 1)
                e.dma_start(out=H["lnh"][:, o:o + w].rearrange("(f p) t -> p f t", p=128),
                            in_=lnb[:, ci % 2, :, 0:w]).then_inc(slnw, 16)

        def vector(e):
            if call == 3:
                e.memset(t["cnt_row"][:], 0.0)
                e.drain()
            for ci, (o, w) in enumerate(plan):
                e.wait_ge(ld, 16 * (ci + 1))
                e.tensor_copy(xbf[:, ci % 2, :, 0:w], xb[:, ci % 2, :, 0:w])
                e.drain().then_inc(vc, 1)
                e.wait_ge(smm, ci + 1)
                mu = t["muR"][0:1, o:o + w]
                r = t["rR"][0:1, o:o + w]
                scr = t["scr512"][0:1, 0:w]
                scrb = scrb_t[0:1, 0:w]
                e.tensor_scalar(mu, b_mu[0:1, 0:w], 1.0 / c.D, None, op0=ALU.mult)
                e.tensor_scalar(scr, b_sq[0:1, 0:w], 1.0 / c.D, None, op0=ALU.mult)
                e.drain()
                e.tensor_tensor(scrb, mu, mu, ALU.mult)
                e.drain()
                e.tensor_tensor(scr, scr, scrb, ALU.subtract)
                e.drain()
                e.tensor_scalar(scr, scr, c.eps, None, op0=ALU.add)
                e.drain()
                e.reciprocal(r, scr)
                e.drain().then_inc(srow, 1)
                e.wait_ge(sbc, ci + 1)
                for f in range(c.FCH):
                    e.tensor_tensor(lnb[:, ci % 2, f, 0:w], xb[:, ci % 2, f, 0:w],
                                    b_bc1[:, 0:w], ALU.subtract)
                e.drain()
                for f in range(c.FCH):
                    e.tensor_tensor(lnb[:, ci % 2, f, 0:w], lnb[:, ci % 2, f, 0:w],
                                    b_bc2[:, 0:w], ALU.mult)
                e.drain().then_inc(sln, 1)
                for (tix, _ci, k) in tiles_of[ci]:
                    e.wait_ge(stok, tix + 1)
                    zt = b_tok[:, ts(tix % 4, 10)]
                    e.tensor_tensor(difc[:, 0:c.E], t["negmat"][:, 0:c.E],
                                    zt[:, 9:10].broadcast_to([128, c.E]), ALU.mult)
                    e.drain()
                    e.tensor_tensor(difc[:, 0:c.E], difc[:, 0:c.E], zt[:, 0:c.E], ALU.add)
                    e.drain()
                    e.tensor_reduce(mxc[:, 0:1], difc[:, 0:c.E], axis=X, op=ALU.max)
                    e.drain()
                    e.tensor_tensor(difc[:, 0:c.E], difc[:, 0:c.E],
                                    mxc[:, 0:1].broadcast_to([128, c.E]), ALU.subtract)
                    e.drain().then_inc(sdif, 1)
                    e.tensor_single_scalar(ohc[:, 0:c.E], difc[:, 0:c.E], 0.0, ALU.is_equal)
                    e.drain()
                    e.tensor_tensor(sc8[:, 0:c.E], ohc[:, 0:c.E], t["epatm9"][:, 0:c.E], ALU.mult)
                    e.drain()
                    e.tensor_reduce(t["tm_eidx"][:, tix:tix + 1], sc8[:, 0:c.E], axis=X, op=ALU.min)
                    e.drain()
                    e.tensor_scalar(t["tm_eidx"][:, tix:tix + 1], t["tm_eidx"][:, tix:tix + 1],
                                    9.0, None, op0=ALU.add)
                    e.drain()
                    e.tensor_tensor(ohc[:, 0:c.E],
                                    t["tm_eidx"][:, tix:tix + 1].broadcast_to([128, c.E]),
                                    t["epat8"][:, 0:c.E], ALU.is_equal)
                    e.drain()
                    e.tensor_copy(ohb[:, 0:c.E], ohc[:, 0:c.E])
                    if c.E < 8:
                        e.memset(ohb[:, c.E:8], 0.0)
                    e.drain().then_inc(soh, 1)
                    e.wait_ge(sexp, tix + 1)
                    e.tensor_reduce(smc[:, 0:1], exc[:, 0:c.E], axis=X, op=ALU.add)
                    e.drain()
                    e.reciprocal(t["tm_pmax"][:, tix:tix + 1], smc[:, 0:1])
                    e.drain()
                    e.wait_ge(spos, tix + 1)
                    e.tensor_copy(sc8[:, 0:c.E], b_cnt[:, 32:32 + c.E])
                    e.drain()
                    e.tensor_tensor(sc8[:, 0:c.E], sc8[:, 0:c.E],
                                    b_pos[:, ts(tix % 4, 8)][:, 0:c.E], ALU.add)
                    e.drain()
                    e.tensor_tensor(sc8[:, 0:c.E], sc8[:, 0:c.E], ohc[:, 0:c.E], ALU.mult)
                    e.drain()
                    e.tensor_reduce(t["tm_pos"][:, tix:tix + 1], sc8[:, 0:c.E], axis=X, op=ALU.add)
                    e.drain()
                    if tix == NTI - 1:
                        e.tensor_copy(t["agrow"][0:1, 0:8], t["cnt_row"][0:1, 0:8])
                        e.drain()
                    e.tensor_tensor(t["cnt_row"][0:1, 0:8], t["cnt_row"][0:1, 0:8],
                                    b_cnt[0:1, ts(tix % 4, 8)], ALU.add)
                    e.drain().then_inc(sposv, 1)
            e.drain().then_inc(sfin, 1)

        def scalar(e):
            for ci, (o, w) in enumerate(plan):
                e.wait_ge(vc, ci + 1)
                e.activation(xsq[:, ci % 2, :, 0:w], xbf[:, ci % 2, :, 0:w], AF.Square)
                e.drain().then_inc(sq, 1)
                e.wait_ge(srow, ci + 1)
                r = t["rR"][0:1, o:o + w]
                e.activation(r, r, AF.Sqrt)
                e.drain().then_inc(sqr, 1)
                for (tix, _ci, k) in tiles_of[ci]:
                    e.wait_ge(sdif, tix + 1)
                    e.copy(rcolsb[:], b_tok[:, ts(tix % 4, 10)][:, 8:9])
                    e.drain()
                    e.activation(exc[:, 0:c.E], difc[:, 0:c.E], AF.Exp, scale=rcolsb[:])
                    e.drain().then_inc(sexp, 1)

        def tensor(e):
            e.wait_ge(self.sem("cst"), 4)
            for ci, (o, w) in enumerate(plan):
                if ci >= 1:
                    e.wait_ge(srow, ci)
                e.wait_ge(vc, ci + 1)
                for f in range(c.FCH):
                    e.matmul(b_mu[0:1, 0:w], t["ones_col_bf"][:], xbf[:, ci % 2, f, 0:w],
                             start=(f == 0), stop=(f == c.FCH - 1))
                e.wait_ge(sq, ci + 1)
                for f in range(c.FCH):
                    mm = e.matmul(b_sq[0:1, 0:w], t["ones_col_bf"][:], xsq[:, ci % 2, f, 0:w],
                                  start=(f == 0), stop=(f == c.FCH - 1))
                mm.then_inc(smm, 1)
                if ci >= 1:
                    e.wait_ge(sln, ci)
                e.wait_ge(sqr, ci + 1)
                e.matmul(b_bc1[:, 0:w], t["ones_row_f"][0:1, :], t["muR"][0:1, o:o + w],
                         start=True, stop=True)
                e.matmul(b_bc2[:, 0:w], t["onesP"][32:33, :], t["rR"][0:1, o:o + w],
                         start=True, stop=True).then_inc(sbc, 1)
                for (tix, _ci, k) in tiles_of[ci]:
                    if tix >= 4:
                        e.wait_ge(sexp, tix - 3)
                    zt = b_tok[:, ts(tix % 4, 10)]
                    for f in range(c.FCH):
                        e.matmul(zt[:, 0:8], xbf[:, ci % 2, f, k:k + 128],
                                 t["wr_bf"][:, f * 8:(f + 1) * 8], start=(f == 0),
                                 stop=(f == c.FCH - 1))
                    e.matmul(zt[:, 8:9], t["rR"][0:1, o + k:o + k + 128],
                             t["onesP"][32:33, 0:1], start=True, stop=True)
                    e.matmul(zt[:, 9:10], t["muR"][0:1, o + k:o + k + 128], t["one1_f"][:],
                             start=True, stop=True).then_inc(stok, 1)
                    e.wait_ge(soh, tix + 1)
                    if tix >= 1:
                        e.wait_ge(sposv, tix)
                    if tix >= 4:
                        e.wait_ge(sposv, tix - 3)
                    pt = b_pos[:, ts(tix % 4, 8)]
                    e.matmul(pt, t["triU_bf"][:], ohb[:], start=True, stop=True)
                    e.matmul(b_cnt[:, 32:40], t["ones_row_f"][0:1, :], t["cnt_row"][0:1, 0:8],
                             start=True, stop=True)
                    ctile = b_cnt[0:1, ts(tix % 4, 8)]
                    e.matmul(ctile, t["ones_col_bf"][:], ohb[:], start=True, stop=True).then_inc(spos, 1)

        self.phase(sync=sync, vector=vector, scalar=scalar, tensor=tensor)
        return dict(plan=plan, tiles=tiles, NTI=NTI, sfin=sfin, slnw=slnw, ld=ld)

    def ph_postag(self, call):
        c, t, H = self.c, self.t, self.H
        pre = "pa_"
        agi = H["ag_in"] if call == 1 else H["ag3_in"]
        ago = H["ag_out"] if call == 1 else H["ag3_out"]
        rfin = self.sem("r_fin")
        rlnw = self.sem("r_lnw")
        NT, NR, NS = c.NTILE, c.NROW, c.NSLOT
        SFW = NS // 16
        SBK = NR // 16
        NCH = len(c.chunk_plan)

        sgd = self.sem(pre + "gd", sw=True)
        sag = self.sem(pre + "ag", sw=True)
        smm = self.sem(pre + "mm")
        svb = self.sem(pre + "vb")
        svo = self.sem(pre + "vo")
        svseg = self.sem(pre + "vseg")
        svtm = self.sem(pre + "vtm")
        sve = self.sem(pre + "ve")
        svA = self.sem(pre + "vA")
        svB = self.sem(pre + "vB")
        svc = self.sem(pre + "vc")
        str_ = self.sem(pre + "tr")
        sg1 = self.sem(pre + "g1")
        sg2 = self.sem(pre + "g2")
        sga = self.sem(pre + "ga")
        sdr = self.sem(pre + "dr")
        sdl = self.sem(pre + "dl")
        sdw = self.sem(pre + "dw")
        fin = self.sem(pre + "fin")

        b_base, b_seg, b_tr, b_h = self.banks[0], self.banks[1], self.banks[2], self.banks[3]
        trsc = self.sb("pa_trsc", [64, 128], F32)
        baserow = self.sb("pa_baserow", [1, 16], F32)
        adj2 = self.sb("pa_adj2", [128, 1], I32)
        adj2f = self.sb("pa_adj2f", [128, 1], F32)
        sc = self.sb("pa_sc", [128, 8 * NT], F32)
        segi = self.sb("pa_segi", [1, 16], I32)
        segf = self.sb("pa_segf", [1, 16], F32)
        tilesf = self.sb("pa_tilesf", [1, 16], F32)
        spansf = self.sb("pa_spansf", [1, 16], F32)
        cumf = self.sb("pa_cumf", [1, 32], F32)
        spantab = self.t["spantab"]
        fwdi = self.t["fwdi"]
        backi = self.t["backi"]
        self.arena_group("PA")
        lnld = self.at("lnld", [128, c.TW], F32)
        g1r = self.at("g1r", [16, NS], I16)
        desti = self.at("desti", [16, NR], I16)
        pdesti = self.at("pdesti", [16, NR], I16)
        iotad = self.at("iotad", [16, NR], I16)
        ilvp = self.at("ilvp2", [1, 2 * NS], I16)
        _mk = self.arena_mark()
        rowsf = self.at("rowsf", [1, 4 * NR], F32)
        self.arena_reset(_mk)
        gfo = self.at("gfo", [128, NS], F32)
        gfb = self.at("gfb", [128, NS], BF16)
        destrow = rowsf[0:1, 0:NR]
        pidxf = rowsf[0:1, NR:2 * NR]
        pmrow = rowsf[0:1, 2 * NR:3 * NR]
        scrow = rowsf[0:1, 3 * NR:4 * NR]
        grow = gfo[0:1, 0:NS]
        gfwd = ilvp[0:1, 0:NS]
        ilv = ilvp[0:1, NS:2 * NS]

        def gpsimd(e):
            e.wait_ge(rfin, 1)
            e.wait_ge(rlnw, 16 * NCH)
            e.dma_start(out=agi[0:1, 0:16], in_=t["agrow"][0:1, 0:16]).then_inc(sgd, 16)
            e.wait_ge(sgd, 16)
            e.collective_compute("AllGather", ALU.bypass,
                                 replica_groups=[list(range(c.n_cores))],
                                 ins=[agi[:]], outs=[ago[:]]).then_inc(sag, 1)
            e.wait_ge(sag, 1)
            e.dma_start(out=t["agres"][0:c.n_cores, :], in_=ago[0:c.n_cores, :]).then_inc(sgd, 16)
            e.iota(adj2[:], [[0, 1]], base=-1, channel_multiplier=1)
            e.drain().then_inc(sg1, 1)
            e.memset(iotad[:], 0)
            e.drain()
            e.iota(iotad[0:1, 0:NR], [[1, NR]], base=1, channel_multiplier=0)
            e.drain().then_inc(sg1, 1)
            e.load_library(LC.local_scatter)
            PIECE = 2016
            npiece = (NS + PIECE - 1) // PIECE
            for k in range(npiece):
                pw = min(PIECE, NS - k * PIECE)
                e.wait_ge(svA, k + 1)
                e.local_scatter(g1r[:, k * PIECE:k * PIECE + pw], iotad[:],
                                pdesti[:], channels=16, num_elems=pw, num_idxs=NR)
                e.drain().then_inc(sg2, 1)
            e.wait_ge(svB, 1)
            e.tensor_copy(ilv.rearrange("o (p s) -> o p s", p=16),
                          gfwd.rearrange("o (s p) -> o p s", p=16))
            e.drain()
            e.dma_start(out=H["rowscr"][0:1, 0:NS], in_=ilv).then_inc(sgd, 16)
            e.wait_ge(sgd, 48)
            for k in range(8):
                e.dma_start(out=fwdi[16 * k:16 * (k + 1), :],
                            in_=H["rowscr"][0:1, 0:NS].rearrange("o (p s) -> (o p) s", p=16)
                            ).then_inc(sgd, 16)
            e.tensor_copy(ilv[0:1, 0:NR].rearrange("o (p s) -> o p s", p=16),
                          desti[0:1, 0:NR].rearrange("o (s p) -> o p s", p=16))
            e.drain()
            e.wait_ge(sgd, 48 + 128)
            e.dma_start(out=H["rowscr"][0:1, 0:NR], in_=ilv[0:1, 0:NR]).then_inc(sgd, 16)
            e.wait_ge(sgd, 48 + 128 + 16)
            for k in range(8):
                e.dma_start(out=backi[16 * k:16 * (k + 1), :],
                            in_=H["rowscr"][0:1, 0:NR].rearrange("o (p s) -> (o p) s", p=16)
                            ).then_inc(sgd, 16)
            e.wait_ge(sgd, 48 + 256 + 16)
            e.load_library(LC.ap_gather)
            for f in range(c.FCH):
                e.wait_ge(sdl, 16 * (f + 1))
                if f >= 1:
                    e.wait_ge(svc, f)
                e.ap_gather(gfo[:], lnld[:], fwdi[:], channels=128, num_elems=c.TW,
                            d=1, num_idxs=NS)
                e.drain().then_inc(sga, 1)
            e.sem_inc(fin, 1)

        def tensor(e):
            e.wait_ge(sgd, 32)
            e.matmul(b_base[0:1, 0:8], t["maskB_sb"][:, 0:1], t["agres"][0:8, 0:8],
                     start=True, stop=True)
            e.matmul(b_h[0:2, 0:8], t["maskLR_sb"][:, 0:2], t["agres"][0:8, 0:8],
                     start=True, stop=True).then_inc(smm, 1)
            e.wait_ge(svb, 1)
            e.matmul(b_base[:, 8:16], t["ones_row_f"][0:1, :], baserow[0:1, 0:8],
                     start=True, stop=True).then_inc(smm, 1)
            e.wait_ge(svseg, 1)
            e.matmul(b_seg[:, 0:8], t["ones_row_f"][0:1, :], t["seg_start"][0:1, 0:8],
                     start=True, stop=True).then_inc(smm, 1)
            e.wait_ge(svtm, 1)
            for i, nm in enumerate(["tm_dsttm", "tm_pmax", "tm_kept"]):
                if i >= 1:
                    e.wait_ge(sve, i)
                e.transpose(b_tr[0:NT, 0:128], t[nm][:, 0:NT], t["ident_f"][:]).then_inc(str_, 1)
                e.drain()

        def vector(e):
            e.wait_ge(smm, 1)
            e.wait_ge(rlnw, 16 * NCH)
            e.tensor_copy(baserow[0:1, 0:8], b_base[0:1, 0:8])
            e.tensor_copy(t["halo2"][0:2, 0:8], b_h[0:2, 0:8])
            e.drain().then_inc(svb, 1)
            e.wait_ge(rfin, 1)
            e.tensor_tensor(t["ohf"][:].rearrange("p (n e) -> p n e", e=8),
                            t["tm_eidx"][:, 0:NT].rearrange("p (n o) -> p n o", o=1)
                            .broadcast_to([128, NT, 8]),
                            t["epat8"][:].rearrange("p (o e) -> p o e", o=1)
                            .broadcast_to([128, NT, 8]), ALU.is_equal)
            e.drain().then_inc(svo, 1)
            e.wait_ge(smm, 2)
            e.tensor_tensor(sc[:].rearrange("p (n e) -> p n e", e=8),
                            t["ohf"][:].rearrange("p (n e) -> p n e", e=8),
                            b_base[:, 8:16].rearrange("p (o e) -> p o e", o=1)
                            .broadcast_to([128, NT, 8]), ALU.mult)
            e.drain()
            e.tensor_reduce(t["tm_gpos"][:, 0:NT], sc[:].rearrange("p (n e) -> p n e", e=8),
                            axis=X, op=ALU.add)
            e.drain()
            e.tensor_tensor(t["tm_gpos"][:, 0:NT], t["tm_gpos"][:, 0:NT],
                            t["tm_pos"][:, 0:NT], ALU.add)
            e.drain()
            if call == 1:
                e.wait_ge(sg1, 1)
                e.tensor_copy(adj2f[0:2, :], adj2[0:2, :])
                e.drain()
                e.tensor_tensor(sc[0:2, 0:8], t["ohf"][0:2, (NT - 1) * 8:NT * 8],
                                t["halo2"][0:2, 0:8], ALU.mult)
                e.drain()
                e.tensor_reduce(sc[0:2, 8:9], sc[0:2, 0:8], axis=X, op=ALU.add)
                e.drain()
                e.tensor_tensor(t["tm_gpos"][0:2, NT - 1:NT], sc[0:2, 8:9],
                                adj2f[0:2, 0:1], ALU.add)
                e.drain()
            e.tensor_single_scalar(t["tm_kept"][:, 0:NT], t["tm_gpos"][:, 0:NT],
                                   float(c.cap), ALU.is_lt)
            e.drain()
            e.tensor_tensor(t["tm_pmax"][:, 0:NT], t["tm_pmax"][:, 0:NT],
                            t["tm_kept"][:, 0:NT], ALU.mult)
            e.drain()
            e.tensor_scalar(t["tm_pmax"][:, 0:NT], t["tm_pmax"][:, 0:NT], 0.5, None,
                            op0=ALU.mult)
            e.tensor_scalar(t["tm_kept"][:, 0:NT], t["tm_kept"][:, 0:NT], -0.5, 0.5,
                            op0=ALU.mult, op1=ALU.add)
            e.drain()
            e.tensor_copy(segi[0:1, 0:8], t["cnt_row"][0:1, 0:8])
            e.drain()
            e.tensor_scalar(segi[0:1, 0:8], segi[0:1, 0:8], 127, None, op0=ALU.add)
            e.drain()
            e.tensor_scalar(segi[0:1, 0:8], segi[0:1, 0:8], 7, None, op0=ALU.arith_shift_right)
            e.drain()
            e.tensor_scalar_max(segi[0:1, 0:8], segi[0:1, 0:8], 1)
            e.drain()
            e.tensor_copy(tilesf[0:1, 0:8], segi[0:1, 0:8])
            e.drain()
            e.tensor_scalar(tilesf[0:1, 0:8], tilesf[0:1, 0:8], 128.0, None, op0=ALU.mult)
            e.drain()
            e.memset(segf[0:1, 0:16], 1.0)
            e.drain()
            e.tensor_tensor_scan(t["seg_start"][0:1, 0:8], segf[0:1, 0:8], tilesf[0:1, 0:8],
                                 0.0, op0=ALU.mult, op1=ALU.add)
            e.drain()
            e.tensor_tensor(t["seg_start"][0:1, 0:8], t["seg_start"][0:1, 0:8],
                            tilesf[0:1, 0:8], ALU.subtract)
            e.drain().then_inc(svseg, 1)
            e.tensor_scalar(segi[0:1, 0:8], segi[0:1, 0:8], 1, None, op0=ALU.add)
            e.drain()
            e.tensor_scalar(segi[0:1, 0:8], segi[0:1, 0:8], 1, None, op0=ALU.arith_shift_right)
            e.drain()
            e.tensor_copy(spansf[0:1, 0:8], segi[0:1, 0:8])
            e.drain()
            e.tensor_tensor_scan(cumf[0:1, 1:9], segf[0:1, 0:8], spansf[0:1, 0:8],
                                 0.0, op0=ALU.mult, op1=ALU.add)
            e.drain()
            e.memset(cumf[0:1, 0:1], 0.0)
            e.drain()
            e.tensor_copy(spantab[0:1, 0:9], cumf[0:1, 0:9])
            e.drain()
            e.tensor_scalar(cumf[0:1, 16:24], cumf[0:1, 0:8], -256.0, None, op0=ALU.mult)
            e.drain()
            e.tensor_tensor(cumf[0:1, 16:24], cumf[0:1, 16:24], t["seg_start"][0:1, 0:8],
                            ALU.add)
            e.drain()
            e.tensor_copy(spantab[0:1, 16:24], cumf[0:1, 16:24])
            e.drain()
            e.wait_ge(smm, 3)
            e.tensor_tensor(sc[:].rearrange("p (n e) -> p n e", e=8),
                            t["ohf"][:].rearrange("p (n e) -> p n e", e=8),
                            b_seg[:, 0:8].rearrange("p (o e) -> p o e", o=1)
                            .broadcast_to([128, NT, 8]), ALU.mult)
            e.drain()
            e.tensor_reduce(t["tm_dsttm"][:, 0:NT], sc[:].rearrange("p (n e) -> p n e", e=8),
                            axis=X, op=ALU.add)
            e.drain()
            e.tensor_tensor(t["tm_dsttm"][:, 0:NT], t["tm_dsttm"][:, 0:NT],
                            t["tm_pos"][:, 0:NT], ALU.add)
            e.drain().then_inc(svtm, 1)
            for i in range(3):
                e.wait_ge(str_, i + 1)
                if i >= 1:
                    e.wait_ge(sdr, 16 * i)
                e.tensor_copy(trsc[0:NT, :], b_tr[0:NT, 0:128])
                e.drain().then_inc(sve, 1)
            e.wait_ge(sdr, 16)
            e.wait_ge(sg1, 2)
            e.memset(desti[:], -1)
            e.drain()
            e.tensor_copy(desti[0:1, 0:NR], destrow)
            e.drain()
            PIECE = 2016
            npiece = (NS + PIECE - 1) // PIECE
            e.memset(pdesti[:], -1)
            e.drain()
            for k in range(npiece):
                pw = min(PIECE, NS - k * PIECE)
                if k >= 1:
                    e.wait_ge(sg2, k)
                e.tensor_scalar(pidxf, destrow, float(-k * PIECE),
                                None, op0=ALU.add)
                e.drain()
                e.tensor_single_scalar(pmrow, pidxf, -0.5, ALU.is_gt)
                e.drain()
                e.tensor_single_scalar(scrow, pidxf, float(pw) - 0.5,
                                       ALU.is_lt)
                e.drain()
                e.tensor_tensor(pmrow, pmrow, scrow,
                                ALU.mult)
                e.drain()
                e.tensor_tensor(pidxf, pidxf, pmrow, ALU.mult)
                e.drain()
                e.tensor_tensor(pidxf, pidxf, pmrow, ALU.add)
                e.drain()
                e.tensor_scalar(pidxf, pidxf, -1.0, None, op0=ALU.add)
                e.drain()
                e.tensor_copy(pdesti[0:1, 0:NR], pidxf)
                e.drain().then_inc(svA, 1)
            e.wait_ge(sg2, npiece)
            e.tensor_copy(grow[0:1, 0:NS], g1r[0:1, 0:NS])
            e.drain()
            e.tensor_scalar(grow[0:1, 0:NS], grow[0:1, 0:NS], -1.0, None, op0=ALU.add)
            e.drain()
            e.tensor_scalar_max(grow[0:1, 0:NS], grow[0:1, 0:NS], 0.0)
            e.drain()
            e.tensor_copy(gfwd[0:1, 0:NS], grow[0:1, 0:NS])
            e.drain().then_inc(svB, 1)
            for f in range(c.FCH):
                e.wait_ge(sga, f + 1)
                if f >= 1:
                    e.wait_ge(sdw, 16 * f)
                e.tensor_copy(gfb[:], gfo[:])
                e.drain().then_inc(svc, 1)

        def sync(e):
            targets = [destrow, H["pmkrow"][0:1, 0:NR], H["fbrow"][0:1, 0:NR]]
            for i in range(3):
                e.wait_ge(sve, i + 1)
                if i >= 1:
                    e.wait_ge(sdr, 16 * i)
                e.dma_start(out=targets[i], in_=trsc[0:NT, :]).then_inc(sdr, 16)
            for f in range(c.FCH):
                e.wait_ge(rlnw, 16 * NCH)
                e.wait_ge(svB, 1)
                if f >= 1:
                    e.wait_ge(sga, f)
                    e.wait_ge(sdl, 16 * f)
                e.dma_start(out=lnld[:], in_=H["lnh"][f * 128:(f + 1) * 128, :]).then_inc(sdl, 16)
                e.wait_ge(svc, f + 1)
                if f >= 1:
                    e.wait_ge(sdw, 16 * f)
                e.dma_start(out=H["glh"][f * 128:(f + 1) * 128, :], in_=gfb[:]).then_inc(sdw, 16)

        self.phase(gpsimd=gpsimd, tensor=tensor, vector=vector, sync=sync)
        return dict(fin=fin, sdw=sdw)

    def ph_moe_ffn(self, call):
        c, t, H = self.c, self.t, self.H
        pre = "mf_"
        pafin = self.sem("pa_fin")
        padw = self.sem("pa_dw")
        E, FCH, FCH_F = c.E, c.FCH, c.FCH_F
        SP = c.SPAN
        GW = FCH * SP
        OW = FCH * SP

        sptab = self.sem(pre + "ptab")
        swr1 = self.sem(pre + "w1", sw=True)
        swr2 = self.sem(pre + "w2", sw=True)
        se2 = self.sem(pre + "e2")
        sgl = self.sem(pre + "gl")
        sm1 = self.sem(pre + "m1")
        sh = self.sem(pre + "h")
        sm2 = self.sem(pre + "m2")
        sm2v = self.sem(pre + "m2v")
        sew = self.sem(pre + "ew")
        fin = self.sem(pre + "fin")

        self.arena_group("MO")
        w1sb = self.at("w1sb", [128, FCH * c.F], BF16)
        w2sb = self.at("w2sb", [128, FCH_F * c.D], BF16)
        glbuf = self.at("glbuf", [128, 2 * FCH * SP], BF16)
        hbuf = self.at("hbuf", [128, FCH_F * SP], BF16)
        o2buf = self.at("o2buf", [128, FCH * SP], F32)
        zero512 = self.at("zero512", [128, 512], F32)
        spt = t["spantab"]

        def loadex(e, ex, need_off=True):
            lo = e.value_load(spt[0:1, ex:ex + 1], min_val=0, max_val=c.MAXSPANS)
            hi = e.value_load(spt[0:1, ex + 1:ex + 2], min_val=0, max_val=c.MAXSPANS)
            offv = (e.value_load(spt[0:1, 16 + ex:17 + ex], min_val=-c.NSLOT, max_val=c.NSLOT)
                    if need_off else None)
            return lo, hi, offv

        szr = self.sem(pre + "zr", sw=True)

        def gpsimd(e):
            e.wait_ge(pafin, 1)
            e.wait_ge(padw, 16 * FCH)
            e.memset(zero512[:], 0.0)
            e.drain()
            e.sem_inc(sptab, 1)
            cumE = e.value_load(spt[0:1, E:E + 1], min_val=0, max_val=c.MAXSPANS)
            offl = e.value_load(spt[0:1, 16 + E - 1:16 + E], min_val=-c.NSLOT, max_val=c.NSLOT)
            wv = self.nc.s_assert_within(e.snap(cumE * SP + offl), c.WVMIN, c.NSLOT,
                                         skip_runtime_assert=True)
            for ex in range(E):
                if ex >= 1:
                    cumx = e.value_load(spt[0:1, ex:ex + 1], min_val=0, max_val=c.MAXSPANS)
                    e.wait_ge(sm1, cumx * FCH_F)
                e.dma_start(out=w1sb[:].rearrange("p (k f) -> p k f", k=FCH),
                            in_=H["W1"][ex * c.D:(ex + 1) * c.D, :]
                            .rearrange("(k p) f -> p k f", p=128)).then_inc(swr1, 16)
                if ex >= 1:
                    e.wait_ge(sm2, cumx * FCH)
                e.dma_start(out=w2sb[:].rearrange("p (k f) -> p k f", k=FCH_F),
                            in_=H["W2"][ex * c.F:(ex + 1) * c.F, :]
                            .rearrange("(k p) f -> p k f", p=128)).then_inc(swr2, 16)
                if ex == 0:
                    for f in range(FCH):
                        for j in range(c.NZCH):
                            e.dma_start(out=H["eout"][f * 128:(f + 1) * 128,
                                                      ds(wv + j * 512, 512)],
                                        in_=zero512[:]).then_inc(szr, 16)

        def sync(e):
            e.wait_ge(sptab, 1)
            e.wait_ge(padw, 16 * FCH)
            for ex in range(E):
                lo, hi, offv = loadex(e, ex)
                with e.Fori(lo, hi) as i:
                    e.wait_ge(sm1, smax((i - 1) * FCH_F, 0))
                    e.wait_ge(sgl, i * 16)
                    col = self.nc.s_assert_within(e.snap(i * SP + offv), 0,
                                                  c.NSLOT - SP, skip_runtime_assert=True)
                    e.dma_start(out=glbuf[:, ds((i % 2) * GW, GW)]
                                .rearrange("p (f s) -> p f s", f=FCH),
                                in_=H["glh"][:, ds(col, SP)]
                                .rearrange("(f p) s -> p f s", p=128)).then_inc(sgl, 16)

        def tensor(e):
            e.wait_ge(sptab, 1)
            for ex in range(E):
                lo, hi, offv = loadex(e, ex, need_off=False)
                e.wait_ge(swr1, 16 * (ex + 1))
                with e.Fori(lo, hi) as i:
                    e.wait_ge(sgl, (i + 1) * 16)
                    for m in range(FCH_F):
                        gm = i * FCH_F + m
                        e.wait_ge(sh, smax(gm - 3, 0))
                        for k in range(FCH):
                            mm = e.matmul(self.banks[m % 4][:, 0:SP],
                                          w1sb[:, k * c.F + m * 128:k * c.F + (m + 1) * 128],
                                          glbuf[:, ds((i % 2) * GW + k * SP, SP)],
                                          start=(k == 0), stop=(k == FCH - 1))
                        mm.then_inc(sm1, 1)
                    e.wait_ge(swr2, 16 * (ex + 1))
                    e.wait_ge(sh, (i + 1) * FCH_F)
                    for m2 in range(FCH):
                        gm2 = i * FCH + m2
                        e.wait_ge(sm2v, smax(gm2 - 3, 0))
                        for f in range(FCH_F):
                            mm = e.matmul(self.banks[4 + m2 % 4][:, 0:SP],
                                          w2sb[:, f * c.D + m2 * 128:f * c.D + (m2 + 1) * 128],
                                          hbuf[:, ds(f * SP, SP)],
                                          start=(f == 0), stop=(f == FCH_F - 1))
                        mm.then_inc(sm2, 1)
                    e.drain()
                e.sem_inc(se2, 1)

        def scalar_direct(e):
            e.wait_ge(sptab, 1)
            for ex in range(E):
                lo, hi, offv = loadex(e, ex)
                with e.Fori(lo, hi) as i:
                    e.wait_ge(sm2, i * FCH)
                    for m in range(FCH_F):
                        gm = i * FCH_F + m
                        e.wait_ge(sm1, gm + 1)
                        e.activation(hbuf[:, ds(m * SP, SP)], self.banks[m % 4][:, 0:SP],
                                     AF.Relu)
                        e.drain().then_inc(sh, 1)
                    e.wait_ge(sm2v, (i + 1) * FCH)
                    e.wait_ge(sew, i * 16)
                    col = self.nc.s_assert_within(e.snap(i * SP + offv), 0,
                                                  c.NSLOT - SP, skip_runtime_assert=True)
                    e.dma_start(out=H["eout"][:, ds(col, SP)]
                                .rearrange("(f p) s -> p f s", p=128),
                                in_=o2buf[:, 0:OW]
                                .rearrange("p (f s) -> p f s", f=FCH)).then_inc(sew, 16)

        def vector(e):
            e.wait_ge(sptab, 1)
            for ex in range(E):
                lo, hi, offv = loadex(e, ex, need_off=False)
                with e.Fori(lo, hi) as i:
                    e.wait_ge(sew, i * 16)
                    for m2 in range(FCH):
                        gm2 = i * FCH + m2
                        e.wait_ge(sm2, gm2 + 1)
                        e.tensor_copy(o2buf[:, ds(m2 * SP, SP)],
                                      self.banks[4 + m2 % 4][:, 0:SP])
                        e.drain().then_inc(sm2v, 1)
            e.drain().then_inc(fin, 1)

        self.phase(gpsimd=gpsimd, sync=sync, tensor=tensor, scalar=scalar_direct,
                   vector=vector)
        return dict(fin=fin, sew=sew, szr=szr)

    def ph_combine(self, call):
        c, t, H = self.c, self.t, self.H
        pre = "cb_"
        mffin = self.sem("mf_fin")
        sew = self.sem("mf_ew")
        NR, NS = c.NROW, c.NSLOT
        src_h = H["xs"] if call == 1 else H["x3"]
        dst_h = H["R2"] if call == 1 else H["out"]
        plan = c.chunk_plan if call == 1 else c.chunk_plan[:-1]
        W = c.TW if call == 1 else c.T
        spt = t["spantab"]

        sel = self.sem(pre + "el")
        sg = self.sem(pre + "g")
        sx = self.sem(pre + "x")
        sl = self.sem(pre + "l")
        sbc = self.sem(pre + "bc")
        sv = self.sem(pre + "v")
        sw = self.sem(pre + "w")
        srt = self.sem(pre + "rt", sw=True)
        fin = self.sem(pre + "fin")

        self.arena_group("CB")
        eosl = self.at("eosl", [128, NS], F32)
        ge = self.at("ge", [128, NR], F32)
        xsl = self.at("xsl", [128, c.TW], F32)
        lnl = self.at("lnl", [128, c.TW], F32)
        osl = self.at("osl", [128, c.TW], F32)
        rowbf = self.at("rowbf", [128, NR], BF16)
        b_p, b_f = self.banks[6], self.banks[7]
        NCHU = len(plan)
        FCH = c.FCH

        sdr_pa = self.sem("pa_dr")

        szr = self.sem("mf_zr")

        def entry(e):
            e.wait_ge(mffin, 1)
            e.wait_ge(sdr_pa, 48)
            e.wait_ge(szr, 16 * c.FCH * c.NZCH)
            cumE = e.value_load(spt[0:1, c.E:c.E + 1], min_val=0, max_val=c.MAXSPANS)
            e.wait_ge(sew, cumE * 16)

        def sync(e):
            entry(e)
            for f in range(FCH):
                if f >= 1:
                    e.wait_ge(sg, f)
                    e.wait_ge(sel, 16 * f)
                e.dma_start(out=eosl[:], in_=H["eout"][f * 128:(f + 1) * 128, 0:NS]
                            ).then_inc(sel, 16)
                if f >= 1:
                    e.wait_ge(sv, f * NCHU)
                    e.wait_ge(sx, 16 * f)
                e.dma_start(out=xsl[:, 0:c.TW], in_=src_h[f * 128:(f + 1) * 128, :]
                            ).then_inc(sx, 16)
                if f >= 1:
                    e.wait_ge(sl, 16 * f)
                e.dma_start(out=lnl[:, 0:c.TW], in_=H["lnh"][f * 128:(f + 1) * 128, :]
                            ).then_inc(sl, 16)

        def gpsimd(e):
            entry(e)
            e.dma_start(out=rowbf[0:1, 0:NR], in_=H["pmkrow"][0:1, 0:NR]).then_inc(srt, 16)
            e.dma_start(out=rowbf[32:33, 0:NR], in_=H["fbrow"][0:1, 0:NR]).then_inc(srt, 16)
            e.load_library(LC.ap_gather)
            for f in range(FCH):
                e.wait_ge(sel, 16 * (f + 1))
                if f >= 1:
                    e.wait_ge(sv, f * NCHU)
                e.ap_gather(ge[:, 0:NR], eosl[:], t["backi"][:], channels=128,
                            num_elems=NS, d=1, num_idxs=NR)
                e.drain().then_inc(sg, 1)

        def tensor(e):
            entry(e)
            e.wait_ge(srt, 32)
            for f in range(FCH):
                for ci, (o, w) in enumerate(plan):
                    gi = f * NCHU + ci
                    if gi >= 1:
                        e.wait_ge(sv, gi)
                    e.matmul(b_p[:, 0:w], t["onesP_bf"][0:1, :], rowbf[0:1, o:o + w],
                             start=True, stop=True)
                    e.matmul(b_f[:, 0:w], t["onesP_bf"][32:33, :], rowbf[32:33, o:o + w],
                             start=True, stop=True).then_inc(sbc, 1)

        def vector(e):
            e.wait_ge(mffin, 1)
            for f in range(FCH):
                e.wait_ge(sg, f + 1)
                e.wait_ge(sx, 16 * (f + 1))
                e.wait_ge(sl, 16 * (f + 1))
                if f >= 1:
                    e.wait_ge(sw, 16 * f)
                for ci, (o, w) in enumerate(plan):
                    gi = f * NCHU + ci
                    e.wait_ge(sbc, gi + 1)
                    e.tensor_tensor(osl[:, o:o + w], ge[:, o:o + w], b_p[:, 0:w], ALU.mult)
                    e.drain()
                    e.tensor_tensor(osl[:, o:o + w], osl[:, o:o + w], xsl[:, o:o + w], ALU.add)
                    e.drain()
                    e.tensor_tensor(lnl[:, o:o + w], lnl[:, o:o + w], b_f[:, 0:w], ALU.mult)
                    e.drain()
                    e.tensor_tensor(osl[:, o:o + w], osl[:, o:o + w], lnl[:, o:o + w], ALU.add)
                    e.drain().then_inc(sv, 1)

        def scalar(e):
            entry(e)
            for f in range(FCH):
                e.wait_ge(sv, (f + 1) * NCHU)
                if f >= 1:
                    e.wait_ge(sw, 16 * f)
                e.dma_start(out=dst_h[f * 128:(f + 1) * 128, 0:W], in_=osl[:, 0:W]
                            ).then_inc(sw, 16)
            e.wait_ge(sw, 16 * FCH)
            e.sem_inc(fin, 1)

        self.phase(sync=sync, gpsimd=gpsimd, tensor=tensor, vector=vector, scalar=scalar)
        return dict(fin=fin)

    def ph_attn_front(self):
        c, t, H = self.c, self.t, self.H
        FCH, L, nch = c.FCH, c.L, c.nch
        plan = c.chunk_plan
        NC = len(plan)
        a4w = self.sem("a4_w", sw=True)
        tb1 = self.sem("a4_tb1")
        tb2 = self.sem("a4_tb2")
        al = self.sem("a4_l")
        av1 = self.sem("a4_v1")
        asq = self.sem("a4_sq")
        amm = self.sem("a4_mm")
        arow1 = self.sem("a4_row1")
        asq1 = self.sem("a4_sq1")
        arow2 = self.sem("a4_row2")
        asq2 = self.sem("a4_sq2")
        arow = self.sem("a4_row")
        abc = self.sem("a4_bc")
        at1 = self.sem("a4_t1")
        ah = self.sem("a4_h")
        ahw = self.sem("a4_hw")
        ax1 = self.sem("a4_x1")
        axe = self.sem("a4_xe")
        arw = self.sem("a4_rw")

        self.arena_group("AT")
        x1T = self.at("x1T", [128, FCH, c.T + 2], BF16)
        ubf = self.at("ubf", [64, c.T], BF16)
        Btab = self.at("Btab", [64, (L + 1) * 64], BF16)
        ALp = self.at("ALp", [64, nch * 64], BF16)
        Broll = self.at("Broll", [64, 128], F32)
        ALf = self.at("ALf", [64, 64], F32)
        Fb = self.at("Fb", [64, max(nch, 16)], BF16)
        SarrB = self.at("SarrB", [64, max(nch, 16)], BF16)
        ag2row = self.at("ag2row", [1, 128], F32)
        ag2sb = self.at("ag2sb", [8, 128], F32)
        sinrow = self.at("sinrow", [1, 128], F32)
        sincolB = self.at("sincolB", [64, 16], BF16)
        colbuf = self.at("colbuf", [64, 16], F32)
        amsb = self.at("amsb", [64, 64], F32)
        bmsb = self.at("bmsb", [128, FCH, c.ST], BF16)
        convsb = self.at("convsb", [128, 3 * FCH], F32)
        lg = self.at("lg", [128, FCH], F32)
        lb = self.at("lb", [128, FCH], F32)
        w1bsb = self.at("w1bsb", [128, FCH], F32)
        w1bLRsb = self.at("w1bLRsb", [128, 2 * FCH], F32)
        cmasksb = self.at("cmasksb", [128, 2], F32)
        sibselsb = self.at("sibselsb", [8, 16], F32)
        zb6 = self.at("zb6", [64, 128], BF16)
        self.AT_MARK4 = self.arena_mark()
        w1sbA = self.at("w1sbA", [128, FCH, c.D], BF16)
        R2c4 = self.at("R2c4", [128, FCH, 512], F32)
        sqh = self.at("sqh", [128, FCH, 512], BF16)
        rowsc = self.at("rowsc", [1, 2048], F32)
        self.AT_MARK5 = self.arena_mark()
        xbf = self.at("xbf4", [128, FCH, 512], BF16)
        scr = t["scr512"]
        b_sx, b_sq2, b_wq, b_mwq = self.banks[7], self.banks[4], self.banks[5], self.banks[6]

        def gpsimd(e):
            loads = [
                (w1sbA[:], H["w1T"][:].rearrange("(k p) o -> p k o", p=128)),
                (bmsb[:], H["BmT"][:].rearrange("(k p) s -> p k s", p=128)),
                (amsb[:], H["Amat"][:]),
                (convsb[:], H["convw"][:]),
                (lg[:], H["ln1g"][:]), (lb[:], H["ln1b"][:]),
                (w1bsb[:], H["w1b"][:]), (w1bLRsb[:], H["w1bLR"][:]),
                (cmasksb[:], H["convmask"][:]), (sibselsb[:, 0:1], H["sibsel"][:]),
            ]
            for o, i in loads:
                e.dma_start(out=o, in_=i).then_inc(a4w, 16)
        self.NA4W = 160

        def tensor(e):
            e.wait_ge(a4w, self.NA4W)
            for k in range(1, L + 1):
                e.wait_ge(tb2, k)
                e.matmul(self.banks[0][0:64, 0:64], amsb[:],
                         Broll[:, ((k - 1) % 2) * 64:((k - 1) % 2) * 64 + 64],
                         start=True, stop=True).then_inc(tb1, 1)
            e.wait_ge(tb2, L + 1)
            e.matmul(self.banks[1][0:64, 0:64], Broll[:, (L % 2) * 64:(L % 2) * 64 + 64],
                     t["ident_f"][0:64, 0:64], start=True, stop=True).then_inc(tb1, 1)
            for k2 in range(1, nch):
                e.wait_ge(tb2, L + 2 + k2)
                e.matmul(self.banks[0][0:64, 0:64], ALf[:],
                         Broll[:, ((k2 - 1) % 2) * 64:((k2 - 1) % 2) * 64 + 64],
                         start=True, stop=True).then_inc(tb1, 1)
            for ci, (o, w) in enumerate(plan):
                e.wait_ge(av1, ci + 1)
                e.wait_ge(asq, ci + 1)
                for k in range(FCH):
                    e.matmul(b_sx[0:1, 0:w], t["ones_col_bf"][:], xbf[:, k, 0:w],
                             start=(k == 0), stop=(k == FCH - 1))
                for k in range(FCH):
                    mm = e.matmul(b_sq2[0:1, 0:w], t["ones_col_bf"][:], sqh[:, k, 0:w],
                                  start=(k == 0), stop=(k == FCH - 1))
                mm.then_inc(amm, 1)
                e.wait_ge(arow, ci + 1)
                e.matmul(b_wq[:, 0:w], t["ones_row_f"][0:1, :], rowsc[0:1, 1536:1536 + w],
                         start=True, stop=True)
                e.matmul(b_mwq[:, 0:w], t["ones_row_f"][0:1, :], scr[0:1, 0:w],
                         start=True, stop=True).then_inc(abc, 1)
                e.wait_ge(ah, ci + 1)
                for m in range(FCH):
                    if ci * FCH + m >= 4:
                        e.wait_ge(axe, ci * FCH + m - 3)
                    for k in range(FCH):
                        mm = e.matmul(self.banks[m % 4][:, 0:w],
                                      w1sbA[:, k, m * 128:(m + 1) * 128],
                                      sqh[:, k, 0:w], start=(k == 0), stop=(k == FCH - 1))
                    mm.then_inc(ax1, 1)

        def scalar(e):
            e.wait_ge(a4w, self.NA4W)
            e.activation(Broll[:, 0:64], t["ident_f"][0:64, 0:64], AF.Identity)
            e.activation(Btab[:, 0:64], t["ident_f"][0:64, 0:64], AF.Identity)
            e.drain().then_inc(tb2, 1)
            for k in range(1, L + 1):
                e.wait_ge(tb1, k)
                e.activation(Broll[:, (k % 2) * 64:(k % 2) * 64 + 64],
                             self.banks[0][0:64, 0:64], AF.Identity)
                e.activation(Btab[:, k * 64:(k + 1) * 64], self.banks[0][0:64, 0:64], AF.Identity)
                e.drain().then_inc(tb2, 1)
            e.wait_ge(tb1, L + 1)
            e.activation(ALf[:], self.banks[1][0:64, 0:64], AF.Identity)
            e.drain().then_inc(tb2, 1)
            e.activation(Broll[:, 0:64], t["ident_f"][0:64, 0:64], AF.Identity)
            e.activation(ALp[:, 0:64], t["ident_f"][0:64, 0:64], AF.Identity)
            e.drain().then_inc(tb2, 1)
            for k2 in range(1, nch):
                e.wait_ge(tb1, L + 1 + k2)
                e.activation(Broll[:, (k2 % 2) * 64:(k2 % 2) * 64 + 64],
                             self.banks[0][0:64, 0:64], AF.Identity)
                e.activation(ALp[:, k2 * 64:(k2 + 1) * 64], self.banks[0][0:64, 0:64], AF.Identity)
                e.drain().then_inc(tb2, 1)
            for ci, (o, w) in enumerate(plan):
                e.wait_ge(al, 16 * (ci + 1))
                if ci >= 1:
                    e.wait_ge(ax1, ci * FCH)
                    e.wait_ge(ahw, 16 * min(ci, 8))
                e.activation(sqh[:, :, 0:w], R2c4[:, :, 0:w], AF.Square)
                e.drain().then_inc(asq, 1)
                e.wait_ge(arow1, ci + 1)
                if ci >= 1:
                    e.wait_ge(arw, 32 * min(ci, 8))
                e.activation(rowsc[0:1, 1024:1024 + w], scr[0:1, 0:w], AF.Sqrt)
                e.drain().then_inc(asq1, 1)
                e.wait_ge(arow2, ci + 1)
                e.activation(rowsc[0:1, 1536:1536 + w], scr[0:1, 0:w], AF.Sqrt)
                e.drain().then_inc(asq2, 1)
                e.wait_ge(at1, ci + 1)
                for f in range(FCH):
                    e.activation(sqh[:, f, 0:w], sqh[:, f, 0:w], AF.Identity,
                                 bias=lb[:, f:f + 1], scale=lg[:, f:f + 1])
                e.drain().then_inc(ah, 1)
                if ci < NC - 1:
                    if ci >= 1:
                        e.wait_ge(ahw, 16 * ci)
                    e.dma_start(out=H["hh"][:, o:o + w].rearrange("(k p) t -> p k t", p=128),
                                in_=sqh[:, :, 0:w]).then_inc(ahw, 16)
                for m in range(FCH):
                    e.wait_ge(ax1, ci * FCH + m + 1)
                    if ci < NC - 1:
                        e.activation(x1T[:, m, 1 + o:1 + o + w], self.banks[m % 4][:, 0:w],
                                     AF.Identity, bias=w1bsb[:, m:m + 1])
                    else:
                        e.activation(x1T[:, m, 0:1], self.banks[m % 4][:, 0:1],
                                     AF.Identity, bias=w1bLRsb[:, m:m + 1],
                                     scale=cmasksb[:, 0:1])
                        e.activation(x1T[:, m, c.T + 1:c.T + 2], self.banks[m % 4][:, 1:2],
                                     AF.Identity, bias=w1bLRsb[:, FCH + m:FCH + m + 1],
                                     scale=cmasksb[:, 1:2])
                    e.drain().then_inc(axe, 1)

        def vector(e):
            for ci, (o, w) in enumerate(plan):
                e.wait_ge(al, 16 * (ci + 1))
                if ci >= 1:
                    e.wait_ge(ah, ci)
                e.tensor_copy(xbf[:, :, 0:w], R2c4[:, :, 0:w])
                e.drain().then_inc(av1, 1)
                e.wait_ge(amm, ci + 1)
                if ci >= 1:
                    e.wait_ge(arw, 32 * min(ci, 8))
                e.tensor_scalar(rowsc[0:1, 0:w], b_sx[0:1, 0:w], 1.0 / c.D, None, op0=ALU.mult)
                e.tensor_scalar(scr[0:1, 0:w], b_sq2[0:1, 0:w], 1.0 / c.D, None, op0=ALU.mult)
                e.drain()
                e.tensor_tensor(rowsc[0:1, 512:512 + w], rowsc[0:1, 0:w], rowsc[0:1, 0:w],
                                ALU.mult)
                e.drain()
                e.tensor_tensor(rowsc[0:1, 512:512 + w], scr[0:1, 0:w],
                                rowsc[0:1, 512:512 + w], ALU.subtract)
                e.drain()
                e.tensor_scalar(scr[0:1, 0:w], rowsc[0:1, 512:512 + w], c.eps, None,
                                op0=ALU.add)
                e.drain().then_inc(arow1, 1)
                e.wait_ge(asq1, ci + 1)
                e.reciprocal(rowsc[0:1, 1024:1024 + w], rowsc[0:1, 1024:1024 + w])
                e.drain()
                e.tensor_tensor(scr[0:1, 0:w], rowsc[0:1, 1024:1024 + w],
                                rowsc[0:1, 1024:1024 + w], ALU.mult)
                e.drain()
                e.tensor_tensor(scr[0:1, 0:w], rowsc[0:1, 512:512 + w], scr[0:1, 0:w],
                                ALU.mult)
                e.drain()
                e.tensor_scalar(scr[0:1, 0:w], scr[0:1, 0:w], c.eps, None, op0=ALU.add)
                e.drain().then_inc(arow2, 1)
                e.wait_ge(asq2, ci + 1)
                e.reciprocal(rowsc[0:1, 1536:1536 + w], rowsc[0:1, 1536:1536 + w])
                e.drain()
                e.tensor_tensor(rowsc[0:1, 1536:1536 + w], rowsc[0:1, 1536:1536 + w],
                                rowsc[0:1, 1024:1024 + w], ALU.mult)
                e.drain()
                e.tensor_tensor(scr[0:1, 0:w], rowsc[0:1, 0:w], rowsc[0:1, 1536:1536 + w],
                                ALU.mult)
                e.drain()
                e.tensor_tensor(rowsc[0:1, 0:w], rowsc[0:1, 0:w], rowsc[0:1, 1024:1024 + w],
                                ALU.mult)
                e.drain().then_inc(arow, 1)
                e.wait_ge(abc, ci + 1)
                for f in range(FCH):
                    e.tensor_tensor(sqh[:, f, 0:w], R2c4[:, f, 0:w], b_wq[:, 0:w], ALU.mult)
                e.drain()
                for f in range(FCH):
                    e.tensor_tensor(sqh[:, f, 0:w], sqh[:, f, 0:w], b_mwq[:, 0:w],
                                    ALU.subtract)
                e.drain().then_inc(at1, 1)

        def sync(e):
            for ci, (o, w) in enumerate(plan):
                if ci >= 1:
                    e.wait_ge(at1, ci)
                    e.wait_ge(asq, ci)
                    e.wait_ge(al, 16 * ci)
                e.dma_start(out=R2c4[:, :, 0:w],
                            in_=H["R2"][:, o:o + w].rearrange("(k p) t -> p k t", p=128)
                            ).then_inc(al, 16)
                if ci < NC - 1:
                    e.wait_ge(arow, ci + 1)
                    if ci >= 1:
                        e.wait_ge(arw, 32 * ci)
                    e.dma_start(out=H["rows2"][0:1, o:o + w],
                                in_=rowsc[0:1, 1024:1024 + w]).then_inc(arw, 16)
                    e.dma_start(out=H["rows2"][0:1, c.T + o:c.T + o + w],
                                in_=rowsc[0:1, 0:w]).then_inc(arw, 16)

        self.phase(gpsimd=gpsimd, tensor=tensor, scalar=scalar, vector=vector, sync=sync)

    def ph_attn_conv(self):
        c, t = self.c, self.t
        FCH, HQ = c.FCH, c.HQ
        x1T = self.t["AT_x1T"]
        ubf = self.t["AT_ubf"]
        convsb = self.t["AT_convsb"]
        bmsb = self.t["AT_bmsb"]
        self.arena_reset(self.AT_MARK5)
        self.arena_grp = "AT"
        tmpA = self.at("tmpA", [128, HQ], F32)
        tmpB = self.at("tmpB", [128, HQ], F32)
        tmps = [tmpA, tmpB]
        axe = self.sem("a4_xe")
        NXE = c.FCH * len(c.chunk_plan)
        acv = self.sem("a5_cv")
        acs = self.sem("a5_cs")
        au = self.sem("a5_u")
        auv = self.sem("a5_uv")
        NU = c.T // 512

        def vector(e):
            e.wait_ge(axe, NXE)
            for f in range(FCH):
                for q in range(4):
                    g = f * 4 + q
                    if g >= 2:
                        e.wait_ge(acs, g - 1)
                    tm = tmps[q % 2]
                    ho = q * HQ
                    e.tensor_scalar(tm[:], x1T[:, f, ho:ho + HQ],
                                    convsb[:, 0 * FCH + f:0 * FCH + f + 1], None,
                                    op0=ALU.mult)
                    e.drain()
                    e.scalar_tensor_tensor(tm[:], x1T[:, f, ho + 1:ho + 1 + HQ],
                                           convsb[:, 1 * FCH + f:1 * FCH + f + 1], tm[:],
                                           op0=ALU.mult, op1=ALU.add)
                    e.drain()
                    e.scalar_tensor_tensor(tm[:], x1T[:, f, ho + 2:ho + 2 + HQ],
                                           convsb[:, 2 * FCH + f:2 * FCH + f + 1], tm[:],
                                           op0=ALU.mult, op1=ALU.add)
                    e.drain().then_inc(acv, 1)
            for j in range(NU):
                e.wait_ge(au, j + 1)
                e.tensor_copy(ubf[0:64, j * 512:(j + 1) * 512]
                              .rearrange("s (r j) -> s j r", j=c.CPB),
                              self.banks[4 + j % 2][0:64, 0:512]
                              .rearrange("s (j r) -> s j r", r=c.L))
                e.drain().then_inc(auv, 1)

        def scalar(e):
            for f in range(FCH):
                for q in range(4):
                    g = f * 4 + q
                    e.wait_ge(acv, min(g + 2, f * 4 + 4))
                    ho = q * HQ
                    e.activation(x1T[:, f, 1 + ho:1 + ho + HQ], tmps[q % 2][:], AF.Silu)
                    e.drain().then_inc(acs, 1)

        def tensor(e):
            e.wait_ge(acs, 4 * FCH)
            for j in range(NU):
                if j >= 2:
                    e.wait_ge(auv, j - 1)
                for k in range(FCH):
                    mm = e.matmul(self.banks[4 + j % 2][0:64, 0:512],
                                  bmsb[:, k, 0:c.ST],
                                  x1T[:, k, 1 + j * 512:1 + j * 512 + 512],
                                  start=(k == 0), stop=(k == FCH - 1))
                mm.then_inc(au, 1)

        self.phase(vector=vector, scalar=scalar, tensor=tensor)

    def ph_attn_ssm(self):
        c, t, H = self.c, self.t, self.H
        L, nch, NBK, CPB = c.L, c.nch, c.NBK, c.CPB
        ubf = t["AT_ubf"]
        Btab, ALp = t["AT_Btab"], t["AT_ALp"]
        Fb, SarrB = t["AT_Fb"], t["AT_SarrB"]
        ag2row, ag2sb = t["AT_ag2row"], t["AT_ag2sb"]
        sinrow, sincolB, colbuf = t["AT_sinrow"], t["AT_sincolB"], t["AT_colbuf"]
        sibselsb = t["AT_sibselsb"]
        hsb = t["hsbT"]
        auv = self.sem("a5_uv")
        NU = c.T // 512

        stap = self.sem("a6_tap")
        sfb = self.sem("a6_fb")
        sl2 = self.sem("a6_l2")
        sso = self.sem("a6_so")
        sagw = self.sem("a6_agw")
        sag = self.sem("a6_ag", sw=True)
        sagr = self.sem("a6_agr", sw=True)
        ssr = self.sem("a6_sr")
        ssre = self.sem("a6_sre")
        ssin = self.sem("a6_sin")
        ssar = self.sem("a6_sar")
        scor = self.sem("a6_cor")
        shs = self.sem("a6_hs")
        fin = self.sem("a6_fin")
        szb = self.sem("a6_zb")
        zb6 = t["AT_zb6"]
        SarrP = self.banks[0][64:128, 0:nch + 1]
        SK = dict(skip_group_check=True)

        def tensor(e):
            e.wait_ge(auv, NU)
            e.wait_ge(szb, 1)
            e.matmul(SarrP[:], ubf[0:64, 0:64], zb6[:, 0:nch + 1],
                     start=True, stop=True, **SK)
            e.matmul(self.banks[1][64:65, 0:64], ubf[0:64, 0:1], zb6[:, 0:64],
                     start=True, stop=True, **SK)
            e.matmul(self.banks[2][64:65, 0:128], ubf[0:64, 0:1], zb6[:, 0:128],
                     start=True, stop=True, **SK)
            e.matmul(self.banks[3][64:128, 0:1], ubf[0:64, 0:64], zb6[:, 0:1],
                     start=True, stop=True, **SK)
            for b in range(NBK):
                for k in range(L):
                    mm = e.matmul(self.banks[b][0:64, k * CPB:512],
                                  Btab[:, k * 64:(k + 1) * 64],
                                  ubf[0:64, b * 512:b * 512 + (L - k) * CPB],
                                  start=(k == 0), stop=False, **SK)
                mm.then_inc(stap, 1)
            e.wait_ge(sfb, NBK)
            for k2 in range(nch):
                mm = e.matmul(SarrP[:, k2 + 1:nch + 1], ALp[:, k2 * 64:(k2 + 1) * 64],
                              Fb[:, 0:nch - k2], start=False, stop=False, **SK)
            mm.then_inc(sl2, 1)
            e.wait_ge(sso, 1)
            e.matmul(self.banks[1][64:65, 0:64], colbuf[:, 0:1], t["ident_f"][0:64, 0:64],
                     start=False, stop=False, **SK).then_inc(ssr, 1)
            e.wait_ge(sagr, 16)
            e.matmul(self.banks[2][64:65, 0:128], sibselsb[0:c.n_cores, 0:1],
                     ag2sb[0:c.n_cores, :],
                     start=False, stop=False, **SK).then_inc(ssr, 1)
            e.wait_ge(ssre, 1)
            e.matmul(self.banks[3][64:128, 0:1], sinrow[0:1, 0:64], t["one1_f"][:],
                     start=False, stop=False, **SK).then_inc(ssr, 1)
            e.wait_ge(ssre, 2)
            for j in range(nch):
                e.matmul(SarrP[:, j:j + 1], ALp[:, j * 64:(j + 1) * 64], sincolB[:, 0:1],
                         start=False, stop=False, **SK)
            e.drain().then_inc(ssin, 1)
            e.wait_ge(ssar, 1)
            for b in range(NBK):
                for k in range(1, L + 1):
                    mm = e.matmul(self.banks[b][0:64, (k - 1) * CPB:k * CPB],
                                  Btab[:, k * 64:(k + 1) * 64],
                                  SarrB[:, b * CPB:(b + 1) * CPB],
                                  start=False, stop=False, **SK)
                mm.then_inc(scor, 1)

        def vector(e):
            e.memset(zb6[:], 0.0)
            e.drain().then_inc(szb, 1)
            for b in range(NBK):
                e.wait_ge(stap, b + 1)
                e.tensor_copy(Fb[:, b * CPB:(b + 1) * CPB],
                              self.banks[b][0:64, (L - 1) * CPB:512])
                e.drain().then_inc(sfb, 1)
            e.memset(ag2row[:], 0.0)
            e.wait_ge(sl2, 1)
            e.tensor_copy(colbuf[:, 0:1], SarrP[:, nch:nch + 1])
            e.drain().then_inc(sso, 1)
            e.wait_ge(ssr, 1)
            e.tensor_copy(ag2row[0:1, 0:64], self.banks[1][64:65, 0:64])
            e.drain().then_inc(sso, 1)
            e.wait_ge(ssr, 2)
            e.tensor_copy(sinrow[:], self.banks[2][64:65, 0:128])
            e.drain().then_inc(ssre, 1)
            e.wait_ge(ssr, 3)
            e.tensor_copy(sincolB[:, 0:1], self.banks[3][64:128, 0:1])
            e.drain().then_inc(ssre, 1)
            e.wait_ge(ssin, 1)
            e.tensor_copy(SarrB[:, 0:nch], SarrP[:, 0:nch])
            e.drain().then_inc(ssar, 1)
            for b in range(NBK):
                e.wait_ge(scor, b + 1)
                e.tensor_copy(
                    hsb[0:64, b * 512:(b + 1) * 512].rearrange("s (j r) -> s j r", r=L),
                    self.banks[b][0:64, 0:512].rearrange("s (r j) -> s j r", j=CPB))
                e.drain().then_inc(shs, 1)
            e.drain().then_inc(fin, 1)

        def sync(e):
            e.wait_ge(sso, 2)
            e.dma_start(out=H["ag2_in"][:], in_=ag2row[:]).then_inc(sagw, 16)

        def gpsimd(e):
            e.wait_ge(sagw, 16)
            e.collective_compute("AllGather", ALU.bypass,
                                 replica_groups=[list(range(c.n_cores))],
                                 ins=[H["ag2_in"][:]], outs=[H["ag2_out"][:]]
                                 ).then_inc(sag, 1)
            e.wait_ge(sag, 1)
            e.dma_start(out=ag2sb[0:c.n_cores, :], in_=H["ag2_out"][0:c.n_cores, :]
                        ).then_inc(sagr, 16)

        self.phase(tensor=tensor, vector=vector, sync=sync, gpsimd=gpsimd)
        return dict(fin=fin)

    def ph_attn_tail(self):
        c, t, H = self.c, self.t, self.H
        FCH, FCH_FF, CHW = c.FCH, c.FCH_FF, c.CHW
        NCH7 = c.NCH7
        hsb = t["hsbT"]
        a6fin = self.sem("a6_fin")

        w7 = self.sem("a7_w", sw=True)
        zv = self.sem("a7_zv")
        l7r = self.sem("a7_lr")
        l7h = self.sem("a7_lh")
        l7w = self.sem("a7_lw")
        t7bc = self.sem("a7_bc")
        t7v = self.sem("a7_tv")
        s7v = self.sem("a7_sv")
        t7y = self.sem("a7_ty")
        v7yv = self.sem("a7_yv")
        t7s = self.sem("a7_ts")
        v7sa = self.sem("a7_sa")
        v7s = self.sem("a7_vs")
        v7sb = self.sem("a7_vsb")
        s7q = self.sem("a7_sq")
        t7st = self.sem("a7_st")
        v7var = self.sem("a7_var")
        s7r = self.sem("a7_sr")
        v7r3 = self.sem("a7_r3")
        t7b3 = self.sem("a7_b3")
        v7u2 = self.sem("a7_u2")
        s7sn = self.sem("a7_sn")
        t7f1 = self.sem("a7_f1")
        s7g = self.sem("a7_g")
        t7f2 = self.sem("a7_f2")
        v7o = self.sem("a7_vo")
        v7x3 = self.sem("a7_x3")
        x3w = self.sem("x3w")

        self.arena_group("A7")
        v1sb = self.at("v1sb", [128, FCH, c.D], BF16)
        w2sb = self.at("w2sb", [128, FCH, c.D], BF16)
        cmsb = self.at("cmsb", [64, c.D], BF16)
        ff1sb = self.at("ff1sb", [128, FCH, c.FF], BF16)
        ff2sb = self.at("ff2sb", [128, FCH_FF, c.D], BF16)
        R2c7 = self.at("R2c7", [128, FCH, CHW], F32)
        s7 = self.at("s7", [128, FCH, CHW], F32)
        FB = max(FCH_FF, 2 * FCH)
        fbuf = self.at("fbuf", [128, FB * CHW], BF16)
        yvb = self.at("yvb", [128, FCH, CHW], BF16)
        rowbuf7 = self.at("rowbuf7", [1, 2 * CHW], F32)
        rowsc7 = self.at("rowsc7", [1, 4 * CHW], F32)
        zbuf = self.at("zbuf", [128, 128], F32)
        v1bsb = self.at("v1bsb", [128, FCH], F32)
        w2bsb = self.at("w2bsb", [128, FCH], F32)
        ff1bsb = self.at("ff1bsb", [128, FCH_FF], F32)
        ff2bsb = self.at("ff2bsb", [128, FCH], F32)
        l2g = self.at("l2g", [128, FCH], F32)
        l2b = self.at("l2b", [128, FCH], F32)
        hhc = fbuf[:, 0:FCH * CHW].rearrange("p (k t) -> p k t", k=FCH)
        vb = fbuf[:, FCH * CHW:2 * FCH * CHW].rearrange("p (k t) -> p k t", k=FCH)
        ssqv = fbuf[:, 0:FCH * CHW].rearrange("p (k t) -> p k t", k=FCH)
        b6, b7 = self.banks[6], self.banks[7]
        NW7 = 176

        def gpsimd(e):
            e.wait_ge(a6fin, 1)
            loads = [
                (v1sb[:], H["v1T"][:].rearrange("(k p) o -> p k o", p=128)),
                (w2sb[:], H["w2T"][:].rearrange("(k p) o -> p k o", p=128)),
                (cmsb[:], H["CmT"][:]),
                (ff1sb[:], H["ff1T"][:].rearrange("(k p) o -> p k o", p=128)),
                (ff2sb[:], H["ff2T"][:].rearrange("(k p) o -> p k o", p=128)),
                (v1bsb[:], H["v1b"][:]), (w2bsb[:], H["w2b"][:]),
                (ff1bsb[:], H["ff1b"][:]), (ff2bsb[:], H["ff2b"][:]),
                (l2g[:], H["ln2g"][:]), (l2b[:], H["ln2b"][:]),
            ]
            for o_, i_ in loads:
                e.dma_start(out=o_, in_=i_).then_inc(w7, 16)

        def sync(e):
            e.wait_ge(a6fin, 1)
            for ci in range(NCH7):
                o = ci * CHW
                if ci >= 1:
                    e.wait_ge(v7x3, ci)
                    e.wait_ge(l7r, 16 * ci)
                e.dma_start(out=R2c7[:],
                            in_=H["R2"][:, o:o + CHW].rearrange("(k p) t -> p k t", p=128)
                            ).then_inc(l7r, 16)
                if ci >= 1:
                    e.wait_ge(t7f2, ci * FCH)
                    e.wait_ge(l7h, 16 * ci)
                e.dma_start(out=hhc,
                            in_=H["hh"][:, o:o + CHW].rearrange("(k p) t -> p k t", p=128)
                            ).then_inc(l7h, 16)
                if ci >= 1:
                    e.wait_ge(t7bc, ci)
                    e.wait_ge(l7w, 16 * ci)
                e.dma_start(out=rowbuf7[0:1, :].rearrange("o (a t) -> o a t", a=2),
                            in_=H["rows2"][0:1, :].rearrange("o (a tt) -> o a tt", a=2)
                            [:, :, o:o + CHW]).then_inc(l7w, 16)
            e.wait_ge(zv, 1)
            e.wait_ge(v7x3, NCH7)
            for f in range(FCH):
                e.dma_start(out=H["x3"][f * 128:(f + 1) * 128, c.T:c.TW],
                            in_=zbuf[:]).then_inc(x3w, 16)

        def tensor(e):
            e.wait_ge(a6fin, 1)
            e.wait_ge(w7, NW7)
            for ci in range(NCH7):
                o = ci * CHW
                e.wait_ge(l7w, 16 * (ci + 1))
                if ci >= 1:
                    e.wait_ge(v7u2, ci)
                e.matmul(b6[:, 0:CHW], t["ones_row_f"][0:1, :], rowbuf7[0:1, 0:CHW],
                         start=True, stop=True)
                e.matmul(b6[:, CHW:2 * CHW], t["ones_row_f"][0:1, :],
                         rowbuf7[0:1, CHW:2 * CHW], start=True, stop=True).then_inc(t7bc, 1)
                e.wait_ge(l7h, 16 * (ci + 1))
                if ci >= 1:
                    e.wait_ge(s7g, ci * FCH_FF)
                for m in range(FCH):
                    if ci * FCH + m >= 2:
                        e.wait_ge(s7v, ci * FCH + m - 1)
                    for k in range(FCH):
                        mm = e.matmul(self.banks[m % 2][:, 0:CHW],
                                      v1sb[:, k, m * 128:(m + 1) * 128], hhc[:, k, :],
                                      start=(k == 0), stop=(k == FCH - 1))
                    mm.then_inc(t7v, 1)
                for m in range(FCH):
                    if ci * FCH + m >= 2:
                        e.wait_ge(v7yv, ci * FCH + m - 1)
                    e.matmul(self.banks[2 + m % 2][:, 0:CHW],
                             cmsb[0:64, m * 128:(m + 1) * 128], hsb[0:64, o:o + CHW],
                             start=True, stop=True).then_inc(t7y, 1)
                e.wait_ge(v7yv, (ci + 1) * FCH)
                if ci >= 1:
                    e.wait_ge(v7o, ci * FCH)
                for m in range(FCH):
                    if ci * FCH + m >= 2:
                        e.wait_ge(v7sa, ci * FCH + m - 1)
                    for k in range(FCH):
                        mm = e.matmul(self.banks[4 + m % 2][:, 0:CHW],
                                      w2sb[:, k, m * 128:(m + 1) * 128], yvb[:, k, :],
                                      start=(k == 0), stop=(k == FCH - 1))
                    mm.then_inc(t7s, 1)
                e.wait_ge(v7sb, ci + 1)
                e.wait_ge(s7q, ci + 1)
                for k in range(FCH):
                    e.matmul(b7[0:1, 0:CHW], t["ones_col_bf"][:], yvb[:, k, :],
                             start=(k == 0), stop=(k == FCH - 1))
                for k in range(FCH):
                    mm = e.matmul(b7[32:33, 0:CHW], t["ones_col_bf"][:], ssqv[:, k, :],
                                  start=(k == 0), stop=(k == FCH - 1))
                mm.then_inc(t7st, 1)
                e.wait_ge(v7r3, ci + 1)
                e.matmul(b6[:, 0:CHW], t["ones_row_f"][0:1, :], rowsc7[0:1, 0:CHW],
                         start=True, stop=True)
                e.matmul(b6[:, CHW:2 * CHW], t["ones_row_f"][0:1, :],
                         rowsc7[0:1, 2 * CHW:3 * CHW], start=True, stop=True).then_inc(t7b3, 1)
                e.wait_ge(s7sn, ci + 1)
                for mf in range(FCH_FF):
                    if ci * FCH_FF + mf >= 4:
                        e.wait_ge(s7g, ci * FCH_FF + mf - 3)
                    for k in range(FCH):
                        mm = e.matmul(self.banks[mf % 4][:, 0:CHW],
                                      ff1sb[:, k, mf * 128:(mf + 1) * 128], yvb[:, k, :],
                                      start=(k == 0), stop=(k == FCH - 1))
                    mm.then_inc(t7f1, 1)
                e.wait_ge(s7g, (ci + 1) * FCH_FF)
                for m in range(FCH):
                    if ci * FCH + m >= 2:
                        e.wait_ge(v7o, ci * FCH + m - 1)
                    for k in range(FCH_FF):
                        mm = e.matmul(self.banks[4 + m % 2][:, 0:CHW],
                                      ff2sb[:, k, m * 128:(m + 1) * 128],
                                      fbuf[:, k * CHW:(k + 1) * CHW],
                                      start=(k == 0), stop=(k == FCH_FF - 1))
                    mm.then_inc(t7f2, 1)

        def scalar(e):
            e.wait_ge(a6fin, 1)
            e.wait_ge(w7, NW7)
            for ci in range(NCH7):
                o = ci * CHW
                if ci >= 1:
                    e.wait_ge(t7f2, ci * FCH)
                for m in range(FCH):
                    e.wait_ge(t7v, ci * FCH + m + 1)
                    e.activation(vb[:, m, :], self.banks[m % 2][:, 0:CHW], AF.Silu,
                                 bias=v1bsb[:, m:m + 1])
                    e.drain().then_inc(s7v, 1)
                e.wait_ge(v7s, ci + 1)
                e.activation(ssqv[:, :, :], s7[:, :, :], AF.Square)
                e.drain().then_inc(s7q, 1)
                e.wait_ge(v7var, ci + 1)
                e.activation(rowsc7[0:1, 2 * CHW:3 * CHW], rowsc7[0:1, CHW:2 * CHW],
                             AF.Sqrt)
                e.drain().then_inc(s7r, 1)
                e.wait_ge(v7u2, ci + 1)
                for k in range(FCH):
                    e.activation(yvb[:, k, :], yvb[:, k, :], AF.Identity,
                                 scale=l2g[:, k:k + 1], bias=l2b[:, k:k + 1])
                e.drain().then_inc(s7sn, 1)
                for mf in range(FCH_FF):
                    e.wait_ge(t7f1, ci * FCH_FF + mf + 1)
                    e.activation(fbuf[:, mf * CHW:(mf + 1) * CHW],
                                 self.banks[mf % 4][:, 0:CHW], AF.Gelu,
                                 bias=ff1bsb[:, mf:mf + 1])
                    e.drain().then_inc(s7g, 1)
                e.wait_ge(v7x3, ci + 1)
                e.dma_start(out=H["x3"][:, o:o + CHW].rearrange("(k p) t -> p k t", p=128),
                            in_=s7[:]).then_inc(x3w, 16)

        def vector(e):
            e.wait_ge(a6fin, 1)
            e.memset(zbuf[:], 0.0)
            e.drain().then_inc(zv, 1)
            for ci in range(NCH7):
                for m in range(FCH):
                    e.wait_ge(t7y, ci * FCH + m + 1)
                    e.wait_ge(s7v, ci * FCH + m + 1)
                    e.tensor_tensor(yvb[:, m, :], vb[:, m, :],
                                    self.banks[2 + m % 2][:, 0:CHW], ALU.mult)
                    e.drain().then_inc(v7yv, 1)
                e.wait_ge(t7bc, ci + 1)
                e.wait_ge(l7r, 16 * (ci + 1))
                if ci >= 1:
                    e.wait_ge(x3w, 16 * ci)
                for k in range(FCH):
                    e.tensor_tensor(s7[:, k, :], R2c7[:, k, :], b6[:, 0:CHW], ALU.mult)
                e.drain()
                for k in range(FCH):
                    e.tensor_tensor(s7[:, k, :], s7[:, k, :], b6[:, CHW:2 * CHW],
                                    ALU.subtract)
                e.drain()
                for m in range(FCH):
                    e.wait_ge(t7s, ci * FCH + m + 1)
                    e.tensor_tensor(s7[:, m, :], s7[:, m, :],
                                    self.banks[4 + m % 2][:, 0:CHW], ALU.add)
                    e.drain()
                    e.tensor_scalar(s7[:, m, :], s7[:, m, :], w2bsb[:, m:m + 1], None,
                                    op0=ALU.add)
                    e.drain().then_inc(v7sa, 1)
                e.sem_inc(v7s, 1)
                e.tensor_copy(yvb[:, :, :], s7[:, :, :])
                e.drain().then_inc(v7sb, 1)
                e.wait_ge(t7st, ci + 1)
                e.tensor_scalar(rowsc7[0:1, 0:CHW], b7[0:1, 0:CHW], 1.0 / c.D, None,
                                op0=ALU.mult)
                e.tensor_scalar(rowsc7[0:1, 3 * CHW:4 * CHW], b7[32:33, 0:CHW], 1.0 / c.D,
                                None, op0=ALU.mult)
                e.drain()
                e.tensor_tensor(rowsc7[0:1, CHW:2 * CHW], rowsc7[0:1, 0:CHW],
                                rowsc7[0:1, 0:CHW], ALU.mult)
                e.drain()
                e.tensor_tensor(rowsc7[0:1, CHW:2 * CHW], rowsc7[0:1, 3 * CHW:4 * CHW],
                                rowsc7[0:1, CHW:2 * CHW], ALU.subtract)
                e.drain()
                e.tensor_scalar(rowsc7[0:1, CHW:2 * CHW], rowsc7[0:1, CHW:2 * CHW], c.eps,
                                None, op0=ALU.add)
                e.drain().then_inc(v7var, 1)
                e.wait_ge(s7r, ci + 1)
                e.reciprocal(rowsc7[0:1, 2 * CHW:3 * CHW], rowsc7[0:1, 2 * CHW:3 * CHW])
                e.drain().then_inc(v7r3, 1)
                e.wait_ge(t7b3, ci + 1)
                for k in range(FCH):
                    e.tensor_tensor(yvb[:, k, :], s7[:, k, :], b6[:, 0:CHW], ALU.subtract)
                e.drain()
                for k in range(FCH):
                    e.tensor_tensor(yvb[:, k, :], yvb[:, k, :], b6[:, CHW:2 * CHW],
                                    ALU.mult)
                e.drain().then_inc(v7u2, 1)
                for m in range(FCH):
                    e.wait_ge(t7f2, ci * FCH + m + 1)
                    e.tensor_tensor(s7[:, m, :], s7[:, m, :],
                                    self.banks[4 + m % 2][:, 0:CHW], ALU.add)
                    e.drain()
                    e.tensor_scalar(s7[:, m, :], s7[:, m, :], ff2bsb[:, m:m + 1], None,
                                    op0=ALU.add)
                    e.drain().then_inc(v7o, 1)
                for k in range(FCH):
                    e.tensor_tensor(s7[:, k, :], s7[:, k, :], R2c7[:, k, :], ALU.add)
                e.drain().then_inc(v7x3, 1)

        self.phase(gpsimd=gpsimd, sync=sync, tensor=tensor, scalar=scalar, vector=vector)
        return dict(x3w=x3w)

    def ph_debug_dump(self, names_2d):
        c, t = self.c, self.t
        dbg = self.sem("dbg")
        outs = []
        for sn, on in names_2d:
            shape = list(t[sn].shape) if hasattr(t[sn], 'shape') else None
            outs.append((sn, self.dram(on, list(t[sn].shape), t[sn].dtype, "ExternalOutput")))

        def gpsimd(e):
            for ph in []:
                pass
            for i, (sn, hap) in enumerate(outs):
                e.dma_start(out=hap[:], in_=t[sn][:]).then_inc(dbg, 16)
            e.wait_ge(dbg, 16 * len(outs))

        self.phase(gpsimd=gpsimd)


def kernel(**inputs):
    cfg = KCfg()
    nc, meta = build_full(cfg)
    maps = stage_inputs(cfg, inputs)
    res = run_bass_kernel_spmd(nc, maps, core_ids=list(range(cfg.n_cores)))
    outs = [r["out"] for r in res.results]
    full = np.concatenate([o.T for o in outs], axis=0)
    return np.ascontiguousarray(full.reshape(cfg.B, cfg.S, cfg.D))


def build_full(cfg):
    kb = KB(cfg)
    kb.declare()
    kb.ph_consts()
    kb.ph_router(1)
    kb.ph_postag(1)
    kb.ph_moe_ffn(1)
    kb.ph_combine(1)
    kb.new_generation()
    kb.ph_attn_front()
    kb.ph_attn_conv()
    kb.ph_attn_ssm()
    kb.ph_attn_tail()
    kb.new_generation()
    kb.ph_router(3)
    kb.ph_postag(3)
    kb.ph_moe_ffn(3)
    kb.ph_combine(3)
    nc = kb.finish()
    return nc, kb
